# version 33
# baseline (speedup 1.0000x reference)
import numpy as np
import ml_dtypes
from contextlib import ExitStack

import concourse.bass as bass
import concourse.mybir as mybir
from concourse.bass_utils import run_bass_kernel_spmd

F32 = mybir.dt.float32
BF16 = mybir.dt.bfloat16
AF = mybir.ActivationFunctionType
ALU = mybir.AluOpType
AX = mybir.AxisListType

T = 4096
C = 1024
NH = 16
KVL = 256
NE = 32
SH = 512
NQG = 8
CVW = 65

ENGINES = ("sync", "gpsimd", "tensor", "vector", "scalar")


class Sem:
    def __init__(self, nc, ctx, name):
        self.h = ctx.enter_context(nc.semaphore(name))
        self.n = 0


class Prog:
    def __init__(self):
        self.q = {e: [] for e in ENGINES}
        self.last_wait = {}

    def wait(self, eng, sem, val):
        if val <= 0:
            return
        key = (eng, id(sem))
        if self.last_wait.get(key, -1) >= val:
            return
        self.last_wait[key] = val
        self.q[eng].append(lambda E, s=sem.h, v=val: E.wait_ge(s, v))

    def op(self, eng, name, *args, inc=None, inc_k=1, **kwargs):
        if inc is not None:
            inc.n += inc_k
            hh, k = inc.h, inc_k
        else:
            hh, k = None, 0

        def f(E, nm=name, a=args, kw=kwargs, h=hh, kk=k):
            ins = getattr(E, nm)(*a, **kw)
            if h is not None:
                ins.then_inc(h, kk)
        self.q[eng].append(f)

    def dma(self, eng, out, in_, sem):
        self.op(eng, "dma_start", inc=sem, inc_k=16, out=out, in_=in_)

    def replay(self, eng, E):
        for f in self.q[eng]:
            f(E)


def build_program(debug=False, stop_after=None):
    nc = bass.Bass("TRN2", target_bir_lowering=False, debug=False)

    x_h = nc.dram_tensor("x", [T, C], F32, kind="ExternalInput").ap()
    wdq_h = nc.dram_tensor("wdq", [KVL, C], F32, kind="ExternalInput").ap()
    wuq_h = nc.dram_tensor("wuq", [C, KVL], F32, kind="ExternalInput").ap()
    wdkv_h = nc.dram_tensor("wdkv", [KVL, C], F32, kind="ExternalInput").ap()
    wuk_h = nc.dram_tensor("wuk", [C, KVL], F32, kind="ExternalInput").ap()
    wuv_h = nc.dram_tensor("wuv", [C, KVL], F32, kind="ExternalInput").ap()
    wo_h = nc.dram_tensor("wo", [C, C], F32, kind="ExternalInput").ap()
    rw_h = nc.dram_tensor("rw", [NE, C], F32, kind="ExternalInput").ap()
    rb_h = nc.dram_tensor("rb", [1, NE], F32, kind="ExternalInput").ap()
    gw_h = nc.dram_tensor("gw", [NE, 128, C], F32, kind="ExternalInput").ap()
    uw_h = nc.dram_tensor("uw", [NE, 128, C], F32, kind="ExternalInput").ap()
    dw_h = nc.dram_tensor("dw", [NE, C, 128], F32, kind="ExternalInput").ap()
    sgw_h = nc.dram_tensor("sgw", [SH, C], F32, kind="ExternalInput").ap()
    suw_h = nc.dram_tensor("suw", [SH, C], F32, kind="ExternalInput").ap()
    sdw_h = nc.dram_tensor("sdw", [C, SH], F32, kind="ExternalInput").ap()
    identb_h = nc.dram_tensor("identb", [128, 128], BF16, kind="ExternalInput").ap()
    identf_h = nc.dram_tensor("identf", [128, 128], F32, kind="ExternalInput").ap()
    masks_h = nc.dram_tensor("masks", [128, 128], F32, kind="ExternalInput").ap()
    onehot_h = nc.dram_tensor("onehot", [NE, NE * 128], BF16, kind="ExternalInput").ap()
    onesb_h = nc.dram_tensor("onesb", [1, 64], BF16, kind="ExternalInput").ap()
    onesf_h = nc.dram_tensor("onesf", [1, 128], F32, kind="ExternalInput").ap()
    negonesf_h = nc.dram_tensor("negonesf", [1, 128], F32, kind="ExternalInput").ap()

    out_h = nc.dram_tensor("out", [T, C], F32, kind="ExternalOutput").ap()

    x2_sc = nc.dram_tensor("x2_sc", [T, C], F32,
                           kind="ExternalOutput" if debug else "Internal").ap()
    gate_sc = nc.dram_tensor("gate_sc", [NE, 128, C], BF16, kind="Internal").ap()
    up_sc = nc.dram_tensor("up_sc", [NE, 128, C], BF16, kind="Internal").ap()
    down_sc = nc.dram_tensor("down_sc", [4, 128, NE * 256], BF16, kind="Internal").ap()
    sg_sc = nc.dram_tensor("sg_sc", [128, 8 * SH], BF16, kind="Internal").ap()
    su_sc = nc.dram_tensor("su_sc", [128, 8 * SH], BF16, kind="Internal").ap()
    sd_sc = nc.dram_tensor("sd_sc", [128, 4 * C], BF16, kind="Internal").ap()

    dbg = {}
    if debug:
        for nm, shp, dt in [("d_ckv", [128, 2 * T], BF16), ("d_qlow", [128, 2 * T], BF16),
                            ("d_cv", [128, 32 * NH * CVW], BF16), ("d_y", [128, 8 * T], BF16),
                            ("d_xn2", [128, 8 * T], BF16), ("d_wfm", [NE, T], BF16),
                            ("d_hall", [128, NE * 512], BF16), ("d_negc", [128, NH], F32),
                            ("d_bht", [128, NH * 2 * KVL], BF16),
                            ("d_t1", [128, 2 * KVL], BF16), ("d_m", [128, 8 * KVL], BF16),
                            ("d_v", [128, 2 * C], BF16), ("d_woT", [128, 8 * C], BF16),
                            ("d_wdkvT", [128, 8 * KVL], BF16),
                            ("d_xnfm", [128, 2 * 4096], BF16)]:
            dbg[nm] = nc.dram_tensor(nm, shp, dt, kind="ExternalOutput").ap()

    P = Prog()
    with ExitStack() as ctx:
        def finish(dumps):
            P.wait("sync", ve, ve.n)
            P.wait("sync", ae, ae.n)
            P.wait("sync", pe, pe.n)
            P.wait("sync", sc, sc.n)
            P.wait("sync", gp, gp.n)
            for nm, src_sb in dumps:
                P.dma("sync", dbg[nm], src_sb[:], st)
            P.wait("sync", st, st.n)
            block = ctx.enter_context(nc.Block())
            block.sync(lambda E: P.replay("sync", E))
            block.gpsimd(lambda E: P.replay("gpsimd", E))
            block.tensor(lambda E: P.replay("tensor", E))
            block.vector(lambda E: P.replay("vector", E))
            block.scalar(lambda E: P.replay("scalar", E))

        sy = Sem(nc, ctx, "sy")
        gp = Sem(nc, ctx, "gp")
        sc = Sem(nc, ctx, "sc")
        pe = Sem(nc, ctx, "pe")
        ve = Sem(nc, ctx, "ve")
        ae = Sem(nc, ctx, "ae")
        st = Sem(nc, ctx, "st")

        import os as _os
        _trace_sb = _os.environ.get("TRACE_SB")

        def sb(stk, name, shape, dtype, side=None):
            h = stk.enter_context(
                nc.sbuf_tensor("s_" + name, shape, dtype, side=side))
            if _trace_sb:
                import math
                per = math.prod(shape[1:]) * (2 if "bfloat" in str(dtype) else 4)
                print(f"SBUF + {name:12s} {per:7d}B/part side={side} "
                      f"remaining={nc.sbuf_bytes_remaining}")
            return h

        def psm(stk, name, shape, dtype=F32):
            return stk.enter_context(nc.psum_tensor(name, shape, dtype))

        identb = sb(ctx, "identb", [128, 128], BF16)
        identf = sb(ctx, "identf", [128, 128], F32)
        onesb = sb(ctx, "onesb", [1, 64], BF16)
        onesf = sb(ctx, "onesf", [1, 128], F32)
        negonesf = sb(ctx, "negonesf", [1, 128], F32)
        rbias = sb(ctx, "rbias", [1, NE], F32)
        rwT = sb(ctx, "rwT", [128, 8 * NE], F32)
        eps_col = sb(ctx, "eps_col", [128, 1], F32)
        sPab = ctx.enter_context(ExitStack())
        pa = psm(sPab, "pa", [128, 512], F32)
        pb = psm(sPab, "pb", [128, 512], F32)

        P.dma("sync", identb[:], identb_h, sy)
        sy_identb = sy.n
        P.op("vector", "memset", eps_col[:], 1e-5, inc=ve)
        for dst, src in [(identf, identf_h),
                         (onesb, onesb_h), (onesf, onesf_h),
                         (negonesf, negonesf_h), (rbias, rb_h)]:
            P.dma("sync", dst[:], src, sy)

        sAB = ctx.enter_context(ExitStack())
        ckv = sb(sAB, "ckv", [128, 2 * T], BF16)
        qlow = sb(sAB, "qlow", [128, 2 * T], BF16)
        bht = sb(sAB, "bht", [128, NH * 2 * KVL], BF16)
        cv = sb(sAB, "cv", [128, 32 * NH * CVW], BF16)

        negc_all = sb(sAB, "negc_all", [128, NH], F32)
        sA = ctx.enter_context(ExitStack())
        wdqT = sb(sA, "wdqT", [128, 8 * KVL], BF16)
        wdkvT = sb(sA, "wdkvT", [128, 8 * KVL], BF16)
        v_b = sb(sA, "v_b", [128, 2 * C], BF16)

        sXr = ctx.enter_context(ExitStack())
        x_ring = sb(sXr, "x_ring", [128, 6 * C], F32)
        xn_ring = sb(sXr, "xn_ring", [128, 4 * C], BF16)

        s0a = ctx.enter_context(ExitStack())
        wuq_b = sb(s0a, "wuq_b", [128, 8 * KVL], BF16)
        wuk_b = sb(s0a, "wuk_b", [128, 8 * KVL], BF16)
        wuv_b = sb(s0a, "wuv_b", [128, 8 * KVL], BF16)
        wdq_b = sb(s0a, "wdq_b", [128, 2 * C], BF16)
        wdkv_b = sb(s0a, "wdkv_b", [128, 2 * C], BF16)
        wo_b = sb(s0a, "wo_b", [128, 4 * C], BF16)
        woT = sb(s0a, "woT", [128, 4 * C], BF16)
        t1 = sb(s0a, "t1", [128, 2 * KVL], BF16)
        m_b = sb(s0a, "m_b", [128, 8 * KVL], BF16)
        rw_f = sb(s0a, "rw_f", [NE, C], F32)

        sPA = ctx.enter_context(ExitStack())
        pt0 = psm(sPA, "pt0", [128, 1024], BF16)
        pt1 = psm(sPA, "pt1", [128, 1024], BF16)
        pob0 = psm(sPA, "pob0", [128, 1024], BF16)
        pob1 = psm(sPA, "pob1", [128, 1024], BF16)
        pob2 = psm(sPA, "pob2", [128, 1024], BF16)
        pf = psm(sPA, "pf", [128, 512], F32)

        P.dma("gpsimd", wdq_b[:].rearrange("p (c k) -> p c k", c=2),
              wdq_h.rearrange("(c p) k -> p c k", p=128), gp)
        P.dma("gpsimd", wdkv_b[:].rearrange("p (c k) -> p c k", c=2),
              wdkv_h.rearrange("(c p) k -> p c k", p=128), gp)
        gp_dq = gp.n
        P.dma("gpsimd", wuq_b[:].rearrange("p (c k) -> p c k", c=8),
              wuq_h.rearrange("(c p) k -> p c k", p=128), gp)
        P.dma("gpsimd", wuk_b[:].rearrange("p (c k) -> p c k", c=8),
              wuk_h.rearrange("(c p) k -> p c k", p=128), gp)
        P.dma("gpsimd", wuv_b[:].rearrange("p (c k) -> p c k", c=8),
              wuv_h.rearrange("(c p) k -> p c k", p=128), gp)
        gp_w = gp.n
        P.dma("gpsimd", wo_b[:].rearrange("p (c k) -> p c k", c=4),
              wo_h[0:512, :].rearrange("(c p) k -> p c k", p=128), gp)
        gp_wo = [gp.n, None]
        P.dma("sync", rw_f[:], rw_h, sy)
        sy_c = sy.n

        tto_free = [0]

        def transpose_to(dst_sb, src_sb, nr, nf, pst):
            blocks = [(a, f) for a in range(nr) for f in range(nf)]
            for i0 in range(0, len(blocks), 8):
                grp = blocks[i0:i0 + 8]
                P.wait("tensor", ve, tto_free[0])
                for j, (a, f) in enumerate(grp):
                    P.op("tensor", "transpose", pst[:, j * 128:(j + 1) * 128],
                         src_sb[:, (a * nf + f) * 128:(a * nf + f) * 128 + 128],
                         identb[:], inc=(pe if j == len(grp) - 1 else None))
                P.wait("vector", pe, pe.n)
                for j, (a, f) in enumerate(grp):
                    P.op("vector", "tensor_copy",
                         out=dst_sb[:, (f * nr + a) * 128:(f * nr + a) * 128 + 128],
                         in_=pst[:, j * 128:(j + 1) * 128],
                         inc=(ve if j == len(grp) - 1 else None))
                tto_free[0] = ve.n

        P.wait("tensor", sy, sy_identb)
        P.wait("tensor", gp, gp_dq)
        transpose_to(wdqT, wdq_b, 2, 8, pt0)
        transpose_to(wdkvT, wdkv_b, 2, 8, pt0)

        P.wait("tensor", gp, gp_w)
        for mt in range(2):
            tgt = pa if mt == 0 else pb
            for ch in range(8):
                P.op("tensor", "matmul", tgt[:, 0:KVL],
                     wuq_b[:, ch * KVL + mt * 128: ch * KVL + mt * 128 + 128],
                     wuk_b[:, ch * KVL:(ch + 1) * KVL],
                     start=(ch == 0), stop=(ch == 7), inc=(pe if ch == 7 else None))
        P.wait("vector", pe, pe.n)
        P.op("vector", "tensor_copy", out=t1[:, 0:KVL], in_=pa[:, 0:KVL])
        P.op("vector", "tensor_copy", out=t1[:, KVL:2 * KVL], in_=pb[:, 0:KVL], inc=ve)
        P.wait("tensor", ve, ve.n)
        m_ps_free = [0, 0]
        for mt in range(8):
            tgt = pa if mt % 2 == 0 else pb
            P.wait("tensor", ve, m_ps_free[mt % 2])
            for ch in range(2):
                P.op("tensor", "matmul", tgt[:, 0:KVL],
                     wdq_b[:, ch * C + mt * 128: ch * C + mt * 128 + 128],
                     t1[:, ch * KVL:(ch + 1) * KVL],
                     start=(ch == 0), stop=(ch == 1), inc=(pe if ch == 1 else None))
            P.wait("vector", pe, pe.n)
            P.op("vector", "tensor_copy", out=m_b[:, mt * KVL:(mt + 1) * KVL],
                 in_=tgt[:, 0:KVL], inc=ve)
            m_ps_free[mt % 2] = ve.n
        P.wait("tensor", ve, ve.n)
        bht_ps_free = 0
        for h in range(NH):
            c2, off = h // 2, (h % 2) * 64
            P.wait("tensor", ve, bht_ps_free)
            for mt in range(2):
                tgt = pa if mt == 0 else pb
                P.op("tensor", "matmul", tgt[:, 0:KVL],
                     wuq_b[off:off + 64, c2 * KVL + mt * 128: c2 * KVL + mt * 128 + 128],
                     m_b[off:off + 64, c2 * KVL:(c2 + 1) * KVL],
                     start=True, stop=True, inc=(pe if mt == 1 else None))
            P.wait("vector", pe, pe.n)
            P.op("vector", "tensor_copy",
                 out=bht[:, (2 * h) * KVL:(2 * h + 1) * KVL], in_=pa[:, 0:KVL])
            P.op("vector", "tensor_copy",
                 out=bht[:, (2 * h + 1) * KVL:(2 * h + 2) * KVL], in_=pb[:, 0:KVL],
                 inc=ve)
            bht_ps_free = ve.n

        v_ps_free = 0
        for half in range(2):
            P.wait("tensor", gp, gp_wo[half])
            transpose_to(woT, wo_b, 4, 8, pt0)
            woT_done_h = ve.n
            if half == 0:
                P.wait("gpsimd", pe, pe.n)
                P.dma("gpsimd", wo_b[:].rearrange("p (c k) -> p c k", c=4),
                      wo_h[512:1024, :].rearrange("(c p) k -> p c k", p=128), gp)
                gp_wo[1] = gp.n
            P.wait("tensor", ve, woT_done_h)
            for mt in range(2):
                tgt = pa if mt == 0 else pb
                P.wait("tensor", ve, v_ps_free)
                for ch in range(8):
                    P.op("tensor", "matmul", tgt[:],
                         wuv_b[:, ch * KVL + mt * 128: ch * KVL + mt * 128 + 128],
                         woT[:, ch * 512: ch * 512 + 512],
                         start=(ch == 0), stop=(ch == 7),
                         inc=(pe if ch == 7 else None))
            P.wait("vector", pe, pe.n)
            P.op("vector", "tensor_copy",
                 out=v_b[:, half * 512: half * 512 + 512], in_=pa[:])
            P.op("vector", "tensor_copy",
                 out=v_b[:, C + half * 512: C + half * 512 + 512],
                 in_=pb[:], inc=ve)
            v_ps_free = ve.n
        P.wait("tensor", sy, sy_c)
        rwt_ps_free = v_ps_free
        for grp in range(2):
            P.wait("tensor", ve, rwt_ps_free)
            for j4 in range(4):
                j = grp * 4 + j4
                P.op("tensor", "transpose", pa[:, j4 * 32:(j4 + 1) * 32],
                     rw_f[:, j * 128:(j + 1) * 128], identf[0:32, 0:32],
                     inc=(pe if j4 == 3 else None))
            P.wait("vector", pe, pe.n)
            for j4 in range(4):
                j = grp * 4 + j4
                P.op("vector", "tensor_copy", out=rwT[:, j * NE:(j + 1) * NE],
                     in_=pa[:, j4 * 32:(j4 + 1) * 32],
                     inc=(ve if j4 == 3 else None))
            rwt_ps_free = ve.n
        pe_0a_done = pe.n
        pab_free_0a = ve.n
        proj_ps_free_init = ve.n
        s0a.close()

        sAt = ctx.enter_context(ExitStack())
        st6 = sb(sAt, "st6", [128, 96], F32)
        st2 = sb(sAt, "st2", [128, 16], F32)
        lnv = sb(sAt, "lnv", [128, 8], F32)
        rstd = sb(sAt, "rstd", [128, 8], F32)
        xnfm = sb(sAt, "xnfm", [128, 4096], BF16)
        gwf_ring = sb(sAt, "gwf_ring", [128, 4 * C], F32)
        gwb_ring = sb(sAt, "gwb_ring", [128, 2 * C], BF16)
        gt_ring = sb(sAt, "gt_ring", [128, 2 * C], BF16)
        sq_tmp = sb(sAt, "sq_tmp", [128, 2 * 512], BF16)
        ones_col = sb(sAt, "ones_col", [128, 1], BF16)
        nrm_tmp = sb(sAt, "nrm_tmp", [1, 16], F32)
        nq2 = sb(sAt, "nq2", [1, 4], F32)
        frx = sb(sAt, "frx", [1, 16], F32)
        frx2 = sb(sAt, "frx2", [1, 32], F32)

        ob_gw_free = [0, 0, 0, 0]
        ob_gt_free = [0, 0]
        ob_ps_free = [0, 0, 0]

        ob_gw_at = {}
        ob_gwb_free = [0, 0]

        ob_jobs = [(k, e) for k in ("g", "u", "d") for e in range(NE)]
        ob_jobs += [("sg", i) for i in range(4)] + [("su", i) for i in range(4)] \
            + [("sd", i) for i in range(4)]
        OB_NPAIRED = 2 * NE

        def ob_load_pair(j0):
            kind, e = ob_jobs[j0]
            slot = j0 % 4
            P.wait("gpsimd", ae, ob_gw_free[slot])
            P.wait("gpsimd", ae, ob_gw_free[slot + 1])
            dst = gwf_ring[:, slot * C: (slot + 2) * C]
            if kind == "g":
                P.dma("gpsimd", dst.rearrange("p (a k) -> p a k", a=2),
                      gw_h[e:e + 2].rearrange("a p k -> p a k"), gp)
            else:
                P.dma("gpsimd", dst.rearrange("p (a k) -> p a k", a=2),
                      uw_h[e:e + 2].rearrange("a p k -> p a k"), gp)
            ob_gw_at[j0] = gp.n
            ob_gw_at[j0 + 1] = gp.n

        def ob_load_single(jidx, src_h3, nr, nf):
            slot = jidx % 4
            ncols = nf * 128
            P.wait("gpsimd", ae, ob_gw_free[slot])
            P.dma("gpsimd",
                  gwf_ring[:, slot * C: slot * C + nr * ncols]
                  .rearrange("p (a k) -> p a k", a=nr),
                  src_h3, gp)
            ob_gw_at[jidx] = gp.n

        def ob_store_pair(jidx):
            kind, e = ob_jobs[jidx]
            if kind == "g":
                P.dma("scalar", gate_sc[e - 1:e + 1].rearrange("a p k -> p a k"),
                      gt_ring[:, 0:2 * C].rearrange("p (a k) -> p a k", a=2), sc)
            else:
                P.dma("scalar", up_sc[e - 1:e + 1].rearrange("a p k -> p a k"),
                      gt_ring[:, 0:2 * C].rearrange("p (a k) -> p a k", a=2), sc)
            ob_gt_free[0] = sc.n
            ob_gt_free[1] = sc.n

        def ob_compute(jidx, nr, nf, write_out_ap, gt_view):
            paired = jidx < OB_NPAIRED
            slot = jidx % 4
            gts = jidx % 2
            gb2 = jidx % 2
            P.wait("scalar", gp, ob_gw_at[jidx])
            P.wait("scalar", pe, ob_gwb_free[gb2])
            P.op("scalar", "activation", out=gwb_ring[:, gb2 * C:(gb2 + 1) * C],
                 in_=gwf_ring[:, slot * C:(slot + 1) * C], func=AF.Copy, inc=ae)
            ob_gw_free[slot] = ae.n
            P.wait("tensor", ae, ae.n)
            P.wait("tensor", sc, ob_gt_free[gts])
            P.wait("tensor", ve, ob_ps_free[jidx % 3])
            pst = (pob0, pob1, pob2)[jidx % 3]
            blocks = [(a, f) for a in range(nr) for f in range(nf)]
            for j, (a, f) in enumerate(blocks):
                P.op("tensor", "transpose", pst[:, j * 128:(j + 1) * 128],
                     gwb_ring[:, gb2 * C + (a * nf + f) * 128:
                              gb2 * C + (a * nf + f) * 128 + 128],
                     identb[:], inc=(pe if j == 7 else None))
            ob_gwb_free[gb2] = pe.n
            P.wait("vector", pe, pe.n)
            if nr == 1 or nf == 1:
                P.op("vector", "tensor_copy",
                     out=gt_ring[:, gts * C:(gts + 1) * C],
                     in_=pst[:, 0:1024], inc=ve)
            else:
                for j, (a, f) in enumerate(blocks):
                    P.op("vector", "tensor_copy",
                         out=gt_ring[:, gts * C + (f * nr + a) * 128:
                                     gts * C + (f * nr + a) * 128 + 128],
                         in_=pst[:, j * 128:(j + 1) * 128],
                         inc=(ve if j == 7 else None))
            ob_ps_free[jidx % 3] = ve.n
            P.wait("scalar", ve, ve.n)
            if paired:
                if gts == 1:
                    ob_store_pair(jidx)
            else:
                P.dma("scalar", write_out_ap,
                      gt_view(gt_ring[:, gts * C: gts * C + C]), sc)
                ob_gt_free[gts] = sc.n

        def ob_spec(job):
            kind, e = job
            if kind == "g":
                return (gw_h[e].rearrange("(a p) k -> p a k", p=128), 1, 8,
                        gate_sc[e], lambda v: v)
            if kind == "u":
                return (uw_h[e].rearrange("(a p) k -> p a k", p=128), 1, 8,
                        up_sc[e], lambda v: v)
            if kind == "d":
                return (dw_h[e].rearrange("(a p) k -> p a k", p=128), 8, 1,
                        down_sc[:, :, e * 256:(e + 1) * 256]
                        .rearrange("dc p k -> p dc k"),
                        lambda v: v.rearrange("p (a k) -> p a k", a=4))
            if kind == "sg":
                return (sgw_h[e * 128:(e + 1) * 128, :]
                        .rearrange("(a p) k -> p a k", p=128), 1, 8,
                        sg_sc.rearrange("p (cc q k) -> p cc q k", cc=8, q=4)
                        [:, :, e, :],
                        lambda v: v.rearrange("p (a k) -> p a k", a=8))
            if kind == "su":
                return (suw_h[e * 128:(e + 1) * 128, :]
                        .rearrange("(a p) k -> p a k", p=128), 1, 8,
                        su_sc.rearrange("p (cc q k) -> p cc q k", cc=8, q=4)
                        [:, :, e, :],
                        lambda v: v.rearrange("p (a k) -> p a k", a=8))
            if kind == "sd":
                return (sdw_h[e * 256:(e + 1) * 256, :]
                        .rearrange("(a p) k -> p a k", p=128), 2, 4,
                        sd_sc.rearrange("p (jc dc k) -> p jc dc k", jc=4, dc=8)
                        [:, :, e * 2:e * 2 + 2, :],
                        lambda v: v.rearrange("p (a b k) -> p a b k", a=4, b=2))

        ob_next_load = [0]
        ob_next_comp = [0]

        def ob_pump(n):
            for _ in range(n):
                if ob_next_comp[0] >= len(ob_jobs):
                    return
                while (ob_next_load[0] < len(ob_jobs)
                       and ob_next_load[0] < ob_next_comp[0] + 3):
                    jl = ob_next_load[0]
                    if jl < OB_NPAIRED:
                        ob_load_pair(jl)
                        ob_next_load[0] += 2
                    else:
                        s = ob_spec(ob_jobs[jl])
                        ob_load_single(jl, s[0], s[1], s[2])
                        ob_next_load[0] += 1
                jc_ = ob_next_comp[0]
                s = ob_spec(ob_jobs[jc_])
                ob_compute(jc_, s[1], s[2], s[3], s[4])
                ob_next_comp[0] += 1

        x_slot_free = [0] * 6
        xn_pe_done = [0, 0, 0, 0]
        xnfm_pe_free = [0, 0]
        proj_ps_free = [0, pab_free_0a]
        ob_gw_free = [0, 0, 0, 0]
        ob_ps_free = [proj_ps_free_init] * 3
        P.wait("gpsimd", pe, pe_0a_done)
        P.wait("gpsimd", ve, proj_ps_free_init)
        rstd_at = {}
        stat_at = {}
        evac_at = [0, 0, 0, 0]

        loads_at = {}

        def a_loads(ck, ts):
            for t in ts:
                tt = ck * 4 + t
                xs = tt % 6
                P.wait("sync", ve, x_slot_free[xs])
                P.dma("sync", x_ring[:, xs * C:(xs + 1) * C],
                      x_h[tt * 128:(tt + 1) * 128, :], sy)
                loads_at[(ck, t)] = sy.n

        def a_s1a(ck):
            sc2 = ck % 2
            for t in range(2):
                xs = (ck * 4 + t) % 6
                P.wait("vector", sy, loads_at[(ck, t)])
                P.op("vector", "bn_stats", out=st6[:, sc2 * 48 + t * 12: sc2 * 48 + t * 12 + 6],
                     in_=x_ring[:, xs * C: xs * C + 512])
                P.op("vector", "bn_stats",
                     out=st6[:, sc2 * 48 + t * 12 + 6: sc2 * 48 + t * 12 + 12],
                     in_=x_ring[:, xs * C + 512: xs * C + 1024])

        def a_s1b(ck):
            sc2 = ck % 2
            for t in range(2, 4):
                xs = (ck * 4 + t) % 6
                P.wait("vector", sy, loads_at[(ck, t)])
                P.op("vector", "bn_stats", out=st6[:, sc2 * 48 + t * 12: sc2 * 48 + t * 12 + 6],
                     in_=x_ring[:, xs * C: xs * C + 512])
                P.op("vector", "bn_stats",
                     out=st6[:, sc2 * 48 + t * 12 + 6: sc2 * 48 + t * 12 + 12],
                     in_=x_ring[:, xs * C + 512: xs * C + 1024],
                     inc=(ve if t == 3 else None))
            P.wait("vector", ve, ve.n)
            for t in range(4):
                P.op("vector", "bn_aggr", out=st2[:, sc2 * 8 + t * 2: sc2 * 8 + t * 2 + 2],
                     in_=st6[:, sc2 * 48 + t * 12: sc2 * 48 + t * 12 + 12],
                     inc=(ve if t == 3 else None))
            P.wait("vector", ve, ve.n)
            P.op("vector", "tensor_scalar", out=lnv[:, sc2 * 4: sc2 * 4 + 4],
                 in0=st2[:, sc2 * 8 + 1: sc2 * 8 + 8: 2],
                 scalar1=1e-5, scalar2=None, op0=ALU.add, inc=ve)
            P.wait("scalar", ve, ve.n)
            P.op("scalar", "activation", out=rstd[:, sc2 * 4: sc2 * 4 + 4],
                 in_=lnv[:, sc2 * 4: sc2 * 4 + 4], func=AF.Ln, inc=ae)
            P.wait("scalar", ae, ae.n)
            P.op("scalar", "activation", out=rstd[:, sc2 * 4: sc2 * 4 + 4],
                 in_=rstd[:, sc2 * 4: sc2 * 4 + 4], func=AF.Exp, scale=-0.5, inc=ae)
            rstd_at[ck] = ae.n

        xn_at_g = {}

        def a_apply(ck, t):
            sc2 = ck % 2
            xs = (ck * 4 + t) % 6
            if t == 0:
                P.wait("vector", ae, rstd_at[ck])
            P.wait("vector", pe, xn_pe_done[t])
            P.op("vector", "tensor_scalar",
                 out=xn_ring[:, t * C:(t + 1) * C],
                 in0=x_ring[:, xs * C:(xs + 1) * C],
                 scalar1=st2[:, sc2 * 8 + t * 2: sc2 * 8 + t * 2 + 1],
                 scalar2=rstd[:, sc2 * 4 + t: sc2 * 4 + t + 1],
                 op0=ALU.subtract, op1=ALU.mult, inc=ve)
            x_slot_free[xs] = ve.n
            xn_at_g[(ck, t)] = ve.n

        def a_s2b(ck):
            cbuf = 0
            rate = 0 if ck < 2 else 3
            for t in range(4):
                ob_pump(rate)
                pst = pt0 if t % 2 == 0 else pt1
                P.wait("tensor", ve, xn_at_g.pop((ck, t)))
                if t >= 2:
                    P.wait("tensor", ve, evac_at[t - 2])
                for j in range(8):
                    P.op("tensor", "transpose", pst[:, j * 128:(j + 1) * 128],
                         xn_ring[:, t * C + j * 128: t * C + j * 128 + 128],
                         identb[:], inc=(pe if j == 7 else None))
                xn_pe_done[t] = pe.n
                P.wait("vector", pe, pe.n)
                P.wait("vector", pe, xnfm_pe_free[cbuf])
                P.op("vector", "tensor_copy",
                     out=xnfm[:, cbuf * 4096:(cbuf + 1) * 4096]
                     .rearrange("p (c q) -> p c q", c=8)[:, :, t * 128:(t + 1) * 128],
                     in_=pst[:].rearrange("p (c q) -> p c q", c=8),
                     inc=ve)
                evac_at[t] = ve.n
                if ck + 1 < 8:
                    a_apply(ck + 1, t)
            xnfm_ready = ve.n
            P.wait("tensor", ve, xnfm_ready)
            for gi, (wT, mt, dst) in enumerate(
                    ((wdkvT, 0, ckv), (wdkvT, 1, ckv),
                     (wdqT, 0, qlow), (wdqT, 1, qlow))):
                ob_pump(0 if ck < 2 else 2)
                tgt = pf if gi % 2 == 0 else pb
                P.wait("tensor", ve, proj_ps_free[gi % 2])
                for ch in range(8):
                    P.op("tensor", "matmul", tgt[:],
                         wT[:, ch * KVL + mt * 128: ch * KVL + mt * 128 + 128],
                         xnfm[:, cbuf * 4096 + ch * 512:
                              cbuf * 4096 + ch * 512 + 512],
                         start=(ch == 0), stop=(ch == 7),
                         inc=(pe if ch == 7 else None))
                P.wait("vector", pe, pe.n)
                P.op("vector", "tensor_copy",
                     out=dst[:, mt * T + ck * 512: mt * T + ck * 512 + 512],
                     in_=tgt[:], inc=ve)
                proj_ps_free[gi % 2] = ve.n
            xnfm_pe_free[cbuf] = pe.n

        a_loads(0, range(4))
        a_loads(1, (0, 1))
        a_s1a(0)
        a_s1b(0)
        for t in range(4):
            a_apply(0, t)
        for ck in range(8):
            if ck + 1 < 8:
                a_s1a(ck + 1)
            if ck + 1 < 8:
                a_loads(ck + 1, (2, 3))
            if ck + 2 < 8:
                a_loads(ck + 2, (0, 1))
            if ck + 1 < 8:
                a_s1b(ck + 1)
            a_s2b(ck)
        ob_pump(len(ob_jobs))
        ob_all_done = sc.n

        P.op("vector", "memset",
             cv[:].rearrange("p (kt h o) -> p kt h o", kt=32, h=NH)[:, :, :, 64:65],
             1.0, inc=ve)
        cv_ps_free = [pab_free_0a, proj_ps_free[1]]
        for kt in range(32):
            for half in range(2):
                tgt = pa if half == 0 else pb
                P.wait("tensor", ve, cv_ps_free[half])
                for ch in range(2):
                    P.op("tensor", "matmul", tgt[:],
                         ckv[:, ch * T + kt * 128: ch * T + kt * 128 + 128],
                         v_b[:, ch * C + half * 512: ch * C + half * 512 + 512],
                         start=(ch == 0), stop=(ch == 1),
                         inc=(pe if ch == 1 else None))
                P.wait("vector", pe, pe.n)
                base = kt * NH * CVW + half * 8 * CVW
                P.op("vector", "tensor_copy",
                     out=cv[:, base: base + 8 * CVW]
                     .rearrange("p (h z) -> p h z", h=8)[:, :, 0:64],
                     in_=tgt[:].rearrange("p (h d) -> p h d", h=8),
                     inc=ve)
                cv_ps_free[half] = ve.n
        P.op("vector", "memset", ones_col[:], 1.0, inc=ve)
        ones_done = ve.n
        nrm_ps_free = cv_ps_free[0]
        sq_free = [0, 0]
        nrm_idx = 0
        for which, srcbuf in ((0, qlow), (1, ckv)):
            for kk in range(8):
                par2 = kk % 2
                tgt = pa if par2 == 0 else pb
                for ch in range(2):
                    P.wait("scalar", pe, sq_free[ch])
                    P.op("scalar", "activation",
                         out=sq_tmp[:, ch * 512:(ch + 1) * 512],
                         in_=srcbuf[:, ch * T + kk * 512: ch * T + kk * 512 + 512],
                         func=AF.Square, inc=ae)
                sq_done = ae.n
                P.wait("tensor", ae, sq_done)
                P.wait("tensor", ve, nrm_ps_free)
                P.wait("tensor", ve, ones_done)
                for ch in range(2):
                    P.op("tensor", "matmul", tgt[0:1, 0:512],
                         ones_col[:], sq_tmp[:, ch * 512:(ch + 1) * 512],
                         start=(ch == 0), stop=(ch == 1),
                         inc=(pe if ch == 1 else None))
                sq_free[0] = pe.n
                sq_free[1] = pe.n
                P.wait("vector", pe, pe.n)
                P.op("vector", "reduce_max",
                     out=nrm_tmp[:, nrm_idx: nrm_idx + 1],
                     in_=tgt[0:1, 0:512], axis=AX.X, inc=ve)
                nrm_ps_free = ve.n
                nrm_idx += 1
            P.wait("vector", ve, ve.n)
            P.op("vector", "reduce_max", out=nq2[:, which:which + 1],
                 in_=nrm_tmp[:, 0:8] if which == 0 else nrm_tmp[:, 8:16],
                 axis=AX.X, inc=ve)
        nq2_done = ve.n
        sqb_free = [sq_free[0], sq_free[1]]
        frx_evac = [nrm_ps_free, nrm_ps_free]
        for j in range(NH):
            sl = j % 2
            tgt = pa if sl == 0 else pb
            P.wait("scalar", pe, sqb_free[sl])
            P.op("scalar", "activation", out=sq_tmp[:, sl * 512:(sl + 1) * 512],
                 in_=bht[:, j * 512:(j + 1) * 512], func=AF.Square, inc=ae)
            P.wait("tensor", ae, ae.n)
            P.wait("tensor", ve, frx_evac[sl])
            P.op("tensor", "matmul", tgt[0:1, 0:512], ones_col[:],
                 sq_tmp[:, sl * 512:(sl + 1) * 512], start=True, stop=True,
                 inc=pe)
            sqb_free[sl] = pe.n
            P.wait("vector", pe, pe.n)
            P.op("vector", "reduce_sum", out=frx[:, j:j + 1],
                 in_=tgt[0:1, 0:512], axis=AX.X, inc=ve)
            frx_evac[sl] = ve.n
        P.wait("vector", ve, nq2_done)
        P.wait("vector", ve, ve.n)
        P.op("vector", "tensor_tensor", out=nq2[:, 2:3], in0=nq2[:, 0:1],
             in1=nq2[:, 1:2], op=ALU.mult, inc=ve)
        P.wait("vector", ve, ve.n)
        P.op("vector", "tensor_tensor", out=frx2[:, 0:16], in0=frx[:, 0:16],
             in1=nq2[:, 2:3].broadcast_to([1, 16]), op=ALU.mult, inc=ve)
        P.wait("scalar", ve, ve.n)
        P.op("scalar", "activation", out=frx2[:, 16:32], in_=frx2[:, 0:16],
             func=AF.Ln, inc=ae)
        P.wait("scalar", ae, ae.n)
        P.op("scalar", "activation", out=frx2[:, 0:16], in_=frx2[:, 16:32],
             func=AF.Exp, scale=0.5, inc=ae)
        P.wait("vector", ae, ae.n)
        P.op("vector", "tensor_scalar", out=frx2[:, 16:32], in0=frx2[:, 0:16],
             scalar1=-0.12875, scalar2=-80.0, op0=ALU.mult, op1=ALU.max, inc=ve)
        P.wait("tensor", ve, ve.n)
        P.op("tensor", "matmul", pa[:, 0:16], onesf[0:1, :], frx2[:, 16:32],
             start=True, stop=True, inc=pe)
        P.wait("vector", pe, pe.n)
        P.op("vector", "tensor_copy", out=negc_all[:], in_=pa[:, 0:16], inc=ve)
        negc_all_done = ve.n

        cv_done = ve.n
        if stop_after == "A":
            finish([("d_ckv", ckv), ("d_qlow", qlow), ("d_cv", cv), ("d_bht", bht),
                    ("d_t1", t1), ("d_m", m_b), ("d_v", v_b), ("d_woT", woT),
                    ("d_wdkvT", wdkvT), ("d_xnfm", xnfm)])
            return nc
        sAt.close()
        sXr.close()
        sA.close()

        sBC = ctx.enter_context(ExitStack())
        yfm = sb(sBC, "yfm", [128, 8 * T], BF16, side="right")
        sBt = ctx.enter_context(ExitStack())
        P.wait("vector", sc, ob_all_done)
        P.wait("scalar", sc, ob_all_done)
        P.wait("vector", gp, gp.n)
        P.wait("scalar", gp, gp.n)
        qlat = sb(sBt, "qlat", [128, 2 * 1024], BF16)
        ptb = sb(sBt, "ptb", [128, 2 * 1024], BF16)
        masks = sb(sBt, "masks", [128, 128], F32)
        P.wait("sync", sc, ob_all_done)
        P.wait("sync", ve, negc_all_done)
        P.wait("sync", ve, cv_done)
        P.dma("sync", masks[:], masks_h, sy)
        masks_ld = sy.n
        P.wait("tensor", sy, masks_ld)
        srow = sb(sBt, "srow", [1, 512], F32)
        rsrow = sb(sBt, "rsrow", [1, 512], BF16)
        rsb_sb = sb(sBt, "rsb_sb", [64, 512], BF16)
        sPA.close()
        sPB = ctx.enter_context(ExitStack())
        ps_s0 = psm(sPB, "ps_s0", [128, 1024], F32)
        ps_s1 = psm(sPB, "ps_s1", [128, 1024], F32)
        ps_y0 = psm(sPB, "ps_y0", [128, 512], F32)
        ps_y1 = psm(sPB, "ps_y1", [128, 512], F32)

        sps_free = [0, 0]
        pt_free = [0, 0]
        pair_spans = {}
        mask_at = {}
        psy_free = [0, 0]
        ql_ps_free = [cv_done, cv_done]
        negc_at = {}
        ql_at = {}
        P.wait("tensor", ve, cv_done)
        P.wait("tensor", ve, negc_all_done)
        P.wait("scalar", ve, negc_all_done)
        P.wait("tensor", ae, ae.n)

        def b_prep(it2):
            qg2, h2 = it2 // NH, it2 % NH
            par = it2 % 2
            for mt in range(2):
                tgt = pa if mt == 0 else pb
                P.wait("tensor", ve, ql_ps_free[mt])
                for ch in range(2):
                    P.op("tensor", "matmul", tgt[:],
                         bht[:, (2 * h2 + ch) * KVL + mt * 128:
                             (2 * h2 + ch) * KVL + mt * 128 + 128],
                         qlow[:, ch * T + qg2 * 512: ch * T + qg2 * 512 + 512],
                         start=(ch == 0), stop=(ch == 1),
                         inc=(pe if ch == 1 else None))
            P.wait("vector", pe, pe.n)
            for mt in range(2):
                tgt = pa if mt == 0 else pb
                P.op("vector", "tensor_copy",
                     out=qlat[:, par * 1024 + mt * 512: par * 1024 + mt * 512 + 512],
                     in_=tgt[:], inc=ve)
            ql_evac = ve.n
            ql_at[it2] = ql_evac
            ql_ps_free[0] = ql_evac
            ql_ps_free[1] = ql_evac

        def b_smm(it2, p):
            qg2, h2 = it2 // NH, it2 % NH
            par = it2 % 2
            pp = p % 2
            sp = ps_s0 if pp == 0 else ps_s1
            P.wait("tensor", ae, sps_free[pp])
            off = 0
            spans = []
            for k2 in range(2):
                kb = 2 * p + k2
                d = kb - 4 * qg2
                qoff = 128 * d if d > 0 else 0
                w = 512 - qoff
                spans.append((off, qoff, w, d))
                for ch in range(2):
                    P.op("tensor", "matmul", sp[:, off:off + w],
                         ckv[:, ch * T + kb * 128: ch * T + kb * 128 + 128],
                         qlat[:, par * 1024 + ch * 512 + qoff:
                              par * 1024 + ch * 512 + qoff + w],
                         start=(ch == 0), stop=(ch == 1),
                         inc=(pe if (ch == 1 and k2 == 1) else None))
                off += w
            s_mm = pe.n
            pair_spans[(it2, p)] = spans
            m_at = None
            for (offk, qoff, w, d) in spans:
                if d >= 0:
                    P.wait("vector", pe, s_mm)
                    P.op("vector", "tensor_tensor",
                         out=sp[:, offk:offk + 128],
                         in0=sp[:, offk:offk + 128],
                         in1=masks[:], op=ALU.add, inc=ve)
                    m_at = ve.n
            P.wait("scalar", pe, s_mm)
            if m_at is not None:
                P.wait("scalar", ve, m_at)
            P.wait("scalar", pe, pt_free[pp])
            P.op("scalar", "activation",
                 out=ptb[:, pp * 1024: pp * 1024 + off],
                 in_=sp[:, 0:off], func=AF.Exp, scale=0.125,
                 bias=negc_all[:, h2:h2 + 1], inc=ae)
            sps_free[pp] = ae.n

        def b_ymm(it2, p, npair, yps):
            qg2, h2 = it2 // NH, it2 % NH
            pp = p % 2
            P.wait("tensor", ae, sps_free[pp])
            spans = pair_spans.pop((it2, p))
            for k2 in range(2):
                kb = 2 * p + k2
                off, qoff, w, d = spans[k2]
                P.op("tensor", "matmul", yps[0:CVW, qoff:qoff + w],
                     cv[:, kb * NH * CVW + h2 * CVW:
                        kb * NH * CVW + (h2 + 1) * CVW],
                     ptb[:, pp * 1024 + off: pp * 1024 + off + w],
                     start=(kb == 0), stop=(kb == npair * 2 - 1),
                     inc=(pe if k2 == 1 else None))
            pt_free[pp] = pe.n

        def b_tail_a(it2, yps, yb):
            P.wait("vector", pe, pe.n)
            P.op("vector", "tensor_copy", out=srow[:], in_=yps[64:65, :], inc=ve)
            P.wait("scalar", ve, ve.n)
            P.op("scalar", "activation", out=srow[:], in_=srow[:], func=AF.Ln,
                 inc=ae)
            P.wait("scalar", ae, ae.n)
            P.op("scalar", "activation", out=rsrow[:], in_=srow[:],
                 func=AF.Exp, scale=-1.0, inc=ae)
            return ae.n

        def b_tail_b(it2, yps, yb, rs_at):
            qg2, h2 = it2 // NH, it2 % NH
            P.wait("tensor", ae, rs_at)
            P.op("tensor", "matmul", yps[64:128, :], onesb[:], rsrow[:],
                 start=True, stop=True, inc=pe)
            P.wait("vector", pe, pe.n)
            P.op("vector", "tensor_copy", out=rsb_sb[:], in_=yps[64:128, :],
                 inc=ve)
            P.wait("vector", ve, ve.n)
            co = (h2 // 2) * T + qg2 * 512
            po = (h2 % 2) * 64
            P.op("vector", "tensor_tensor",
                 out=yfm[po:po + 64, co:co + 512],
                 in0=yps[0:64, :], in1=rsb_sb[:], op=ALU.mult, inc=ve)
            psy_free[yb] = ve.n

        NIT = NQG * NH
        b_prep(0)
        pend = None
        for it in range(NIT):
            qg = it // NH
            npair = 2 * qg + 2
            yb = it % 2
            yps = ps_y0 if yb == 0 else ps_y1
            P.wait("tensor", ve, psy_free[yb])
            P.wait("tensor", ve, ql_at[it])
            rs_at = None
            for p in range(npair + 1):
                if p < npair:
                    b_smm(it, p)
                if p == 1:
                    if pend is not None:
                        rs_at = b_tail_a(*pend)
                    if it + 1 < NIT:
                        b_prep(it + 1)
                if p == max(1, npair - 1) and pend is not None:
                    b_tail_b(*pend, rs_at)
                    pend = None
                if p >= 1:
                    b_ymm(it, p - 1, npair, yps)
            pend = (it, yps, yb)
        rs_at = b_tail_a(*pend)
        b_tail_b(*pend, rs_at)
        attn_done = ve.n
        if stop_after == "B":
            finish([("d_y", yfm), ("d_ckv", ckv), ("d_cv", cv), ("d_negc", negc_all)])
            return nc

        sBt.close()
        sAB.close()
        sPB.close()

        sCD = ctx.enter_context(ExitStack())
        xn2 = sb(sCD, "xn2", [128, 8 * T], BF16)
        w_fm = sb(sCD, "w_fm", [NE, T], BF16)
        sCt = ctx.enter_context(ExitStack())
        xC = sb(sCt, "xC", [128, 4 * C], F32)
        x2_sb = sb(sCt, "x2_sb", [128, 8 * C], F32)
        xn2t = sb(sCt, "xn2t", [128, 2 * C], F32)
        xn2T = sb(sCt, "xn2T", [128, 2 * C], F32)
        c_st6 = sb(sCt, "c_st6", [128, 96], F32)
        c_st2 = sb(sCt, "c_st2", [128, 16], F32)
        c_lnv = sb(sCt, "c_lnv", [128, 8], F32)
        c_rstd = sb(sCt, "c_rstd", [128, 8], F32)
        lg_sb = sb(sCt, "lg_sb", [128, 4 * NE], F32)
        tk_t0 = sb(sCt, "tk_t0", [128, 4 * NE], F32)
        tk_mx = sb(sCt, "tk_mx", [128, 4 * 4], F32)
        tk_msk = sb(sCt, "tk_msk", [128, 4 * NE], F32)
        tk_es = sb(sCt, "tk_es", [128, 4 * NE], F32)
        tk_s = sb(sCt, "tk_s", [128, 8], F32)
        w_tok = sb(sCt, "w_tok", [128, 2 * 4 * NE], F32)

        sPC = ctx.enter_context(ExitStack())
        pc_t0 = psm(sPC, "pc_t0", [128, 1024], BF16)
        pc_t1 = psm(sPC, "pc_t1", [128, 1024], BF16)
        pc_f = psm(sPC, "pc_f", [128, 512], F32)
        pc_f2 = psm(sPC, "pc_f2", [128, 512], F32)

        xC_free = [0, 0, 0, 0]
        x2_free4 = [0] * 8
        x2_dma_done = [0] * 8
        yT_ps_free = [0, 0]
        xn2T_ps_free = [0, 0]
        wfm_ps_free = 0
        wfm_pe_at = [0, 0]
        router_ps_free = 0
        xn2T_ps_free_pe = [0, 0]
        xn2T_router_free = [0, 0]
        c_rstd_at = {}
        P.wait("sync", ve, attn_done)
        g4 = lambda v: v.rearrange("p (g k) -> p g k", g=4)

        c_pend = [None]

        def c_trans(grp, t):
            tt = grp * 4 + t
            cb = t % 2
            pst = pc_t0 if cb == 0 else pc_t1
            P.wait("tensor", ae, yT_ps_free[cb])
            if tt == 0:
                P.wait("tensor", ve, attn_done)
            for j in range(8):
                P.op("tensor", "transpose", pst[:, j * 128:(j + 1) * 128],
                     yfm[:, j * T + tt * 128: j * T + tt * 128 + 128],
                     identb[:], inc=(pe if j == 7 else None))
            return pe.n

        def c_evac(grp, t, load_at, guard_mm):
            gb = grp % 2
            tt = grp * 4 + t
            t8 = gb * 4 + t
            cb = t % 2
            pst = pc_t0 if cb == 0 else pc_t1
            P.wait("scalar", pe, guard_mm)
            P.wait("scalar", gp, x2_dma_done[t8])
            P.wait("scalar", ve, x2_free4[t8])
            P.op("scalar", "activation", out=x2_sb[:, t8 * C:(t8 + 1) * C],
                 in_=pst[:], func=AF.Copy, inc=ae)
            yT_ps_free[cb] = ae.n
            P.wait("vector", ae, ae.n)
            P.wait("vector", sy, load_at)
            P.op("vector", "tensor_tensor", out=x2_sb[:, t8 * C:(t8 + 1) * C],
                 in0=x2_sb[:, t8 * C:(t8 + 1) * C],
                 in1=xC[:, t * C:(t + 1) * C], op=ALU.add, inc=ve)
            xC_free[t] = ve.n
            P.wait("gpsimd", ve, ve.n)
            P.dma("gpsimd", x2_sc[tt * 128:(tt + 1) * 128, :],
                  x2_sb[:, t8 * C:(t8 + 1) * C], gp)
            x2_dma_done[t8] = gp.n
            P.wait("vector", ve, ve.n)
            P.op("vector", "bn_stats",
                 out=c_st6[:, gb * 48 + t * 12: gb * 48 + t * 12 + 6],
                 in_=x2_sb[:, t8 * C: t8 * C + 512])
            P.op("vector", "bn_stats",
                 out=c_st6[:, gb * 48 + t * 12 + 6: gb * 48 + t * 12 + 12],
                 in_=x2_sb[:, t8 * C + 512: t8 * C + 1024],
                 inc=(ve if t == 3 else None))
            if t != 3:
                return
            P.wait("vector", ve, ve.n)
            for t2 in range(4):
                P.op("vector", "bn_aggr",
                     out=c_st2[:, gb * 8 + t2 * 2: gb * 8 + t2 * 2 + 2],
                     in_=c_st6[:, gb * 48 + t2 * 12: gb * 48 + t2 * 12 + 12],
                     inc=(ve if t2 == 3 else None))
            P.wait("vector", ve, ve.n)
            P.op("vector", "tensor_scalar", out=c_lnv[:, gb * 4: gb * 4 + 4],
                 in0=c_st2[:, gb * 8 + 1: gb * 8 + 8: 2],
                 scalar1=1e-5, scalar2=None, op0=ALU.add, inc=ve)
            P.wait("scalar", ve, ve.n)
            P.op("scalar", "activation", out=c_rstd[:, gb * 4: gb * 4 + 4],
                 in_=c_lnv[:, gb * 4: gb * 4 + 4], func=AF.Ln, inc=ae)
            P.wait("scalar", ae, ae.n)
            P.op("scalar", "activation", out=c_rstd[:, gb * 4: gb * 4 + 4],
                 in_=c_rstd[:, gb * 4: gb * 4 + 4], func=AF.Exp, scale=-0.5,
                 inc=ae)
            c_rstd_at[grp] = ae.n

        def c_s1(grp):
            g_mm = c_trans(grp, 0)
            if c_pend[0] is not None:
                c_evac(*c_pend[0], g_mm)
            loads = []
            for t in range(4):
                tt = grp * 4 + t
                P.wait("sync", ve, xC_free[t])
                P.dma("sync", xC[:, t * C:(t + 1) * C],
                      x_h[tt * 128:(tt + 1) * 128, :], sy)
                loads.append(sy.n)
            prev = (grp, 0, loads[0])
            for t in range(1, 4):
                g_mm = c_trans(grp, t)
                c_evac(*prev, g_mm)
                prev = (grp, t, loads[t])
            c_pend[0] = prev


        def c_heavy(grp, t, xn2_done):
            gb = grp % 2
            tt = grp * 4 + t
            cb = t % 2
            pstf = pc_f if cb == 0 else pc_f2
            P.wait("tensor", ve, xn2_done)
            P.wait("tensor", ve, xn2T_ps_free[cb])
            for gg in range(2):
                for j4 in range(4):
                    j = gg * 4 + j4
                    P.op("tensor", "transpose",
                         pstf[:, j4 * 128:(j4 + 1) * 128],
                         xn2t[:, cb * C + j * 128: cb * C + j * 128 + 128],
                         identf[:], inc=(pe if j4 == 3 else None))
                if gg == 0:
                    xn2T_ps_free_pe[cb] = pe.n
                P.wait("vector", pe, pe.n)
                for j4 in range(4):
                    j = gg * 4 + j4
                    P.op("vector", "tensor_copy",
                         out=xn2T[:, cb * C + j * 128: cb * C + j * 128 + 128],
                         in_=pstf[:, j4 * 128:(j4 + 1) * 128],
                         inc=(ve if j4 == 3 else None))
                if gg == 0:
                    P.wait("tensor", ve, ve.n)
            xn2T_ps_free[cb] = ve.n
            xn2T_ps_free_pe[cb] = pe.n
            xn2T_done = ve.n
            P.wait("vector", ve, ve.n)
            P.op("vector", "tensor_copy",
                 out=xn2[:].rearrange("p (c q) -> p c q", c=8)
                 [:, :, tt * 128:(tt + 1) * 128],
                 in_=xn2T[:, cb * C:(cb + 1) * C]
                 .rearrange("p (c q) -> p c q", c=8),
                 inc=ve)
            P.wait("tensor", ve, xn2T_done)
            if t == 0:
                P.wait("tensor", ve, router_ps_free)
            for j in range(8):
                P.op("tensor", "matmul", pa[:, t * NE:(t + 1) * NE],
                     xn2T[:, cb * C + j * 128: cb * C + j * 128 + 128],
                     rwT[:, j * NE:(j + 1) * NE],
                     start=(j == 0), stop=False)
            P.op("tensor", "matmul", pa[:, t * NE:(t + 1) * NE], onesf[:],
                 rbias[:], start=False, stop=(t == 3),
                 inc=(pe if t == 3 else None))

        def c_s2(grp):
            nonlocal router_ps_free, wfm_ps_free
            gb = grp % 2
            P.wait("vector", ae, c_rstd_at[grp])
            xn_at = {}
            for t in range(4):
                cb = t % 2
                t8 = gb * 4 + t
                if t >= 2:
                    P.wait("vector", pe, xn2T_ps_free_pe[cb])
                    P.wait("vector", pe, xn2T_router_free[cb])
                P.op("vector", "tensor_scalar", out=xn2t[:, cb * C:(cb + 1) * C],
                     in0=x2_sb[:, t8 * C:(t8 + 1) * C],
                     scalar1=c_st2[:, gb * 8 + t * 2: gb * 8 + t * 2 + 1],
                     scalar2=c_rstd[:, gb * 4 + t: gb * 4 + t + 1],
                     op0=ALU.subtract, op1=ALU.mult, inc=ve)
                x2_free4[t8] = ve.n
                xn_at[t] = ve.n
                if t % 2 == 0:
                    continue
                for tt2 in (t - 1, t):
                    c_heavy(grp, tt2, xn_at[tt2])
            r_mm = pe.n
            xn2T_router_free[0] = r_mm
            xn2T_router_free[1] = r_mm
            lg = lg_sb
            t0 = tk_t0
            msk = tk_msk
            es = tk_es
            mx = tk_mx
            ts_ = tk_s
            wt = w_tok[:, gb * 4 * NE:(gb + 1) * 4 * NE]
            P.wait("vector", pe, r_mm)
            P.wait("vector", pe, wfm_pe_at[gb])
            P.op("vector", "tensor_copy", out=lg[:], in_=pa[:, 0:4 * NE], inc=ve)
            router_ps_free = ve.n
            P.wait("vector", ve, ve.n)
            P.op("vector", "reduce_max", out=mx[:, 0:4], in_=g4(lg[:]),
                 axis=AX.X)
            P.op("vector", "tensor_copy", out=t0[:], in_=lg[:], inc=ve)
            for k in range(3):
                P.wait("vector", ve, ve.n)
                P.op("vector", "tensor_tensor", out=g4(msk[:]), in0=g4(t0[:]),
                     in1=mx[:, k * 4:(k + 1) * 4]
                     .rearrange("p (g o) -> p g o", o=1).broadcast_to([128, 4, NE]),
                     op=ALU.is_ge, inc=ve)
                P.wait("vector", ve, ve.n)
                P.op("vector", "scalar_tensor_tensor", out=t0[:], in0=msk[:],
                     scalar=-1e30, in1=t0[:], op0=ALU.mult, op1=ALU.add, inc=ve)
                P.wait("vector", ve, ve.n)
                P.op("vector", "reduce_max", out=mx[:, (k + 1) * 4:(k + 2) * 4],
                     in_=g4(t0[:]), axis=AX.X, inc=ve)
            P.wait("vector", ve, ve.n)
            P.op("vector", "tensor_tensor", out=g4(msk[:]), in0=g4(lg[:]),
                 in1=mx[:, 12:16].rearrange("p (g o) -> p g o", o=1)
                 .broadcast_to([128, 4, NE]), op=ALU.is_ge, inc=ve)
            P.wait("vector", ve, ve.n)
            P.op("vector", "tensor_tensor", out=g4(es[:]), in0=g4(lg[:]),
                 in1=mx[:, 0:4].rearrange("p (g o) -> p g o", o=1)
                 .broadcast_to([128, 4, NE]), op=ALU.subtract, inc=ve)
            P.wait("scalar", ve, ve.n)
            P.op("scalar", "activation", out=es[:], in_=es[:], func=AF.Exp,
                 inc=ae)
            P.wait("vector", ae, ae.n)
            P.op("vector", "tensor_mul", out=es[:], in0=es[:], in1=msk[:],
                 inc=ve)
            P.wait("vector", ve, ve.n)
            P.op("vector", "reduce_sum", out=ts_[:, 0:4], in_=g4(es[:]),
                 axis=AX.X, inc=ve)
            P.wait("vector", ve, ve.n)
            P.op("vector", "reciprocal", out=ts_[:, 4:8], in_=ts_[:, 0:4], inc=ve)
            P.wait("vector", ve, ve.n)
            P.op("vector", "tensor_tensor", out=g4(wt[:]), in0=g4(es[:]),
                 in1=ts_[:, 4:8].rearrange("p (g o) -> p g o", o=1)
                 .broadcast_to([128, 4, NE]), op=ALU.mult, inc=ve)
            w_done = ve.n
            P.wait("tensor", ve, w_done)
            P.wait("tensor", ve, wfm_ps_free)
            for t in range(4):
                P.op("tensor", "transpose", pb[0:NE, t * 128:(t + 1) * 128],
                     wt[:, t * NE:(t + 1) * NE], identf[:],
                     inc=(pe if t == 3 else None))
            wfm_pe_at[gb] = pe.n
            P.wait("vector", pe, pe.n)
            for t in range(4):
                tt = grp * 4 + t
                P.op("vector", "tensor_copy",
                     out=w_fm[:, tt * 128:(tt + 1) * 128],
                     in_=pb[0:NE, t * 128:(t + 1) * 128],
                     inc=(ve if t == 3 else None))
            wfm_ps_free = ve.n

        c_s1(0)
        for grp in range(8):
            if grp + 1 < 8:
                c_s1(grp + 1)
            else:
                c_evac(*c_pend[0], pe.n)
                c_pend[0] = None
            c_s2(grp)
        phaseC_done = ve.n
        if stop_after == "C":
            finish([("d_xn2", xn2), ("d_wfm", w_fm), ("d_y", yfm)])
            return nc

        sPC.close()
        sCt.close()
        sBC.close()
        sDt = ctx.enter_context(ExitStack())
        hall = sb(sDt, "hall", [128, NE * 512], BF16)
        hs_fm = sb(sDt, "hs_fm", [128, 4 * 512], BF16)
        sgT = sb(sDt, "sgT", [128, 8 * SH], BF16)
        suT = sb(sDt, "suT", [128, 8 * SH], BF16)
        sdT = sb(sDt, "sdT", [128, 4 * C], BF16)
        gu_ring = sb(sDt, "gu_ring", [128, 4 * C], BF16)
        slab = sb(sDt, "slab", [128, 2 * NE * 256], BF16)
        x2r = sb(sDt, "x2r", [128, 4 * C], F32)
        sg_sb2 = sb(sDt, "sg_sb2", [128, 2 * 512], BF16)
        tmul = sb(sDt, "tmul", [128, 2 * 512], BF16)
        onehot = sb(sDt, "onehot", [NE, NE * 128], BF16)
        sPab.close()
        sPD = ctx.enter_context(ExitStack())
        ps_g = [psm(sPD, "ps_g0", [128, 512], F32),
                psm(sPD, "ps_g1", [128, 512], F32)]
        ps_u = [psm(sPD, "ps_u0", [128, 512], F32),
                psm(sPD, "ps_u1", [128, 512], F32)]
        ps_w = psm(sPD, "ps_w", [128, 512], F32)
        ps_m0 = psm(sPD, "ps_m0", [128, 256], F32)
        ps_m1 = psm(sPD, "ps_m1", [128, 256], F32)

        P.wait("sync", sc, ob_all_done)
        P.wait("sync", ve, phaseC_done)
        P.wait("sync", gp, gp.n)
        P.wait("scalar", ve, phaseC_done)
        P.dma("sync", onehot[:], onehot_h, sy)
        P.dma("sync", sgT[:], sg_sc, sy)
        P.dma("sync", suT[:], su_sc, sy)
        P.dma("sync", sdT[:], sd_sc, sy)
        shared_ld = sy.n

        gu_free = [0, 0, 0, 0]
        slab_free = [0, 0]
        x2_free = [0, 0, 0, 0]
        out_free = [0, 0, 0, 0]
        psm_free = [0, 0]
        psg_free = [0, 0]
        psu_free = [0, 0]
        psw_free = 0
        sg2_free = [0, 0]
        hall_free = 0
        def gu_load(e):
            gs, us = e % 2, 2 + e % 2
            P.wait("gpsimd", pe, gu_free[gs])
            P.dma("gpsimd", gu_ring[:, gs * C:(gs + 1) * C], gate_sc[e], gp)
            P.wait("gpsimd", pe, gu_free[us])
            P.dma("gpsimd", gu_ring[:, us * C:(us + 1) * C], up_sc[e], gp)
            guw_at[e] = gp.n

        guw_at = {}
        P.wait("gpsimd", sc, ob_all_done)
        P.wait("gpsimd", ve, phaseC_done)
        gu_load(0)
        gu_load(1)
        for ck in range(8):
            for e in range(NE):
                gs, us = e % 2, 2 + e % 2
                P.wait("tensor", gp, guw_at[e])
                if ck == 0 and e == 0:
                    P.wait("tensor", ve, phaseC_done)
                    P.wait("tensor", sy, shared_ld)
                pg, pu = ps_g[e % 2], ps_u[e % 2]
                P.wait("tensor", ae, psg_free[e % 2])
                for ch in range(8):
                    P.op("tensor", "matmul", pg[:],
                         gu_ring[:, gs * C + ch * 128: gs * C + ch * 128 + 128],
                         xn2[:, ch * T + ck * 512: ch * T + ck * 512 + 512],
                         start=(ch == 0), stop=(ch == 7),
                         inc=(pe if ch == 7 else None))
                g_mm = pe.n
                P.wait("tensor", ve, psu_free[e % 2])
                for ch in range(8):
                    P.op("tensor", "matmul", pu[:],
                         gu_ring[:, us * C + ch * 128: us * C + ch * 128 + 128],
                         xn2[:, ch * T + ck * 512: ch * T + ck * 512 + 512],
                         start=(ch == 0), stop=(ch == 7),
                         inc=(pe if ch == 7 else None))
                u_mm = pe.n
                gu_free[gs] = g_mm
                gu_free[us] = u_mm
                if e + 2 < NE:
                    gu_load(e + 2)
                elif ck + 1 < 8:
                    gu_load((e + 2) % NE)
                P.wait("tensor", ve, psw_free)
                P.op("tensor", "matmul", ps_w[:],
                     onehot[:, e * 128:(e + 1) * 128],
                     w_fm[:, ck * 512:(ck + 1) * 512],
                     start=True, stop=True, inc=pe)
                w_mm = pe.n
                es = e % 2
                P.wait("scalar", pe, g_mm)
                P.wait("scalar", ve, sg2_free[es])
                P.op("scalar", "activation", out=sg_sb2[:, es * 512:(es + 1) * 512],
                     in_=pg[:], func=AF.Silu, inc=ae)
                psg_free[e % 2] = ae.n
                P.wait("vector", ae, ae.n)
                P.wait("vector", pe, u_mm)
                P.op("vector", "tensor_tensor", out=tmul[:, es * 512:(es + 1) * 512],
                     in0=pu[:], in1=sg_sb2[:, es * 512:(es + 1) * 512],
                     op=ALU.mult, inc=ve)
                psu_free[e % 2] = ve.n
                sg2_free[es] = ve.n
                P.wait("vector", ve, ve.n)
                P.wait("vector", pe, w_mm)
                P.wait("vector", pe, hall_free)
                P.op("vector", "tensor_tensor", out=hall[:, e * 512:(e + 1) * 512],
                     in0=ps_w[:], in1=tmul[:, es * 512:(es + 1) * 512],
                     op=ALU.mult, inc=ve)
                psw_free = ve.n
            P.wait("tensor", sy, shared_ld)
            for jmt in range(4):
                pg, pu = ps_g[jmt % 2], ps_u[jmt % 2]
                P.wait("tensor", ae, psg_free[jmt % 2])
                for ch in range(8):
                    P.op("tensor", "matmul", pg[:],
                         sgT[:, ch * SH + jmt * 128: ch * SH + jmt * 128 + 128],
                         xn2[:, ch * T + ck * 512: ch * T + ck * 512 + 512],
                         start=(ch == 0), stop=(ch == 7),
                         inc=(pe if ch == 7 else None))
                g_mm = pe.n
                P.wait("tensor", ve, psu_free[jmt % 2])
                for ch in range(8):
                    P.op("tensor", "matmul", pu[:],
                         suT[:, ch * SH + jmt * 128: ch * SH + jmt * 128 + 128],
                         xn2[:, ch * T + ck * 512: ch * T + ck * 512 + 512],
                         start=(ch == 0), stop=(ch == 7),
                         inc=(pe if ch == 7 else None))
                u_mm = pe.n
                es = jmt % 2
                P.wait("scalar", pe, g_mm)
                P.wait("scalar", ve, sg2_free[es])
                P.op("scalar", "activation", out=sg_sb2[:, es * 512:(es + 1) * 512],
                     in_=pg[:], func=AF.Silu, inc=ae)
                psg_free[jmt % 2] = ae.n
                P.wait("vector", ae, ae.n)
                P.wait("vector", pe, u_mm)
                P.wait("vector", pe, hall_free)
                P.op("vector", "tensor_tensor",
                     out=hs_fm[:, jmt * 512:(jmt + 1) * 512],
                     in0=pu[:], in1=sg_sb2[:, es * 512:(es + 1) * 512],
                     op=ALU.mult, inc=ve)
                psu_free[jmt % 2] = ve.n
                sg2_free[es] = ve.n
            hall_done = ve.n
            for s in range(4):
                P.wait("sync", ve, x2_free[s])
                P.wait("sync", st, out_free[s])
                P.dma("sync", x2r[:, s * C:(s + 1) * C],
                      x2_sc[(ck * 4 + s) * 128:(ck * 4 + s) * 128 + 128, :], sy)
            x2_ld = sy.n
            for dc in range(4):
                sl = dc % 2
                P.wait("sync", pe, slab_free[sl])
                P.dma("sync", slab[:, sl * NE * 256:(sl + 1) * NE * 256],
                      down_sc[dc], sy)
                slab_ld = sy.n
                for s in range(4):
                    pmx = (dc * 4 + s) % 2
                    pm = ps_m0 if pmx == 0 else ps_m1
                    P.wait("tensor", sy, slab_ld)
                    P.wait("tensor", ve, hall_done)
                    P.wait("tensor", ve, psm_free[pmx])
                    for e in range(NE):
                        P.op("tensor", "matmul", pm[:],
                             hall[:, e * 512 + s * 128: e * 512 + s * 128 + 128],
                             slab[:, sl * NE * 256 + e * 256:
                                  sl * NE * 256 + e * 256 + 256],
                             start=(e == 0), stop=False)
                    for jc in range(4):
                        P.op("tensor", "matmul", pm[:],
                             hs_fm[:, jc * 512 + s * 128: jc * 512 + s * 128 + 128],
                             sdT[:, jc * C + dc * 256: jc * C + dc * 256 + 256],
                             start=False, stop=(jc == 3),
                             inc=(pe if jc == 3 else None))
                    m_mm = pe.n
                    if s == 3:
                        slab_free[sl] = m_mm
                    P.wait("vector", pe, m_mm)
                    P.wait("vector", sy, x2_ld)
                    P.op("vector", "tensor_tensor",
                         out=x2r[:, s * C + dc * 256: s * C + dc * 256 + 256],
                         in0=pm[:],
                         in1=x2r[:, s * C + dc * 256: s * C + dc * 256 + 256],
                         op=ALU.add, inc=ve)
                    psm_free[pmx] = ve.n
                    if dc == 3:
                        P.wait("sync", ve, ve.n)
                        P.dma("sync",
                              out_h[(ck * 4 + s) * 128:(ck * 4 + s) * 128 + 128, :],
                              x2r[:, s * C:(s + 1) * C], st)
                        out_free[s] = st.n
                        x2_free[s] = ve.n
            hall_free = pe.n

        if debug:
            finish([("d_xn2", xn2), ("d_hall", hall)])
            return nc
        finish([])

    return nc


def host_consts():
    ki = np.arange(128)[:, None]
    col = np.arange(128)[None, :]
    masks = np.where(col >= ki, 0.0, -1e30).astype(np.float32)
    onehot = np.zeros((NE, NE * 128), dtype=ml_dtypes.bfloat16)
    for e in range(NE):
        onehot[e, e * 128:(e + 1) * 128] = 1.0
    return dict(
        identb=np.eye(128, dtype=ml_dtypes.bfloat16),
        identf=np.eye(128, dtype=np.float32),
        masks=masks,
        onehot=onehot,
        onesb=np.ones((1, 64), dtype=ml_dtypes.bfloat16),
        onesf=np.ones((1, 128), dtype=np.float32),
        negonesf=-np.ones((1, 128), dtype=np.float32),
    )


def make_in_maps(inputs, n_cores=8):
    x = np.asarray(inputs["L0_attn_x"], dtype=np.float32)
    f = np.float32
    wmap = dict(
        wdq=np.ascontiguousarray(inputs["L0_attn_W_dq"], dtype=f),
        wuq=np.ascontiguousarray(inputs["L0_attn_W_uq"], dtype=f),
        wdkv=np.ascontiguousarray(inputs["L0_attn_W_dkv"], dtype=f),
        wuk=np.ascontiguousarray(inputs["L0_attn_W_uk"], dtype=f),
        wuv=np.ascontiguousarray(inputs["L0_attn_W_uv"], dtype=f),
        wo=np.ascontiguousarray(inputs["L0_attn_W_o"], dtype=f),
        rw=np.ascontiguousarray(inputs["L0_ffn_router_w"], dtype=f),
        rb=np.ascontiguousarray(
            np.asarray(inputs["L0_ffn_router_bias"], dtype=f).reshape(1, NE)),
        gw=np.ascontiguousarray(inputs["L0_ffn_gate_w"], dtype=f),
        uw=np.ascontiguousarray(inputs["L0_ffn_up_w"], dtype=f),
        dw=np.ascontiguousarray(inputs["L0_ffn_down_w"], dtype=f),
        sgw=np.ascontiguousarray(inputs["L0_ffn_sgate_w"], dtype=f),
        suw=np.ascontiguousarray(inputs["L0_ffn_sup_w"], dtype=f),
        sdw=np.ascontiguousarray(inputs["L0_ffn_sdown_w"], dtype=f),
        **host_consts(),
    )
    return [dict(x=np.ascontiguousarray(x[i]), **wmap) for i in range(n_cores)]


def kernel(**inputs):
    in_maps = make_in_maps(inputs, 8)
    nc = build_program(debug=False)
    res = run_bass_kernel_spmd(nc, in_maps, core_ids=list(range(8)))
    return np.stack([np.asarray(res.results[i]["out"]) for i in range(8)], axis=0)


# version 34
# speedup vs baseline: 1.0001x; 1.0001x over previous
import numpy as np
import ml_dtypes
from contextlib import ExitStack

import concourse.bass as bass
import concourse.mybir as mybir
from concourse.bass_utils import run_bass_kernel_spmd

F32 = mybir.dt.float32
BF16 = mybir.dt.bfloat16
AF = mybir.ActivationFunctionType
ALU = mybir.AluOpType
AX = mybir.AxisListType

T = 4096
C = 1024
NH = 16
KVL = 256
NE = 32
SH = 512
NQG = 8
CVW = 65

ENGINES = ("sync", "gpsimd", "tensor", "vector", "scalar")


class Sem:
    def __init__(self, nc, ctx, name):
        self.h = ctx.enter_context(nc.semaphore(name))
        self.n = 0


class Prog:
    def __init__(self):
        self.q = {e: [] for e in ENGINES}
        self.last_wait = {}

    def wait(self, eng, sem, val):
        if val <= 0:
            return
        key = (eng, id(sem))
        if self.last_wait.get(key, -1) >= val:
            return
        self.last_wait[key] = val
        self.q[eng].append(lambda E, s=sem.h, v=val: E.wait_ge(s, v))

    def op(self, eng, name, *args, inc=None, inc_k=1, **kwargs):
        if inc is not None:
            inc.n += inc_k
            hh, k = inc.h, inc_k
        else:
            hh, k = None, 0

        def f(E, nm=name, a=args, kw=kwargs, h=hh, kk=k):
            ins = getattr(E, nm)(*a, **kw)
            if h is not None:
                ins.then_inc(h, kk)
        self.q[eng].append(f)

    def dma(self, eng, out, in_, sem):
        self.op(eng, "dma_start", inc=sem, inc_k=16, out=out, in_=in_)

    def replay(self, eng, E):
        for f in self.q[eng]:
            f(E)


def build_program(debug=False, stop_after=None):
    nc = bass.Bass("TRN2", target_bir_lowering=False, debug=False)

    x_h = nc.dram_tensor("x", [T, C], F32, kind="ExternalInput").ap()
    wdq_h = nc.dram_tensor("wdq", [KVL, C], F32, kind="ExternalInput").ap()
    wuq_h = nc.dram_tensor("wuq", [C, KVL], F32, kind="ExternalInput").ap()
    wdkv_h = nc.dram_tensor("wdkv", [KVL, C], F32, kind="ExternalInput").ap()
    wuk_h = nc.dram_tensor("wuk", [C, KVL], F32, kind="ExternalInput").ap()
    wuv_h = nc.dram_tensor("wuv", [C, KVL], F32, kind="ExternalInput").ap()
    wo_h = nc.dram_tensor("wo", [C, C], F32, kind="ExternalInput").ap()
    rw_h = nc.dram_tensor("rw", [NE, C], F32, kind="ExternalInput").ap()
    rb_h = nc.dram_tensor("rb", [1, NE], F32, kind="ExternalInput").ap()
    gw_h = nc.dram_tensor("gw", [NE, 128, C], F32, kind="ExternalInput").ap()
    uw_h = nc.dram_tensor("uw", [NE, 128, C], F32, kind="ExternalInput").ap()
    dw_h = nc.dram_tensor("dw", [NE, C, 128], F32, kind="ExternalInput").ap()
    sgw_h = nc.dram_tensor("sgw", [SH, C], F32, kind="ExternalInput").ap()
    suw_h = nc.dram_tensor("suw", [SH, C], F32, kind="ExternalInput").ap()
    sdw_h = nc.dram_tensor("sdw", [C, SH], F32, kind="ExternalInput").ap()
    identb_h = nc.dram_tensor("identb", [128, 128], BF16, kind="ExternalInput").ap()
    identf_h = nc.dram_tensor("identf", [128, 128], F32, kind="ExternalInput").ap()
    masks_h = nc.dram_tensor("masks", [128, 128], F32, kind="ExternalInput").ap()
    onehot_h = nc.dram_tensor("onehot", [NE, NE * 128], BF16, kind="ExternalInput").ap()
    onesb_h = nc.dram_tensor("onesb", [1, 64], BF16, kind="ExternalInput").ap()
    onesf_h = nc.dram_tensor("onesf", [1, 128], F32, kind="ExternalInput").ap()
    negonesf_h = nc.dram_tensor("negonesf", [1, 128], F32, kind="ExternalInput").ap()

    out_h = nc.dram_tensor("out", [T, C], F32, kind="ExternalOutput").ap()

    x2_sc = nc.dram_tensor("x2_sc", [T, C], F32,
                           kind="ExternalOutput" if debug else "Internal").ap()
    gate_sc = nc.dram_tensor("gate_sc", [NE, 128, C], BF16, kind="Internal").ap()
    up_sc = nc.dram_tensor("up_sc", [NE, 128, C], BF16, kind="Internal").ap()
    down_sc = nc.dram_tensor("down_sc", [4, 128, NE * 256], BF16, kind="Internal").ap()
    sg_sc = nc.dram_tensor("sg_sc", [128, 8 * SH], BF16, kind="Internal").ap()
    su_sc = nc.dram_tensor("su_sc", [128, 8 * SH], BF16, kind="Internal").ap()
    sd_sc = nc.dram_tensor("sd_sc", [128, 4 * C], BF16, kind="Internal").ap()

    dbg = {}
    if debug:
        for nm, shp, dt in [("d_ckv", [128, 2 * T], BF16), ("d_qlow", [128, 2 * T], BF16),
                            ("d_cv", [128, 32 * NH * CVW], BF16), ("d_y", [128, 8 * T], BF16),
                            ("d_xn2", [128, 8 * T], BF16), ("d_wfm", [NE, T], BF16),
                            ("d_hall", [128, NE * 512], BF16), ("d_negc", [128, NH], F32),
                            ("d_bht", [128, NH * 2 * KVL], BF16),
                            ("d_t1", [128, 2 * KVL], BF16), ("d_m", [128, 8 * KVL], BF16),
                            ("d_v", [128, 2 * C], BF16), ("d_woT", [128, 8 * C], BF16),
                            ("d_wdkvT", [128, 8 * KVL], BF16),
                            ("d_xnfm", [128, 2 * 4096], BF16)]:
            dbg[nm] = nc.dram_tensor(nm, shp, dt, kind="ExternalOutput").ap()

    P = Prog()
    with ExitStack() as ctx:
        def finish(dumps):
            P.wait("sync", ve, ve.n)
            P.wait("sync", ae, ae.n)
            P.wait("sync", pe, pe.n)
            P.wait("sync", sc, sc.n)
            P.wait("sync", gp, gp.n)
            for nm, src_sb in dumps:
                P.dma("sync", dbg[nm], src_sb[:], st)
            P.wait("sync", st, st.n)
            block = ctx.enter_context(nc.Block())
            block.sync(lambda E: P.replay("sync", E))
            block.gpsimd(lambda E: P.replay("gpsimd", E))
            block.tensor(lambda E: P.replay("tensor", E))
            block.vector(lambda E: P.replay("vector", E))
            block.scalar(lambda E: P.replay("scalar", E))

        sy = Sem(nc, ctx, "sy")
        gp = Sem(nc, ctx, "gp")
        sc = Sem(nc, ctx, "sc")
        pe = Sem(nc, ctx, "pe")
        ve = Sem(nc, ctx, "ve")
        ae = Sem(nc, ctx, "ae")
        st = Sem(nc, ctx, "st")

        import os as _os
        _trace_sb = _os.environ.get("TRACE_SB")

        def sb(stk, name, shape, dtype, side=None):
            h = stk.enter_context(
                nc.sbuf_tensor("s_" + name, shape, dtype, side=side))
            if _trace_sb:
                import math
                per = math.prod(shape[1:]) * (2 if "bfloat" in str(dtype) else 4)
                print(f"SBUF + {name:12s} {per:7d}B/part side={side} "
                      f"remaining={nc.sbuf_bytes_remaining}")
            return h

        def psm(stk, name, shape, dtype=F32):
            return stk.enter_context(nc.psum_tensor(name, shape, dtype))

        identb = sb(ctx, "identb", [128, 128], BF16)
        identf = sb(ctx, "identf", [128, 128], F32)
        onesb = sb(ctx, "onesb", [1, 64], BF16)
        onesf = sb(ctx, "onesf", [1, 128], F32)
        negonesf = sb(ctx, "negonesf", [1, 128], F32)
        rbias = sb(ctx, "rbias", [1, NE], F32)
        rwT = sb(ctx, "rwT", [128, 8 * NE], F32)
        eps_col = sb(ctx, "eps_col", [128, 1], F32)
        sPab = ctx.enter_context(ExitStack())
        pa = psm(sPab, "pa", [128, 512], F32)
        pb = psm(sPab, "pb", [128, 512], F32)

        P.dma("sync", identb[:], identb_h, sy)
        sy_identb = sy.n
        P.op("vector", "memset", eps_col[:], 1e-5, inc=ve)
        for dst, src in [(identf, identf_h),
                         (onesb, onesb_h), (onesf, onesf_h),
                         (negonesf, negonesf_h), (rbias, rb_h)]:
            P.dma("sync", dst[:], src, sy)

        sAB = ctx.enter_context(ExitStack())
        ckv = sb(sAB, "ckv", [128, 2 * T], BF16)
        qlow = sb(sAB, "qlow", [128, 2 * T], BF16)
        bht = sb(sAB, "bht", [128, NH * 2 * KVL], BF16)
        cv = sb(sAB, "cv", [128, 32 * NH * CVW], BF16)

        negc_all = sb(sAB, "negc_all", [128, NH], F32)
        sA = ctx.enter_context(ExitStack())
        wdqT = sb(sA, "wdqT", [128, 8 * KVL], BF16)
        wdkvT = sb(sA, "wdkvT", [128, 8 * KVL], BF16)
        v_b = sb(sA, "v_b", [128, 2 * C], BF16)

        sXr = ctx.enter_context(ExitStack())
        x_ring = sb(sXr, "x_ring", [128, 6 * C], F32)
        xn_ring = sb(sXr, "xn_ring", [128, 4 * C], BF16)

        s0a = ctx.enter_context(ExitStack())
        wuq_b = sb(s0a, "wuq_b", [128, 8 * KVL], BF16)
        wuk_b = sb(s0a, "wuk_b", [128, 8 * KVL], BF16)
        wuv_b = sb(s0a, "wuv_b", [128, 8 * KVL], BF16)
        wdq_b = sb(s0a, "wdq_b", [128, 2 * C], BF16)
        wdkv_b = sb(s0a, "wdkv_b", [128, 2 * C], BF16)
        wo_b = sb(s0a, "wo_b", [128, 4 * C], BF16)
        woT = sb(s0a, "woT", [128, 4 * C], BF16)
        t1 = sb(s0a, "t1", [128, 2 * KVL], BF16)
        m_b = sb(s0a, "m_b", [128, 8 * KVL], BF16)
        rw_f = sb(s0a, "rw_f", [NE, C], F32)

        sPA = ctx.enter_context(ExitStack())
        pt0 = psm(sPA, "pt0", [128, 1024], BF16)
        pt1 = psm(sPA, "pt1", [128, 1024], BF16)
        pob0 = psm(sPA, "pob0", [128, 1024], BF16)
        pob1 = psm(sPA, "pob1", [128, 1024], BF16)
        pob2 = psm(sPA, "pob2", [128, 1024], BF16)
        pf = psm(sPA, "pf", [128, 512], F32)

        P.dma("gpsimd", wdq_b[:].rearrange("p (c k) -> p c k", c=2),
              wdq_h.rearrange("(c p) k -> p c k", p=128), gp)
        P.dma("gpsimd", wdkv_b[:].rearrange("p (c k) -> p c k", c=2),
              wdkv_h.rearrange("(c p) k -> p c k", p=128), gp)
        gp_dq = gp.n
        P.dma("gpsimd", wuq_b[:].rearrange("p (c k) -> p c k", c=8),
              wuq_h.rearrange("(c p) k -> p c k", p=128), gp)
        P.dma("gpsimd", wuk_b[:].rearrange("p (c k) -> p c k", c=8),
              wuk_h.rearrange("(c p) k -> p c k", p=128), gp)
        P.dma("gpsimd", wuv_b[:].rearrange("p (c k) -> p c k", c=8),
              wuv_h.rearrange("(c p) k -> p c k", p=128), gp)
        gp_w = gp.n
        P.dma("gpsimd", wo_b[:].rearrange("p (c k) -> p c k", c=4),
              wo_h[0:512, :].rearrange("(c p) k -> p c k", p=128), gp)
        gp_wo = [gp.n, None]
        P.dma("sync", rw_f[:], rw_h, sy)
        sy_c = sy.n

        tto_free = [0]

        def transpose_to(dst_sb, src_sb, nr, nf, pst):
            blocks = [(a, f) for a in range(nr) for f in range(nf)]
            for i0 in range(0, len(blocks), 8):
                grp = blocks[i0:i0 + 8]
                P.wait("tensor", ve, tto_free[0])
                for j, (a, f) in enumerate(grp):
                    P.op("tensor", "transpose", pst[:, j * 128:(j + 1) * 128],
                         src_sb[:, (a * nf + f) * 128:(a * nf + f) * 128 + 128],
                         identb[:], inc=(pe if j == len(grp) - 1 else None))
                P.wait("vector", pe, pe.n)
                for j, (a, f) in enumerate(grp):
                    P.op("vector", "tensor_copy",
                         out=dst_sb[:, (f * nr + a) * 128:(f * nr + a) * 128 + 128],
                         in_=pst[:, j * 128:(j + 1) * 128],
                         inc=(ve if j == len(grp) - 1 else None))
                tto_free[0] = ve.n

        P.wait("tensor", sy, sy_identb)
        P.wait("tensor", gp, gp_dq)
        transpose_to(wdqT, wdq_b, 2, 8, pt0)
        transpose_to(wdkvT, wdkv_b, 2, 8, pt0)

        P.wait("tensor", gp, gp_w)
        for mt in range(2):
            tgt = pa if mt == 0 else pb
            for ch in range(8):
                P.op("tensor", "matmul", tgt[:, 0:KVL],
                     wuq_b[:, ch * KVL + mt * 128: ch * KVL + mt * 128 + 128],
                     wuk_b[:, ch * KVL:(ch + 1) * KVL],
                     start=(ch == 0), stop=(ch == 7), inc=(pe if ch == 7 else None))
        P.wait("vector", pe, pe.n)
        P.op("vector", "tensor_copy", out=t1[:, 0:KVL], in_=pa[:, 0:KVL])
        P.op("vector", "tensor_copy", out=t1[:, KVL:2 * KVL], in_=pb[:, 0:KVL], inc=ve)
        P.wait("tensor", ve, ve.n)
        m_ps_free = [0, 0]
        for mt in range(8):
            tgt = pa if mt % 2 == 0 else pb
            P.wait("tensor", ve, m_ps_free[mt % 2])
            for ch in range(2):
                P.op("tensor", "matmul", tgt[:, 0:KVL],
                     wdq_b[:, ch * C + mt * 128: ch * C + mt * 128 + 128],
                     t1[:, ch * KVL:(ch + 1) * KVL],
                     start=(ch == 0), stop=(ch == 1), inc=(pe if ch == 1 else None))
            P.wait("vector", pe, pe.n)
            P.op("vector", "tensor_copy", out=m_b[:, mt * KVL:(mt + 1) * KVL],
                 in_=tgt[:, 0:KVL], inc=ve)
            m_ps_free[mt % 2] = ve.n
        P.wait("tensor", ve, ve.n)
        bht_ps_free = 0
        for h in range(NH):
            c2, off = h // 2, (h % 2) * 64
            P.wait("tensor", ve, bht_ps_free)
            for mt in range(2):
                tgt = pa if mt == 0 else pb
                P.op("tensor", "matmul", tgt[:, 0:KVL],
                     wuq_b[off:off + 64, c2 * KVL + mt * 128: c2 * KVL + mt * 128 + 128],
                     m_b[off:off + 64, c2 * KVL:(c2 + 1) * KVL],
                     start=True, stop=True, inc=(pe if mt == 1 else None))
            P.wait("vector", pe, pe.n)
            P.op("vector", "tensor_copy",
                 out=bht[:, (2 * h) * KVL:(2 * h + 1) * KVL], in_=pa[:, 0:KVL])
            P.op("vector", "tensor_copy",
                 out=bht[:, (2 * h + 1) * KVL:(2 * h + 2) * KVL], in_=pb[:, 0:KVL],
                 inc=ve)
            bht_ps_free = ve.n

        v_ps_free = 0
        for half in range(2):
            P.wait("tensor", gp, gp_wo[half])
            transpose_to(woT, wo_b, 4, 8, pt0)
            woT_done_h = ve.n
            if half == 0:
                P.wait("gpsimd", pe, pe.n)
                P.dma("gpsimd", wo_b[:].rearrange("p (c k) -> p c k", c=4),
                      wo_h[512:1024, :].rearrange("(c p) k -> p c k", p=128), gp)
                gp_wo[1] = gp.n
            P.wait("tensor", ve, woT_done_h)
            for mt in range(2):
                tgt = pa if mt == 0 else pb
                P.wait("tensor", ve, v_ps_free)
                for ch in range(8):
                    P.op("tensor", "matmul", tgt[:],
                         wuv_b[:, ch * KVL + mt * 128: ch * KVL + mt * 128 + 128],
                         woT[:, ch * 512: ch * 512 + 512],
                         start=(ch == 0), stop=(ch == 7),
                         inc=(pe if ch == 7 else None))
            P.wait("vector", pe, pe.n)
            P.op("vector", "tensor_copy",
                 out=v_b[:, half * 512: half * 512 + 512], in_=pa[:])
            P.op("vector", "tensor_copy",
                 out=v_b[:, C + half * 512: C + half * 512 + 512],
                 in_=pb[:], inc=ve)
            v_ps_free = ve.n
        P.wait("tensor", sy, sy_c)
        rwt_ps_free = v_ps_free
        for grp in range(2):
            P.wait("tensor", ve, rwt_ps_free)
            for j4 in range(4):
                j = grp * 4 + j4
                P.op("tensor", "transpose", pa[:, j4 * 32:(j4 + 1) * 32],
                     rw_f[:, j * 128:(j + 1) * 128], identf[0:32, 0:32],
                     inc=(pe if j4 == 3 else None))
            P.wait("vector", pe, pe.n)
            for j4 in range(4):
                j = grp * 4 + j4
                P.op("vector", "tensor_copy", out=rwT[:, j * NE:(j + 1) * NE],
                     in_=pa[:, j4 * 32:(j4 + 1) * 32],
                     inc=(ve if j4 == 3 else None))
            rwt_ps_free = ve.n
        pe_0a_done = pe.n
        pab_free_0a = ve.n
        proj_ps_free_init = ve.n
        s0a.close()

        sAt = ctx.enter_context(ExitStack())
        st6 = sb(sAt, "st6", [128, 96], F32)
        st2 = sb(sAt, "st2", [128, 16], F32)
        lnv = sb(sAt, "lnv", [128, 8], F32)
        rstd = sb(sAt, "rstd", [128, 8], F32)
        xnfm = sb(sAt, "xnfm", [128, 4096], BF16)
        gwf_ring = sb(sAt, "gwf_ring", [128, 4 * C], F32)
        gwb_ring = sb(sAt, "gwb_ring", [128, 2 * C], BF16)
        gt_ring = sb(sAt, "gt_ring", [128, 2 * C], BF16)
        sq_tmp = sb(sAt, "sq_tmp", [128, 2 * 512], BF16)
        ones_col = sb(sAt, "ones_col", [128, 1], BF16)
        nrm_tmp = sb(sAt, "nrm_tmp", [1, 16], F32)
        nq2 = sb(sAt, "nq2", [1, 4], F32)
        frx = sb(sAt, "frx", [1, 16], F32)
        frx2 = sb(sAt, "frx2", [1, 32], F32)

        ob_gw_free = [0, 0, 0, 0]
        ob_gt_free = [0, 0]
        ob_ps_free = [0, 0, 0]

        ob_gw_at = {}
        ob_gwb_free = [0, 0]

        ob_jobs = [(k, e) for k in ("g", "u", "d") for e in range(NE)]
        ob_jobs += [("sg", i) for i in range(4)] + [("su", i) for i in range(4)] \
            + [("sd", i) for i in range(4)]
        OB_NPAIRED = 2 * NE

        def ob_load_pair(j0):
            kind, e = ob_jobs[j0]
            slot = j0 % 4
            P.wait("gpsimd", ae, ob_gw_free[slot])
            P.wait("gpsimd", ae, ob_gw_free[slot + 1])
            dst = gwf_ring[:, slot * C: (slot + 2) * C]
            if kind == "g":
                P.dma("gpsimd", dst.rearrange("p (a k) -> p a k", a=2),
                      gw_h[e:e + 2].rearrange("a p k -> p a k"), gp)
            else:
                P.dma("gpsimd", dst.rearrange("p (a k) -> p a k", a=2),
                      uw_h[e:e + 2].rearrange("a p k -> p a k"), gp)
            ob_gw_at[j0] = gp.n
            ob_gw_at[j0 + 1] = gp.n

        def ob_load_single(jidx, src_h3, nr, nf):
            slot = jidx % 4
            ncols = nf * 128
            P.wait("gpsimd", ae, ob_gw_free[slot])
            P.dma("gpsimd",
                  gwf_ring[:, slot * C: slot * C + nr * ncols]
                  .rearrange("p (a k) -> p a k", a=nr),
                  src_h3, gp)
            ob_gw_at[jidx] = gp.n

        def ob_store_pair(jidx):
            kind, e = ob_jobs[jidx]
            if kind == "g":
                P.dma("scalar", gate_sc[e - 1:e + 1].rearrange("a p k -> p a k"),
                      gt_ring[:, 0:2 * C].rearrange("p (a k) -> p a k", a=2), sc)
            else:
                P.dma("scalar", up_sc[e - 1:e + 1].rearrange("a p k -> p a k"),
                      gt_ring[:, 0:2 * C].rearrange("p (a k) -> p a k", a=2), sc)
            ob_gt_free[0] = sc.n
            ob_gt_free[1] = sc.n

        def ob_compute(jidx, nr, nf, write_out_ap, gt_view):
            paired = jidx < OB_NPAIRED
            slot = jidx % 4
            gts = jidx % 2
            gb2 = jidx % 2
            P.wait("scalar", gp, ob_gw_at[jidx])
            P.wait("scalar", pe, ob_gwb_free[gb2])
            P.op("scalar", "activation", out=gwb_ring[:, gb2 * C:(gb2 + 1) * C],
                 in_=gwf_ring[:, slot * C:(slot + 1) * C], func=AF.Copy, inc=ae)
            ob_gw_free[slot] = ae.n
            P.wait("tensor", ae, ae.n)
            P.wait("tensor", sc, ob_gt_free[gts])
            P.wait("tensor", ve, ob_ps_free[jidx % 3])
            pst = (pob0, pob1, pob2)[jidx % 3]
            blocks = [(a, f) for a in range(nr) for f in range(nf)]
            for j, (a, f) in enumerate(blocks):
                P.op("tensor", "transpose", pst[:, j * 128:(j + 1) * 128],
                     gwb_ring[:, gb2 * C + (a * nf + f) * 128:
                              gb2 * C + (a * nf + f) * 128 + 128],
                     identb[:], inc=(pe if j == 7 else None))
            ob_gwb_free[gb2] = pe.n
            P.wait("vector", pe, pe.n)
            if nr == 1 or nf == 1:
                P.op("vector", "tensor_copy",
                     out=gt_ring[:, gts * C:(gts + 1) * C],
                     in_=pst[:, 0:1024], inc=ve)
            else:
                for j, (a, f) in enumerate(blocks):
                    P.op("vector", "tensor_copy",
                         out=gt_ring[:, gts * C + (f * nr + a) * 128:
                                     gts * C + (f * nr + a) * 128 + 128],
                         in_=pst[:, j * 128:(j + 1) * 128],
                         inc=(ve if j == 7 else None))
            ob_ps_free[jidx % 3] = ve.n
            P.wait("scalar", ve, ve.n)
            if paired:
                if gts == 1:
                    ob_store_pair(jidx)
            else:
                P.dma("scalar", write_out_ap,
                      gt_view(gt_ring[:, gts * C: gts * C + C]), sc)
                ob_gt_free[gts] = sc.n

        def ob_spec(job):
            kind, e = job
            if kind == "g":
                return (gw_h[e].rearrange("(a p) k -> p a k", p=128), 1, 8,
                        gate_sc[e], lambda v: v)
            if kind == "u":
                return (uw_h[e].rearrange("(a p) k -> p a k", p=128), 1, 8,
                        up_sc[e], lambda v: v)
            if kind == "d":
                return (dw_h[e].rearrange("(a p) k -> p a k", p=128), 8, 1,
                        down_sc[:, :, e * 256:(e + 1) * 256]
                        .rearrange("dc p k -> p dc k"),
                        lambda v: v.rearrange("p (a k) -> p a k", a=4))
            if kind == "sg":
                return (sgw_h[e * 128:(e + 1) * 128, :]
                        .rearrange("(a p) k -> p a k", p=128), 1, 8,
                        sg_sc.rearrange("p (cc q k) -> p cc q k", cc=8, q=4)
                        [:, :, e, :],
                        lambda v: v.rearrange("p (a k) -> p a k", a=8))
            if kind == "su":
                return (suw_h[e * 128:(e + 1) * 128, :]
                        .rearrange("(a p) k -> p a k", p=128), 1, 8,
                        su_sc.rearrange("p (cc q k) -> p cc q k", cc=8, q=4)
                        [:, :, e, :],
                        lambda v: v.rearrange("p (a k) -> p a k", a=8))
            if kind == "sd":
                return (sdw_h[e * 256:(e + 1) * 256, :]
                        .rearrange("(a p) k -> p a k", p=128), 2, 4,
                        sd_sc.rearrange("p (jc dc k) -> p jc dc k", jc=4, dc=8)
                        [:, :, e * 2:e * 2 + 2, :],
                        lambda v: v.rearrange("p (a b k) -> p a b k", a=4, b=2))

        ob_next_load = [0]
        ob_next_comp = [0]

        def ob_pump(n):
            for _ in range(n):
                if ob_next_comp[0] >= len(ob_jobs):
                    return
                while (ob_next_load[0] < len(ob_jobs)
                       and ob_next_load[0] < ob_next_comp[0] + 3):
                    jl = ob_next_load[0]
                    if jl < OB_NPAIRED:
                        ob_load_pair(jl)
                        ob_next_load[0] += 2
                    else:
                        s = ob_spec(ob_jobs[jl])
                        ob_load_single(jl, s[0], s[1], s[2])
                        ob_next_load[0] += 1
                jc_ = ob_next_comp[0]
                s = ob_spec(ob_jobs[jc_])
                ob_compute(jc_, s[1], s[2], s[3], s[4])
                ob_next_comp[0] += 1

        x_slot_free = [0] * 6
        xn_pe_done = [0, 0, 0, 0]
        xnfm_pe_free = [0, 0]
        proj_ps_free = [0, 0]
        ob_gw_free = [0, 0, 0, 0]
        ob_ps_free = [proj_ps_free_init] * 3
        P.wait("gpsimd", pe, pe_0a_done)
        P.wait("gpsimd", ve, proj_ps_free_init)
        rstd_at = {}
        stat_at = {}
        evac_at = [0, 0, 0, 0]

        loads_at = {}

        def a_loads(ck, ts):
            for t in ts:
                tt = ck * 4 + t
                xs = tt % 6
                P.wait("sync", ve, x_slot_free[xs])
                P.dma("sync", x_ring[:, xs * C:(xs + 1) * C],
                      x_h[tt * 128:(tt + 1) * 128, :], sy)
                loads_at[(ck, t)] = sy.n

        def a_s1a(ck):
            sc2 = ck % 2
            for t in range(2):
                xs = (ck * 4 + t) % 6
                P.wait("vector", sy, loads_at[(ck, t)])
                P.op("vector", "bn_stats", out=st6[:, sc2 * 48 + t * 12: sc2 * 48 + t * 12 + 6],
                     in_=x_ring[:, xs * C: xs * C + 512])
                P.op("vector", "bn_stats",
                     out=st6[:, sc2 * 48 + t * 12 + 6: sc2 * 48 + t * 12 + 12],
                     in_=x_ring[:, xs * C + 512: xs * C + 1024])

        def a_s1b(ck):
            sc2 = ck % 2
            for t in range(2, 4):
                xs = (ck * 4 + t) % 6
                P.wait("vector", sy, loads_at[(ck, t)])
                P.op("vector", "bn_stats", out=st6[:, sc2 * 48 + t * 12: sc2 * 48 + t * 12 + 6],
                     in_=x_ring[:, xs * C: xs * C + 512])
                P.op("vector", "bn_stats",
                     out=st6[:, sc2 * 48 + t * 12 + 6: sc2 * 48 + t * 12 + 12],
                     in_=x_ring[:, xs * C + 512: xs * C + 1024],
                     inc=(ve if t == 3 else None))
            P.wait("vector", ve, ve.n)
            for t in range(4):
                P.op("vector", "bn_aggr", out=st2[:, sc2 * 8 + t * 2: sc2 * 8 + t * 2 + 2],
                     in_=st6[:, sc2 * 48 + t * 12: sc2 * 48 + t * 12 + 12],
                     inc=(ve if t == 3 else None))
            P.wait("vector", ve, ve.n)
            P.op("vector", "tensor_scalar", out=lnv[:, sc2 * 4: sc2 * 4 + 4],
                 in0=st2[:, sc2 * 8 + 1: sc2 * 8 + 8: 2],
                 scalar1=1e-5, scalar2=None, op0=ALU.add, inc=ve)
            P.wait("scalar", ve, ve.n)
            P.op("scalar", "activation", out=rstd[:, sc2 * 4: sc2 * 4 + 4],
                 in_=lnv[:, sc2 * 4: sc2 * 4 + 4], func=AF.Ln, inc=ae)
            P.wait("scalar", ae, ae.n)
            P.op("scalar", "activation", out=rstd[:, sc2 * 4: sc2 * 4 + 4],
                 in_=rstd[:, sc2 * 4: sc2 * 4 + 4], func=AF.Exp, scale=-0.5, inc=ae)
            rstd_at[ck] = ae.n

        xn_at_g = {}

        def a_apply(ck, t):
            sc2 = ck % 2
            xs = (ck * 4 + t) % 6
            if t == 0:
                P.wait("vector", ae, rstd_at[ck])
            P.wait("vector", pe, xn_pe_done[t])
            P.op("vector", "tensor_scalar",
                 out=xn_ring[:, t * C:(t + 1) * C],
                 in0=x_ring[:, xs * C:(xs + 1) * C],
                 scalar1=st2[:, sc2 * 8 + t * 2: sc2 * 8 + t * 2 + 1],
                 scalar2=rstd[:, sc2 * 4 + t: sc2 * 4 + t + 1],
                 op0=ALU.subtract, op1=ALU.mult, inc=ve)
            x_slot_free[xs] = ve.n
            xn_at_g[(ck, t)] = ve.n

        def a_s2b(ck):
            cbuf = 0
            rate = 0 if ck < 2 else 3
            for t in range(4):
                ob_pump(rate)
                pst = pt0 if t % 2 == 0 else pt1
                P.wait("tensor", ve, xn_at_g.pop((ck, t)))
                if t >= 2:
                    P.wait("tensor", ve, evac_at[t - 2])
                for j in range(8):
                    P.op("tensor", "transpose", pst[:, j * 128:(j + 1) * 128],
                         xn_ring[:, t * C + j * 128: t * C + j * 128 + 128],
                         identb[:], inc=(pe if j == 7 else None))
                xn_pe_done[t] = pe.n
                P.wait("vector", pe, pe.n)
                P.wait("vector", pe, xnfm_pe_free[cbuf])
                P.op("vector", "tensor_copy",
                     out=xnfm[:, cbuf * 4096:(cbuf + 1) * 4096]
                     .rearrange("p (c q) -> p c q", c=8)[:, :, t * 128:(t + 1) * 128],
                     in_=pst[:].rearrange("p (c q) -> p c q", c=8),
                     inc=ve)
                evac_at[t] = ve.n
                if ck + 1 < 8:
                    a_apply(ck + 1, t)
            xnfm_ready = ve.n
            P.wait("tensor", ve, xnfm_ready)
            for (wT, dst) in ((wdkvT, ckv), (wdqT, qlow)):
                for mt in range(2):
                    ob_pump(0 if ck < 2 else 2)
                    P.wait("tensor", ve, proj_ps_free[0])
                    for ch in range(8):
                        P.op("tensor", "matmul", pf[:],
                             wT[:, ch * KVL + mt * 128: ch * KVL + mt * 128 + 128],
                             xnfm[:, cbuf * 4096 + ch * 512:
                                  cbuf * 4096 + ch * 512 + 512],
                             start=(ch == 0), stop=(ch == 7),
                             inc=(pe if ch == 7 else None))
                    P.wait("vector", pe, pe.n)
                    P.op("vector", "tensor_copy",
                         out=dst[:, mt * T + ck * 512: mt * T + ck * 512 + 512],
                         in_=pf[:], inc=ve)
                    proj_ps_free[0] = ve.n
            xnfm_pe_free[cbuf] = pe.n

        a_loads(0, range(4))
        a_loads(1, (0, 1))
        a_s1a(0)
        a_s1b(0)
        for t in range(4):
            a_apply(0, t)
        for ck in range(8):
            if ck + 1 < 8:
                a_s1a(ck + 1)
            if ck + 1 < 8:
                a_loads(ck + 1, (2, 3))
            if ck + 2 < 8:
                a_loads(ck + 2, (0, 1))
            if ck + 1 < 8:
                a_s1b(ck + 1)
            a_s2b(ck)
        ob_pump(len(ob_jobs))
        ob_all_done = sc.n

        P.op("vector", "memset",
             cv[:].rearrange("p (kt h o) -> p kt h o", kt=32, h=NH)[:, :, :, 64:65],
             1.0, inc=ve)
        cv_ps_free = [pab_free_0a, pab_free_0a]
        for kt in range(32):
            for half in range(2):
                tgt = pa if half == 0 else pb
                P.wait("tensor", ve, cv_ps_free[half])
                for ch in range(2):
                    P.op("tensor", "matmul", tgt[:],
                         ckv[:, ch * T + kt * 128: ch * T + kt * 128 + 128],
                         v_b[:, ch * C + half * 512: ch * C + half * 512 + 512],
                         start=(ch == 0), stop=(ch == 1),
                         inc=(pe if ch == 1 else None))
                P.wait("vector", pe, pe.n)
                base = kt * NH * CVW + half * 8 * CVW
                P.op("vector", "tensor_copy",
                     out=cv[:, base: base + 8 * CVW]
                     .rearrange("p (h z) -> p h z", h=8)[:, :, 0:64],
                     in_=tgt[:].rearrange("p (h d) -> p h d", h=8),
                     inc=ve)
                cv_ps_free[half] = ve.n
        P.op("vector", "memset", ones_col[:], 1.0, inc=ve)
        ones_done = ve.n
        nrm_ps_free = cv_ps_free[0]
        sq_free = [0, 0]
        nrm_idx = 0
        for which, srcbuf in ((0, qlow), (1, ckv)):
            for kk in range(8):
                par2 = kk % 2
                tgt = pa if par2 == 0 else pb
                for ch in range(2):
                    P.wait("scalar", pe, sq_free[ch])
                    P.op("scalar", "activation",
                         out=sq_tmp[:, ch * 512:(ch + 1) * 512],
                         in_=srcbuf[:, ch * T + kk * 512: ch * T + kk * 512 + 512],
                         func=AF.Square, inc=ae)
                sq_done = ae.n
                P.wait("tensor", ae, sq_done)
                P.wait("tensor", ve, nrm_ps_free)
                P.wait("tensor", ve, ones_done)
                for ch in range(2):
                    P.op("tensor", "matmul", tgt[0:1, 0:512],
                         ones_col[:], sq_tmp[:, ch * 512:(ch + 1) * 512],
                         start=(ch == 0), stop=(ch == 1),
                         inc=(pe if ch == 1 else None))
                sq_free[0] = pe.n
                sq_free[1] = pe.n
                P.wait("vector", pe, pe.n)
                P.op("vector", "reduce_max",
                     out=nrm_tmp[:, nrm_idx: nrm_idx + 1],
                     in_=tgt[0:1, 0:512], axis=AX.X, inc=ve)
                nrm_ps_free = ve.n
                nrm_idx += 1
            P.wait("vector", ve, ve.n)
            P.op("vector", "reduce_max", out=nq2[:, which:which + 1],
                 in_=nrm_tmp[:, 0:8] if which == 0 else nrm_tmp[:, 8:16],
                 axis=AX.X, inc=ve)
        nq2_done = ve.n
        sqb_free = [sq_free[0], sq_free[1]]
        frx_evac = [nrm_ps_free, nrm_ps_free]
        for j in range(NH):
            sl = j % 2
            tgt = pa if sl == 0 else pb
            P.wait("scalar", pe, sqb_free[sl])
            P.op("scalar", "activation", out=sq_tmp[:, sl * 512:(sl + 1) * 512],
                 in_=bht[:, j * 512:(j + 1) * 512], func=AF.Square, inc=ae)
            P.wait("tensor", ae, ae.n)
            P.wait("tensor", ve, frx_evac[sl])
            P.op("tensor", "matmul", tgt[0:1, 0:512], ones_col[:],
                 sq_tmp[:, sl * 512:(sl + 1) * 512], start=True, stop=True,
                 inc=pe)
            sqb_free[sl] = pe.n
            P.wait("vector", pe, pe.n)
            P.op("vector", "reduce_sum", out=frx[:, j:j + 1],
                 in_=tgt[0:1, 0:512], axis=AX.X, inc=ve)
            frx_evac[sl] = ve.n
        P.wait("vector", ve, nq2_done)
        P.wait("vector", ve, ve.n)
        P.op("vector", "tensor_tensor", out=nq2[:, 2:3], in0=nq2[:, 0:1],
             in1=nq2[:, 1:2], op=ALU.mult, inc=ve)
        P.wait("vector", ve, ve.n)
        P.op("vector", "tensor_tensor", out=frx2[:, 0:16], in0=frx[:, 0:16],
             in1=nq2[:, 2:3].broadcast_to([1, 16]), op=ALU.mult, inc=ve)
        P.wait("scalar", ve, ve.n)
        P.op("scalar", "activation", out=frx2[:, 16:32], in_=frx2[:, 0:16],
             func=AF.Ln, inc=ae)
        P.wait("scalar", ae, ae.n)
        P.op("scalar", "activation", out=frx2[:, 0:16], in_=frx2[:, 16:32],
             func=AF.Exp, scale=0.5, inc=ae)
        P.wait("vector", ae, ae.n)
        P.op("vector", "tensor_scalar", out=frx2[:, 16:32], in0=frx2[:, 0:16],
             scalar1=-0.12875, scalar2=-80.0, op0=ALU.mult, op1=ALU.max, inc=ve)
        P.wait("tensor", ve, ve.n)
        P.op("tensor", "matmul", pa[:, 0:16], onesf[0:1, :], frx2[:, 16:32],
             start=True, stop=True, inc=pe)
        P.wait("vector", pe, pe.n)
        P.op("vector", "tensor_copy", out=negc_all[:], in_=pa[:, 0:16], inc=ve)
        negc_all_done = ve.n

        cv_done = ve.n
        if stop_after == "A":
            finish([("d_ckv", ckv), ("d_qlow", qlow), ("d_cv", cv), ("d_bht", bht),
                    ("d_t1", t1), ("d_m", m_b), ("d_v", v_b), ("d_woT", woT),
                    ("d_wdkvT", wdkvT), ("d_xnfm", xnfm)])
            return nc
        sAt.close()
        sXr.close()
        sA.close()

        sBC = ctx.enter_context(ExitStack())
        yfm = sb(sBC, "yfm", [128, 8 * T], BF16, side="right")
        sBt = ctx.enter_context(ExitStack())
        P.wait("vector", sc, ob_all_done)
        P.wait("scalar", sc, ob_all_done)
        P.wait("vector", gp, gp.n)
        P.wait("scalar", gp, gp.n)
        qlat = sb(sBt, "qlat", [128, 2 * 1024], BF16)
        ptb = sb(sBt, "ptb", [128, 2 * 1024], BF16)
        masks = sb(sBt, "masks", [128, 128], F32)
        P.wait("sync", sc, ob_all_done)
        P.wait("sync", ve, negc_all_done)
        P.wait("sync", ve, cv_done)
        P.dma("sync", masks[:], masks_h, sy)
        masks_ld = sy.n
        P.wait("tensor", sy, masks_ld)
        srow = sb(sBt, "srow", [1, 512], F32)
        rsrow = sb(sBt, "rsrow", [1, 512], BF16)
        rsb_sb = sb(sBt, "rsb_sb", [64, 512], BF16)
        sPA.close()
        sPB = ctx.enter_context(ExitStack())
        ps_s0 = psm(sPB, "ps_s0", [128, 1024], F32)
        ps_s1 = psm(sPB, "ps_s1", [128, 1024], F32)
        ps_y0 = psm(sPB, "ps_y0", [128, 512], F32)
        ps_y1 = psm(sPB, "ps_y1", [128, 512], F32)

        sps_free = [0, 0]
        pt_free = [0, 0]
        pair_spans = {}
        mask_at = {}
        psy_free = [0, 0]
        ql_ps_free = [cv_done, cv_done]
        negc_at = {}
        ql_at = {}
        P.wait("tensor", ve, cv_done)
        P.wait("tensor", ve, negc_all_done)
        P.wait("scalar", ve, negc_all_done)
        P.wait("tensor", ae, ae.n)

        def b_prep(it2):
            qg2, h2 = it2 % NQG, it2 // NQG
            par = it2 % 2
            for mt in range(2):
                tgt = pa if mt == 0 else pb
                P.wait("tensor", ve, ql_ps_free[mt])
                for ch in range(2):
                    P.op("tensor", "matmul", tgt[:],
                         bht[:, (2 * h2 + ch) * KVL + mt * 128:
                             (2 * h2 + ch) * KVL + mt * 128 + 128],
                         qlow[:, ch * T + qg2 * 512: ch * T + qg2 * 512 + 512],
                         start=(ch == 0), stop=(ch == 1),
                         inc=(pe if ch == 1 else None))
            P.wait("vector", pe, pe.n)
            for mt in range(2):
                tgt = pa if mt == 0 else pb
                P.op("vector", "tensor_copy",
                     out=qlat[:, par * 1024 + mt * 512: par * 1024 + mt * 512 + 512],
                     in_=tgt[:], inc=ve)
            ql_evac = ve.n
            ql_at[it2] = ql_evac
            ql_ps_free[0] = ql_evac
            ql_ps_free[1] = ql_evac

        def b_smm(it2, p):
            qg2, h2 = it2 % NQG, it2 // NQG
            par = it2 % 2
            pp = p % 2
            sp = ps_s0 if pp == 0 else ps_s1
            P.wait("tensor", ae, sps_free[pp])
            off = 0
            spans = []
            for k2 in range(2):
                kb = 2 * p + k2
                d = kb - 4 * qg2
                qoff = 128 * d if d > 0 else 0
                w = 512 - qoff
                spans.append((off, qoff, w, d))
                for ch in range(2):
                    P.op("tensor", "matmul", sp[:, off:off + w],
                         ckv[:, ch * T + kb * 128: ch * T + kb * 128 + 128],
                         qlat[:, par * 1024 + ch * 512 + qoff:
                              par * 1024 + ch * 512 + qoff + w],
                         start=(ch == 0), stop=(ch == 1),
                         inc=(pe if (ch == 1 and k2 == 1) else None))
                off += w
            s_mm = pe.n
            pair_spans[(it2, p)] = spans
            m_at = None
            for (offk, qoff, w, d) in spans:
                if d >= 0:
                    P.wait("vector", pe, s_mm)
                    P.op("vector", "tensor_tensor",
                         out=sp[:, offk:offk + 128],
                         in0=sp[:, offk:offk + 128],
                         in1=masks[:], op=ALU.add, inc=ve)
                    m_at = ve.n
            P.wait("scalar", pe, s_mm)
            if m_at is not None:
                P.wait("scalar", ve, m_at)
            P.wait("scalar", pe, pt_free[pp])
            P.op("scalar", "activation",
                 out=ptb[:, pp * 1024: pp * 1024 + off],
                 in_=sp[:, 0:off], func=AF.Exp, scale=0.125,
                 bias=negc_all[:, h2:h2 + 1], inc=ae)
            sps_free[pp] = ae.n

        def b_ymm(it2, p, npair, yps):
            qg2, h2 = it2 % NQG, it2 // NQG
            pp = p % 2
            P.wait("tensor", ae, sps_free[pp])
            spans = pair_spans.pop((it2, p))
            for k2 in range(2):
                kb = 2 * p + k2
                off, qoff, w, d = spans[k2]
                P.op("tensor", "matmul", yps[0:CVW, qoff:qoff + w],
                     cv[:, kb * NH * CVW + h2 * CVW:
                        kb * NH * CVW + (h2 + 1) * CVW],
                     ptb[:, pp * 1024 + off: pp * 1024 + off + w],
                     start=(kb == 0), stop=(kb == npair * 2 - 1),
                     inc=(pe if k2 == 1 else None))
            pt_free[pp] = pe.n

        def b_tail_a(it2, yps, yb):
            P.wait("vector", pe, pe.n)
            P.op("vector", "tensor_copy", out=srow[:], in_=yps[64:65, :], inc=ve)
            P.wait("scalar", ve, ve.n)
            P.op("scalar", "activation", out=srow[:], in_=srow[:], func=AF.Ln,
                 inc=ae)
            P.wait("scalar", ae, ae.n)
            P.op("scalar", "activation", out=rsrow[:], in_=srow[:],
                 func=AF.Exp, scale=-1.0, inc=ae)
            return ae.n

        def b_tail_b(it2, yps, yb, rs_at):
            qg2, h2 = it2 % NQG, it2 // NQG
            P.wait("tensor", ae, rs_at)
            P.op("tensor", "matmul", yps[64:128, :], onesb[:], rsrow[:],
                 start=True, stop=True, inc=pe)
            P.wait("vector", pe, pe.n)
            P.op("vector", "tensor_copy", out=rsb_sb[:], in_=yps[64:128, :],
                 inc=ve)
            P.wait("vector", ve, ve.n)
            co = (h2 // 2) * T + qg2 * 512
            po = (h2 % 2) * 64
            P.op("vector", "tensor_tensor",
                 out=yfm[po:po + 64, co:co + 512],
                 in0=yps[0:64, :], in1=rsb_sb[:], op=ALU.mult, inc=ve)
            psy_free[yb] = ve.n

        NIT = NQG * NH
        b_prep(0)
        pend = None
        for it in range(NIT):
            qg = it % NQG
            npair = 2 * qg + 2
            yb = it % 2
            yps = ps_y0 if yb == 0 else ps_y1
            P.wait("tensor", ve, psy_free[yb])
            P.wait("tensor", ve, ql_at[it])
            rs_at = None
            for p in range(npair + 1):
                if p < npair:
                    b_smm(it, p)
                if p == 1:
                    if pend is not None:
                        rs_at = b_tail_a(*pend)
                    if it + 1 < NIT:
                        b_prep(it + 1)
                if p == max(1, npair - 1) and pend is not None:
                    b_tail_b(*pend, rs_at)
                    pend = None
                if p >= 1:
                    b_ymm(it, p - 1, npair, yps)
            pend = (it, yps, yb)
        rs_at = b_tail_a(*pend)
        b_tail_b(*pend, rs_at)
        attn_done = ve.n
        if stop_after == "B":
            finish([("d_y", yfm), ("d_ckv", ckv), ("d_cv", cv), ("d_negc", negc_all)])
            return nc

        sBt.close()
        sAB.close()
        sPB.close()

        sCD = ctx.enter_context(ExitStack())
        xn2 = sb(sCD, "xn2", [128, 8 * T], BF16)
        w_fm = sb(sCD, "w_fm", [NE, T], BF16)
        sCt = ctx.enter_context(ExitStack())
        xC = sb(sCt, "xC", [128, 4 * C], F32)
        x2_sb = sb(sCt, "x2_sb", [128, 8 * C], F32)
        xn2t = sb(sCt, "xn2t", [128, 2 * C], F32)
        xn2T = sb(sCt, "xn2T", [128, 2 * C], F32)
        c_st6 = sb(sCt, "c_st6", [128, 96], F32)
        c_st2 = sb(sCt, "c_st2", [128, 16], F32)
        c_lnv = sb(sCt, "c_lnv", [128, 8], F32)
        c_rstd = sb(sCt, "c_rstd", [128, 8], F32)
        lg_sb = sb(sCt, "lg_sb", [128, 4 * NE], F32)
        tk_t0 = sb(sCt, "tk_t0", [128, 4 * NE], F32)
        tk_mx = sb(sCt, "tk_mx", [128, 4 * 4], F32)
        tk_msk = sb(sCt, "tk_msk", [128, 4 * NE], F32)
        tk_es = sb(sCt, "tk_es", [128, 4 * NE], F32)
        tk_s = sb(sCt, "tk_s", [128, 8], F32)
        w_tok = sb(sCt, "w_tok", [128, 2 * 4 * NE], F32)

        sPC = ctx.enter_context(ExitStack())
        pc_t0 = psm(sPC, "pc_t0", [128, 1024], BF16)
        pc_t1 = psm(sPC, "pc_t1", [128, 1024], BF16)
        pc_f = psm(sPC, "pc_f", [128, 512], F32)
        pc_f2 = psm(sPC, "pc_f2", [128, 512], F32)

        xC_free = [0, 0, 0, 0]
        x2_free4 = [0] * 8
        x2_dma_done = [0] * 8
        yT_ps_free = [0, 0]
        xn2T_ps_free = [0, 0]
        c_rnd = [0]
        wfm_ps_free = 0
        wfm_pe_at = [0, 0]
        router_ps_free = 0
        xn2T_ps_free_pe = [0, 0]
        xn2T_router_free = [0, 0]
        c_rstd_at = {}
        P.wait("sync", ve, attn_done)
        g4 = lambda v: v.rearrange("p (g k) -> p g k", g=4)

        c_pend = [None]

        def c_trans(grp, t):
            tt = grp * 4 + t
            cb = t % 2
            pst = pc_t0 if cb == 0 else pc_t1
            P.wait("tensor", ae, yT_ps_free[cb])
            if tt == 0:
                P.wait("tensor", ve, attn_done)
            for j in range(8):
                P.op("tensor", "transpose", pst[:, j * 128:(j + 1) * 128],
                     yfm[:, j * T + tt * 128: j * T + tt * 128 + 128],
                     identb[:], inc=(pe if j == 7 else None))
            return pe.n

        def c_evac(grp, t, load_at, guard_mm):
            gb = grp % 2
            tt = grp * 4 + t
            t8 = gb * 4 + t
            cb = t % 2
            pst = pc_t0 if cb == 0 else pc_t1
            P.wait("scalar", pe, guard_mm)
            P.wait("scalar", gp, x2_dma_done[t8])
            P.wait("scalar", ve, x2_free4[t8])
            P.op("scalar", "activation", out=x2_sb[:, t8 * C:(t8 + 1) * C],
                 in_=pst[:], func=AF.Copy, inc=ae)
            yT_ps_free[cb] = ae.n
            P.wait("vector", ae, ae.n)
            P.wait("vector", sy, load_at)
            P.op("vector", "tensor_tensor", out=x2_sb[:, t8 * C:(t8 + 1) * C],
                 in0=x2_sb[:, t8 * C:(t8 + 1) * C],
                 in1=xC[:, t * C:(t + 1) * C], op=ALU.add, inc=ve)
            xC_free[t] = ve.n
            P.wait("gpsimd", ve, ve.n)
            P.dma("gpsimd", x2_sc[tt * 128:(tt + 1) * 128, :],
                  x2_sb[:, t8 * C:(t8 + 1) * C], gp)
            x2_dma_done[t8] = gp.n
            P.wait("vector", ve, ve.n)
            P.op("vector", "bn_stats",
                 out=c_st6[:, gb * 48 + t * 12: gb * 48 + t * 12 + 6],
                 in_=x2_sb[:, t8 * C: t8 * C + 512])
            P.op("vector", "bn_stats",
                 out=c_st6[:, gb * 48 + t * 12 + 6: gb * 48 + t * 12 + 12],
                 in_=x2_sb[:, t8 * C + 512: t8 * C + 1024],
                 inc=(ve if t == 3 else None))
            if t != 3:
                return
            P.wait("vector", ve, ve.n)
            for t2 in range(4):
                P.op("vector", "bn_aggr",
                     out=c_st2[:, gb * 8 + t2 * 2: gb * 8 + t2 * 2 + 2],
                     in_=c_st6[:, gb * 48 + t2 * 12: gb * 48 + t2 * 12 + 12],
                     inc=(ve if t2 == 3 else None))
            P.wait("vector", ve, ve.n)
            P.op("vector", "tensor_scalar", out=c_lnv[:, gb * 4: gb * 4 + 4],
                 in0=c_st2[:, gb * 8 + 1: gb * 8 + 8: 2],
                 scalar1=1e-5, scalar2=None, op0=ALU.add, inc=ve)
            P.wait("scalar", ve, ve.n)
            P.op("scalar", "activation", out=c_rstd[:, gb * 4: gb * 4 + 4],
                 in_=c_lnv[:, gb * 4: gb * 4 + 4], func=AF.Ln, inc=ae)
            P.wait("scalar", ae, ae.n)
            P.op("scalar", "activation", out=c_rstd[:, gb * 4: gb * 4 + 4],
                 in_=c_rstd[:, gb * 4: gb * 4 + 4], func=AF.Exp, scale=-0.5,
                 inc=ae)
            c_rstd_at[grp] = ae.n

        def c_s1(grp):
            g_mm = c_trans(grp, 0)
            if c_pend[0] is not None:
                c_evac(*c_pend[0], g_mm)
            loads = []
            for t in range(4):
                tt = grp * 4 + t
                P.wait("sync", ve, xC_free[t])
                P.dma("sync", xC[:, t * C:(t + 1) * C],
                      x_h[tt * 128:(tt + 1) * 128, :], sy)
                loads.append(sy.n)
            prev = (grp, 0, loads[0])
            for t in range(1, 4):
                g_mm = c_trans(grp, t)
                c_evac(*prev, g_mm)
                prev = (grp, t, loads[t])
            c_pend[0] = prev


        def c_heavy(grp, t, xn2_done):
            gb = grp % 2
            tt = grp * 4 + t
            cb = t % 2
            P.wait("tensor", ve, xn2_done)
            for gg in range(2):
                rb = c_rnd[0] % 2
                pstf = pc_f if rb == 0 else pc_f2
                P.wait("tensor", ve, xn2T_ps_free[rb])
                for j4 in range(4):
                    j = gg * 4 + j4
                    P.op("tensor", "transpose",
                         pstf[:, j4 * 128:(j4 + 1) * 128],
                         xn2t[:, cb * C + j * 128: cb * C + j * 128 + 128],
                         identf[:], inc=(pe if j4 == 3 else None))
                P.wait("vector", pe, pe.n)
                for j4 in range(4):
                    j = gg * 4 + j4
                    P.op("vector", "tensor_copy",
                         out=xn2T[:, cb * C + j * 128: cb * C + j * 128 + 128],
                         in_=pstf[:, j4 * 128:(j4 + 1) * 128],
                         inc=(ve if j4 == 3 else None))
                xn2T_ps_free[rb] = ve.n
                c_rnd[0] += 1
            xn2T_ps_free_pe[cb] = pe.n
            xn2T_done = ve.n
            P.wait("vector", ve, ve.n)
            P.op("vector", "tensor_copy",
                 out=xn2[:].rearrange("p (c q) -> p c q", c=8)
                 [:, :, tt * 128:(tt + 1) * 128],
                 in_=xn2T[:, cb * C:(cb + 1) * C]
                 .rearrange("p (c q) -> p c q", c=8),
                 inc=ve)
            P.wait("tensor", ve, xn2T_done)
            if t == 0:
                P.wait("tensor", ve, router_ps_free)
            for j in range(8):
                P.op("tensor", "matmul", pa[:, t * NE:(t + 1) * NE],
                     xn2T[:, cb * C + j * 128: cb * C + j * 128 + 128],
                     rwT[:, j * NE:(j + 1) * NE],
                     start=(j == 0), stop=False)
            P.op("tensor", "matmul", pa[:, t * NE:(t + 1) * NE], onesf[:],
                 rbias[:], start=False, stop=(t == 3),
                 inc=(pe if t == 3 else None))

        def c_s2(grp):
            nonlocal router_ps_free, wfm_ps_free
            gb = grp % 2
            P.wait("vector", ae, c_rstd_at[grp])
            xn_at = {}
            for t in range(4):
                cb = t % 2
                t8 = gb * 4 + t
                if t >= 2:
                    P.wait("vector", pe, xn2T_ps_free_pe[cb])
                    P.wait("vector", pe, xn2T_router_free[cb])
                P.op("vector", "tensor_scalar", out=xn2t[:, cb * C:(cb + 1) * C],
                     in0=x2_sb[:, t8 * C:(t8 + 1) * C],
                     scalar1=c_st2[:, gb * 8 + t * 2: gb * 8 + t * 2 + 1],
                     scalar2=c_rstd[:, gb * 4 + t: gb * 4 + t + 1],
                     op0=ALU.subtract, op1=ALU.mult, inc=ve)
                x2_free4[t8] = ve.n
                xn_at[t] = ve.n
                if t % 2 == 0:
                    continue
                for tt2 in (t - 1, t):
                    c_heavy(grp, tt2, xn_at[tt2])
            r_mm = pe.n
            xn2T_router_free[0] = r_mm
            xn2T_router_free[1] = r_mm
            lg = lg_sb
            t0 = tk_t0
            msk = tk_msk
            es = tk_es
            mx = tk_mx
            ts_ = tk_s
            wt = w_tok[:, gb * 4 * NE:(gb + 1) * 4 * NE]
            P.wait("vector", pe, r_mm)
            P.wait("vector", pe, wfm_pe_at[gb])
            P.op("vector", "tensor_copy", out=lg[:], in_=pa[:, 0:4 * NE], inc=ve)
            router_ps_free = ve.n
            P.wait("vector", ve, ve.n)
            P.op("vector", "reduce_max", out=mx[:, 0:4], in_=g4(lg[:]),
                 axis=AX.X)
            P.op("vector", "tensor_copy", out=t0[:], in_=lg[:], inc=ve)
            for k in range(3):
                P.wait("vector", ve, ve.n)
                P.op("vector", "tensor_tensor", out=g4(msk[:]), in0=g4(t0[:]),
                     in1=mx[:, k * 4:(k + 1) * 4]
                     .rearrange("p (g o) -> p g o", o=1).broadcast_to([128, 4, NE]),
                     op=ALU.is_ge, inc=ve)
                P.wait("vector", ve, ve.n)
                P.op("vector", "scalar_tensor_tensor", out=t0[:], in0=msk[:],
                     scalar=-1e30, in1=t0[:], op0=ALU.mult, op1=ALU.add, inc=ve)
                P.wait("vector", ve, ve.n)
                P.op("vector", "reduce_max", out=mx[:, (k + 1) * 4:(k + 2) * 4],
                     in_=g4(t0[:]), axis=AX.X, inc=ve)
            P.wait("vector", ve, ve.n)
            P.op("vector", "tensor_tensor", out=g4(msk[:]), in0=g4(lg[:]),
                 in1=mx[:, 12:16].rearrange("p (g o) -> p g o", o=1)
                 .broadcast_to([128, 4, NE]), op=ALU.is_ge, inc=ve)
            P.wait("vector", ve, ve.n)
            P.op("vector", "tensor_tensor", out=g4(es[:]), in0=g4(lg[:]),
                 in1=mx[:, 0:4].rearrange("p (g o) -> p g o", o=1)
                 .broadcast_to([128, 4, NE]), op=ALU.subtract, inc=ve)
            P.wait("scalar", ve, ve.n)
            P.op("scalar", "activation", out=es[:], in_=es[:], func=AF.Exp,
                 inc=ae)
            P.wait("vector", ae, ae.n)
            P.op("vector", "tensor_mul", out=es[:], in0=es[:], in1=msk[:],
                 inc=ve)
            P.wait("vector", ve, ve.n)
            P.op("vector", "reduce_sum", out=ts_[:, 0:4], in_=g4(es[:]),
                 axis=AX.X, inc=ve)
            P.wait("vector", ve, ve.n)
            P.op("vector", "reciprocal", out=ts_[:, 4:8], in_=ts_[:, 0:4], inc=ve)
            P.wait("vector", ve, ve.n)
            P.op("vector", "tensor_tensor", out=g4(wt[:]), in0=g4(es[:]),
                 in1=ts_[:, 4:8].rearrange("p (g o) -> p g o", o=1)
                 .broadcast_to([128, 4, NE]), op=ALU.mult, inc=ve)
            w_done = ve.n
            P.wait("tensor", ve, w_done)
            P.wait("tensor", ve, wfm_ps_free)
            for t in range(4):
                P.op("tensor", "transpose", pb[0:NE, t * 128:(t + 1) * 128],
                     wt[:, t * NE:(t + 1) * NE], identf[:],
                     inc=(pe if t == 3 else None))
            wfm_pe_at[gb] = pe.n
            P.wait("vector", pe, pe.n)
            for t in range(4):
                tt = grp * 4 + t
                P.op("vector", "tensor_copy",
                     out=w_fm[:, tt * 128:(tt + 1) * 128],
                     in_=pb[0:NE, t * 128:(t + 1) * 128],
                     inc=(ve if t == 3 else None))
            wfm_ps_free = ve.n

        c_s1(0)
        for grp in range(8):
            if grp + 1 < 8:
                c_s1(grp + 1)
            else:
                c_evac(*c_pend[0], pe.n)
                c_pend[0] = None
            c_s2(grp)
        phaseC_done = ve.n
        if stop_after == "C":
            finish([("d_xn2", xn2), ("d_wfm", w_fm), ("d_y", yfm)])
            return nc

        sPC.close()
        sCt.close()
        sBC.close()
        sDt = ctx.enter_context(ExitStack())
        hall = sb(sDt, "hall", [128, NE * 512], BF16)
        hs_fm = sb(sDt, "hs_fm", [128, 4 * 512], BF16)
        sgT = sb(sDt, "sgT", [128, 8 * SH], BF16)
        suT = sb(sDt, "suT", [128, 8 * SH], BF16)
        sdT = sb(sDt, "sdT", [128, 4 * C], BF16)
        gu_ring = sb(sDt, "gu_ring", [128, 4 * C], BF16)
        slab = sb(sDt, "slab", [128, 2 * NE * 256], BF16)
        x2r = sb(sDt, "x2r", [128, 4 * C], F32)
        sg_sb2 = sb(sDt, "sg_sb2", [128, 2 * 512], BF16)
        tmul = sb(sDt, "tmul", [128, 2 * 512], BF16)
        onehot = sb(sDt, "onehot", [NE, NE * 128], BF16)
        sPab.close()
        sPD = ctx.enter_context(ExitStack())
        ps_g = [psm(sPD, "ps_g0", [128, 512], F32),
                psm(sPD, "ps_g1", [128, 512], F32)]
        ps_u = [psm(sPD, "ps_u0", [128, 512], F32),
                psm(sPD, "ps_u1", [128, 512], F32)]
        ps_w = psm(sPD, "ps_w", [128, 512], F32)
        ps_m0 = psm(sPD, "ps_m0", [128, 256], F32)
        ps_m1 = psm(sPD, "ps_m1", [128, 256], F32)

        P.wait("sync", sc, ob_all_done)
        P.wait("sync", ve, phaseC_done)
        P.wait("sync", gp, gp.n)
        P.wait("scalar", ve, phaseC_done)
        P.dma("sync", onehot[:], onehot_h, sy)
        P.dma("sync", sgT[:], sg_sc, sy)
        P.dma("sync", suT[:], su_sc, sy)
        P.dma("sync", sdT[:], sd_sc, sy)
        shared_ld = sy.n

        gu_free = [0, 0, 0, 0]
        slab_free = [0, 0]
        x2_free = [0, 0, 0, 0]
        out_free = [0, 0, 0, 0]
        psm_free = [0, 0]
        psg_free = [0, 0]
        psu_free = [0, 0]
        psw_free = 0
        sg2_free = [0, 0]
        hall_free = 0
        def gu_load(e):
            gs, us = e % 2, 2 + e % 2
            P.wait("gpsimd", pe, gu_free[gs])
            P.dma("gpsimd", gu_ring[:, gs * C:(gs + 1) * C], gate_sc[e], gp)
            P.wait("gpsimd", pe, gu_free[us])
            P.dma("gpsimd", gu_ring[:, us * C:(us + 1) * C], up_sc[e], gp)
            guw_at[e] = gp.n

        guw_at = {}
        P.wait("gpsimd", sc, ob_all_done)
        P.wait("gpsimd", ve, phaseC_done)
        gu_load(0)
        gu_load(1)
        for ck in range(8):
            for e in range(NE):
                gs, us = e % 2, 2 + e % 2
                P.wait("tensor", gp, guw_at[e])
                if ck == 0 and e == 0:
                    P.wait("tensor", ve, phaseC_done)
                    P.wait("tensor", sy, shared_ld)
                pg, pu = ps_g[e % 2], ps_u[e % 2]
                P.wait("tensor", ae, psg_free[e % 2])
                for ch in range(8):
                    P.op("tensor", "matmul", pg[:],
                         gu_ring[:, gs * C + ch * 128: gs * C + ch * 128 + 128],
                         xn2[:, ch * T + ck * 512: ch * T + ck * 512 + 512],
                         start=(ch == 0), stop=(ch == 7),
                         inc=(pe if ch == 7 else None))
                g_mm = pe.n
                P.wait("tensor", ve, psu_free[e % 2])
                for ch in range(8):
                    P.op("tensor", "matmul", pu[:],
                         gu_ring[:, us * C + ch * 128: us * C + ch * 128 + 128],
                         xn2[:, ch * T + ck * 512: ch * T + ck * 512 + 512],
                         start=(ch == 0), stop=(ch == 7),
                         inc=(pe if ch == 7 else None))
                u_mm = pe.n
                gu_free[gs] = g_mm
                gu_free[us] = u_mm
                if e + 2 < NE:
                    gu_load(e + 2)
                elif ck + 1 < 8:
                    gu_load((e + 2) % NE)
                P.wait("tensor", ve, psw_free)
                P.op("tensor", "matmul", ps_w[:],
                     onehot[:, e * 128:(e + 1) * 128],
                     w_fm[:, ck * 512:(ck + 1) * 512],
                     start=True, stop=True, inc=pe)
                w_mm = pe.n
                es = e % 2
                P.wait("scalar", pe, g_mm)
                P.wait("scalar", ve, sg2_free[es])
                P.op("scalar", "activation", out=sg_sb2[:, es * 512:(es + 1) * 512],
                     in_=pg[:], func=AF.Silu, inc=ae)
                psg_free[e % 2] = ae.n
                P.wait("vector", ae, ae.n)
                P.wait("vector", pe, u_mm)
                P.op("vector", "tensor_tensor", out=tmul[:, es * 512:(es + 1) * 512],
                     in0=pu[:], in1=sg_sb2[:, es * 512:(es + 1) * 512],
                     op=ALU.mult, inc=ve)
                psu_free[e % 2] = ve.n
                sg2_free[es] = ve.n
                P.wait("vector", ve, ve.n)
                P.wait("vector", pe, w_mm)
                P.wait("vector", pe, hall_free)
                P.op("vector", "tensor_tensor", out=hall[:, e * 512:(e + 1) * 512],
                     in0=ps_w[:], in1=tmul[:, es * 512:(es + 1) * 512],
                     op=ALU.mult, inc=ve)
                psw_free = ve.n
            P.wait("tensor", sy, shared_ld)
            for jmt in range(4):
                pg, pu = ps_g[jmt % 2], ps_u[jmt % 2]
                P.wait("tensor", ae, psg_free[jmt % 2])
                for ch in range(8):
                    P.op("tensor", "matmul", pg[:],
                         sgT[:, ch * SH + jmt * 128: ch * SH + jmt * 128 + 128],
                         xn2[:, ch * T + ck * 512: ch * T + ck * 512 + 512],
                         start=(ch == 0), stop=(ch == 7),
                         inc=(pe if ch == 7 else None))
                g_mm = pe.n
                P.wait("tensor", ve, psu_free[jmt % 2])
                for ch in range(8):
                    P.op("tensor", "matmul", pu[:],
                         suT[:, ch * SH + jmt * 128: ch * SH + jmt * 128 + 128],
                         xn2[:, ch * T + ck * 512: ch * T + ck * 512 + 512],
                         start=(ch == 0), stop=(ch == 7),
                         inc=(pe if ch == 7 else None))
                u_mm = pe.n
                es = jmt % 2
                P.wait("scalar", pe, g_mm)
                P.wait("scalar", ve, sg2_free[es])
                P.op("scalar", "activation", out=sg_sb2[:, es * 512:(es + 1) * 512],
                     in_=pg[:], func=AF.Silu, inc=ae)
                psg_free[jmt % 2] = ae.n
                P.wait("vector", ae, ae.n)
                P.wait("vector", pe, u_mm)
                P.wait("vector", pe, hall_free)
                P.op("vector", "tensor_tensor",
                     out=hs_fm[:, jmt * 512:(jmt + 1) * 512],
                     in0=pu[:], in1=sg_sb2[:, es * 512:(es + 1) * 512],
                     op=ALU.mult, inc=ve)
                psu_free[jmt % 2] = ve.n
                sg2_free[es] = ve.n
            hall_done = ve.n
            for s in range(4):
                P.wait("sync", ve, x2_free[s])
                P.wait("sync", st, out_free[s])
                P.dma("sync", x2r[:, s * C:(s + 1) * C],
                      x2_sc[(ck * 4 + s) * 128:(ck * 4 + s) * 128 + 128, :], sy)
            x2_ld = sy.n
            for dc in range(4):
                sl = dc % 2
                P.wait("sync", pe, slab_free[sl])
                P.dma("sync", slab[:, sl * NE * 256:(sl + 1) * NE * 256],
                      down_sc[dc], sy)
                slab_ld = sy.n
                for s in range(4):
                    pmx = (dc * 4 + s) % 2
                    pm = ps_m0 if pmx == 0 else ps_m1
                    P.wait("tensor", sy, slab_ld)
                    P.wait("tensor", ve, hall_done)
                    P.wait("tensor", ve, psm_free[pmx])
                    for e in range(NE):
                        P.op("tensor", "matmul", pm[:],
                             hall[:, e * 512 + s * 128: e * 512 + s * 128 + 128],
                             slab[:, sl * NE * 256 + e * 256:
                                  sl * NE * 256 + e * 256 + 256],
                             start=(e == 0), stop=False)
                    for jc in range(4):
                        P.op("tensor", "matmul", pm[:],
                             hs_fm[:, jc * 512 + s * 128: jc * 512 + s * 128 + 128],
                             sdT[:, jc * C + dc * 256: jc * C + dc * 256 + 256],
                             start=False, stop=(jc == 3),
                             inc=(pe if jc == 3 else None))
                    m_mm = pe.n
                    if s == 3:
                        slab_free[sl] = m_mm
                    P.wait("vector", pe, m_mm)
                    P.wait("vector", sy, x2_ld)
                    P.op("vector", "tensor_tensor",
                         out=x2r[:, s * C + dc * 256: s * C + dc * 256 + 256],
                         in0=pm[:],
                         in1=x2r[:, s * C + dc * 256: s * C + dc * 256 + 256],
                         op=ALU.add, inc=ve)
                    psm_free[pmx] = ve.n
                    if dc == 3:
                        P.wait("sync", ve, ve.n)
                        P.dma("sync",
                              out_h[(ck * 4 + s) * 128:(ck * 4 + s) * 128 + 128, :],
                              x2r[:, s * C:(s + 1) * C], st)
                        out_free[s] = st.n
                        x2_free[s] = ve.n
            hall_free = pe.n

        if debug:
            finish([("d_xn2", xn2), ("d_hall", hall)])
            return nc
        finish([])

    return nc


def host_consts():
    ki = np.arange(128)[:, None]
    col = np.arange(128)[None, :]
    masks = np.where(col >= ki, 0.0, -1e30).astype(np.float32)
    onehot = np.zeros((NE, NE * 128), dtype=ml_dtypes.bfloat16)
    for e in range(NE):
        onehot[e, e * 128:(e + 1) * 128] = 1.0
    return dict(
        identb=np.eye(128, dtype=ml_dtypes.bfloat16),
        identf=np.eye(128, dtype=np.float32),
        masks=masks,
        onehot=onehot,
        onesb=np.ones((1, 64), dtype=ml_dtypes.bfloat16),
        onesf=np.ones((1, 128), dtype=np.float32),
        negonesf=-np.ones((1, 128), dtype=np.float32),
    )


def make_in_maps(inputs, n_cores=8):
    x = np.asarray(inputs["L0_attn_x"], dtype=np.float32)
    f = np.float32
    wmap = dict(
        wdq=np.ascontiguousarray(inputs["L0_attn_W_dq"], dtype=f),
        wuq=np.ascontiguousarray(inputs["L0_attn_W_uq"], dtype=f),
        wdkv=np.ascontiguousarray(inputs["L0_attn_W_dkv"], dtype=f),
        wuk=np.ascontiguousarray(inputs["L0_attn_W_uk"], dtype=f),
        wuv=np.ascontiguousarray(inputs["L0_attn_W_uv"], dtype=f),
        wo=np.ascontiguousarray(inputs["L0_attn_W_o"], dtype=f),
        rw=np.ascontiguousarray(inputs["L0_ffn_router_w"], dtype=f),
        rb=np.ascontiguousarray(
            np.asarray(inputs["L0_ffn_router_bias"], dtype=f).reshape(1, NE)),
        gw=np.ascontiguousarray(inputs["L0_ffn_gate_w"], dtype=f),
        uw=np.ascontiguousarray(inputs["L0_ffn_up_w"], dtype=f),
        dw=np.ascontiguousarray(inputs["L0_ffn_down_w"], dtype=f),
        sgw=np.ascontiguousarray(inputs["L0_ffn_sgate_w"], dtype=f),
        suw=np.ascontiguousarray(inputs["L0_ffn_sup_w"], dtype=f),
        sdw=np.ascontiguousarray(inputs["L0_ffn_sdown_w"], dtype=f),
        **host_consts(),
    )
    return [dict(x=np.ascontiguousarray(x[i]), **wmap) for i in range(n_cores)]


def kernel(**inputs):
    in_maps = make_in_maps(inputs, 8)
    nc = build_program(debug=False)
    res = run_bass_kernel_spmd(nc, in_maps, core_ids=list(range(8)))
    return np.stack([np.asarray(res.results[i]["out"]) for i in range(8)], axis=0)


# version 35 (final)
# speedup vs baseline: 1.0069x; 1.0068x over previous
import numpy as np
import ml_dtypes
from contextlib import ExitStack

import concourse.bass as bass
import concourse.mybir as mybir
from concourse.bass_utils import run_bass_kernel_spmd

F32 = mybir.dt.float32
BF16 = mybir.dt.bfloat16
AF = mybir.ActivationFunctionType
ALU = mybir.AluOpType
AX = mybir.AxisListType

T = 4096
C = 1024
NH = 16
KVL = 256
NE = 32
SH = 512
NQG = 8
CVW = 65

ENGINES = ("sync", "gpsimd", "tensor", "vector", "scalar")


class Sem:
    def __init__(self, nc, ctx, name):
        self.h = ctx.enter_context(nc.semaphore(name))
        self.n = 0


class Prog:
    def __init__(self):
        self.q = {e: [] for e in ENGINES}
        self.last_wait = {}

    def wait(self, eng, sem, val):
        if val <= 0:
            return
        key = (eng, id(sem))
        if self.last_wait.get(key, -1) >= val:
            return
        self.last_wait[key] = val
        self.q[eng].append(lambda E, s=sem.h, v=val: E.wait_ge(s, v))

    def op(self, eng, name, *args, inc=None, inc_k=1, **kwargs):
        if inc is not None:
            inc.n += inc_k
            hh, k = inc.h, inc_k
        else:
            hh, k = None, 0

        def f(E, nm=name, a=args, kw=kwargs, h=hh, kk=k):
            ins = getattr(E, nm)(*a, **kw)
            if h is not None:
                ins.then_inc(h, kk)
        self.q[eng].append(f)

    def dma(self, eng, out, in_, sem):
        self.op(eng, "dma_start", inc=sem, inc_k=16, out=out, in_=in_)

    def replay(self, eng, E):
        for f in self.q[eng]:
            f(E)


def build_program(debug=False, stop_after=None):
    nc = bass.Bass("TRN2", target_bir_lowering=False, debug=False)

    x_h = nc.dram_tensor("x", [T, C], F32, kind="ExternalInput").ap()
    wdq_h = nc.dram_tensor("wdq", [KVL, C], F32, kind="ExternalInput").ap()
    wuq_h = nc.dram_tensor("wuq", [C, KVL], F32, kind="ExternalInput").ap()
    wdkv_h = nc.dram_tensor("wdkv", [KVL, C], F32, kind="ExternalInput").ap()
    wuk_h = nc.dram_tensor("wuk", [C, KVL], F32, kind="ExternalInput").ap()
    wuv_h = nc.dram_tensor("wuv", [C, KVL], F32, kind="ExternalInput").ap()
    wo_h = nc.dram_tensor("wo", [C, C], F32, kind="ExternalInput").ap()
    rw_h = nc.dram_tensor("rw", [NE, C], F32, kind="ExternalInput").ap()
    rb_h = nc.dram_tensor("rb", [1, NE], F32, kind="ExternalInput").ap()
    gw_h = nc.dram_tensor("gw", [NE, 128, C], F32, kind="ExternalInput").ap()
    uw_h = nc.dram_tensor("uw", [NE, 128, C], F32, kind="ExternalInput").ap()
    dw_h = nc.dram_tensor("dw", [NE, C, 128], F32, kind="ExternalInput").ap()
    sgw_h = nc.dram_tensor("sgw", [SH, C], F32, kind="ExternalInput").ap()
    suw_h = nc.dram_tensor("suw", [SH, C], F32, kind="ExternalInput").ap()
    sdw_h = nc.dram_tensor("sdw", [C, SH], F32, kind="ExternalInput").ap()
    identb_h = nc.dram_tensor("identb", [128, 128], BF16, kind="ExternalInput").ap()
    identf_h = nc.dram_tensor("identf", [128, 128], F32, kind="ExternalInput").ap()
    masks_h = nc.dram_tensor("masks", [128, 128], F32, kind="ExternalInput").ap()
    onehot_h = nc.dram_tensor("onehot", [NE, NE * 128], BF16, kind="ExternalInput").ap()
    onesb_h = nc.dram_tensor("onesb", [1, 64], BF16, kind="ExternalInput").ap()
    onesf_h = nc.dram_tensor("onesf", [1, 128], F32, kind="ExternalInput").ap()
    negonesf_h = nc.dram_tensor("negonesf", [1, 128], F32, kind="ExternalInput").ap()

    out_h = nc.dram_tensor("out", [T, C], F32, kind="ExternalOutput").ap()

    x2_sc = nc.dram_tensor("x2_sc", [T, C], F32,
                           kind="ExternalOutput" if debug else "Internal").ap()
    gate_sc = nc.dram_tensor("gate_sc", [NE, 128, C], BF16, kind="Internal").ap()
    up_sc = nc.dram_tensor("up_sc", [NE, 128, C], BF16, kind="Internal").ap()
    down_sc = nc.dram_tensor("down_sc", [4, 128, NE * 256], BF16, kind="Internal").ap()
    sg_sc = nc.dram_tensor("sg_sc", [128, 8 * SH], BF16, kind="Internal").ap()
    su_sc = nc.dram_tensor("su_sc", [128, 8 * SH], BF16, kind="Internal").ap()
    sd_sc = nc.dram_tensor("sd_sc", [128, 4 * C], BF16, kind="Internal").ap()

    dbg = {}
    if debug:
        for nm, shp, dt in [("d_ckv", [128, 2 * T], BF16), ("d_qlow", [128, 2 * T], BF16),
                            ("d_cv", [128, 32 * NH * CVW], BF16), ("d_y", [128, 8 * T], BF16),
                            ("d_xn2", [128, 8 * T], BF16), ("d_wfm", [NE, T], BF16),
                            ("d_hall", [128, NE * 512], BF16), ("d_negc", [128, NH], F32),
                            ("d_bht", [128, NH * 2 * KVL], BF16),
                            ("d_t1", [128, 2 * KVL], BF16), ("d_m", [128, 8 * KVL], BF16),
                            ("d_v", [128, 2 * C], BF16), ("d_woT", [128, 8 * C], BF16),
                            ("d_wdkvT", [128, 8 * KVL], BF16),
                            ("d_xnfm", [128, 2 * 4096], BF16)]:
            dbg[nm] = nc.dram_tensor(nm, shp, dt, kind="ExternalOutput").ap()

    P = Prog()
    with ExitStack() as ctx:
        def finish(dumps):
            P.wait("sync", ve, ve.n)
            P.wait("sync", ae, ae.n)
            P.wait("sync", pe, pe.n)
            P.wait("sync", sc, sc.n)
            P.wait("sync", gp, gp.n)
            for nm, src_sb in dumps:
                P.dma("sync", dbg[nm], src_sb[:], st)
            P.wait("sync", st, st.n)
            block = ctx.enter_context(nc.Block())
            block.sync(lambda E: P.replay("sync", E))
            block.gpsimd(lambda E: P.replay("gpsimd", E))
            block.tensor(lambda E: P.replay("tensor", E))
            block.vector(lambda E: P.replay("vector", E))
            block.scalar(lambda E: P.replay("scalar", E))

        sy = Sem(nc, ctx, "sy")
        gp = Sem(nc, ctx, "gp")
        sc = Sem(nc, ctx, "sc")
        pe = Sem(nc, ctx, "pe")
        ve = Sem(nc, ctx, "ve")
        ae = Sem(nc, ctx, "ae")
        st = Sem(nc, ctx, "st")

        import os as _os
        _trace_sb = _os.environ.get("TRACE_SB")

        def sb(stk, name, shape, dtype, side=None):
            h = stk.enter_context(
                nc.sbuf_tensor("s_" + name, shape, dtype, side=side))
            if _trace_sb:
                import math
                per = math.prod(shape[1:]) * (2 if "bfloat" in str(dtype) else 4)
                print(f"SBUF + {name:12s} {per:7d}B/part side={side} "
                      f"remaining={nc.sbuf_bytes_remaining}")
            return h

        def psm(stk, name, shape, dtype=F32):
            return stk.enter_context(nc.psum_tensor(name, shape, dtype))

        identb = sb(ctx, "identb", [128, 128], BF16)
        identf = sb(ctx, "identf", [128, 128], F32)
        onesb = sb(ctx, "onesb", [1, 64], BF16)
        onesf = sb(ctx, "onesf", [1, 128], F32)
        negonesf = sb(ctx, "negonesf", [1, 128], F32)
        rbias = sb(ctx, "rbias", [1, NE], F32)
        rwT = sb(ctx, "rwT", [128, 8 * NE], F32)
        eps_col = sb(ctx, "eps_col", [128, 1], F32)
        sPab = ctx.enter_context(ExitStack())
        pa = psm(sPab, "pa", [128, 512], F32)
        pb = psm(sPab, "pb", [128, 512], F32)

        P.dma("sync", identb[:], identb_h, sy)
        sy_identb = sy.n
        P.op("vector", "memset", eps_col[:], 1e-5, inc=ve)
        for dst, src in [(identf, identf_h),
                         (onesb, onesb_h), (onesf, onesf_h),
                         (negonesf, negonesf_h), (rbias, rb_h)]:
            P.dma("sync", dst[:], src, sy)

        sAB = ctx.enter_context(ExitStack())
        ckv = sb(sAB, "ckv", [128, 2 * T], BF16)
        qlow = sb(sAB, "qlow", [128, 2 * T], BF16)
        bht = sb(sAB, "bht", [128, NH * 2 * KVL], BF16)
        cv = sb(sAB, "cv", [128, 32 * NH * CVW], BF16)

        negc_all = sb(sAB, "negc_all", [128, NH], F32)
        sA = ctx.enter_context(ExitStack())
        wdqT = sb(sA, "wdqT", [128, 8 * KVL], BF16)
        wdkvT = sb(sA, "wdkvT", [128, 8 * KVL], BF16)
        v_b = sb(sA, "v_b", [128, 2 * C], BF16)

        sXr = ctx.enter_context(ExitStack())
        x_ring = sb(sXr, "x_ring", [128, 6 * C], F32)
        xn_ring = sb(sXr, "xn_ring", [128, 4 * C], BF16)

        s0a = ctx.enter_context(ExitStack())
        wuq_b = sb(s0a, "wuq_b", [128, 8 * KVL], BF16)
        wuk_b = sb(s0a, "wuk_b", [128, 8 * KVL], BF16)
        wuv_b = sb(s0a, "wuv_b", [128, 8 * KVL], BF16)
        wdq_b = sb(s0a, "wdq_b", [128, 2 * C], BF16)
        wdkv_b = sb(s0a, "wdkv_b", [128, 2 * C], BF16)
        wo_b = sb(s0a, "wo_b", [128, 4 * C], BF16)
        woT = sb(s0a, "woT", [128, 4 * C], BF16)
        t1 = sb(s0a, "t1", [128, 2 * KVL], BF16)
        m_b = sb(s0a, "m_b", [128, 8 * KVL], BF16)
        rw_f = sb(s0a, "rw_f", [NE, C], F32)

        sPA = ctx.enter_context(ExitStack())
        pt0 = psm(sPA, "pt0", [128, 1024], BF16)
        pt1 = psm(sPA, "pt1", [128, 1024], BF16)
        pob0 = psm(sPA, "pob0", [128, 1024], BF16)
        pob1 = psm(sPA, "pob1", [128, 1024], BF16)
        pob2 = psm(sPA, "pob2", [128, 1024], BF16)
        pf = psm(sPA, "pf", [128, 512], F32)

        P.dma("gpsimd", wdq_b[:].rearrange("p (c k) -> p c k", c=2),
              wdq_h.rearrange("(c p) k -> p c k", p=128), gp)
        P.dma("gpsimd", wdkv_b[:].rearrange("p (c k) -> p c k", c=2),
              wdkv_h.rearrange("(c p) k -> p c k", p=128), gp)
        gp_dq = gp.n
        P.dma("gpsimd", wuq_b[:].rearrange("p (c k) -> p c k", c=8),
              wuq_h.rearrange("(c p) k -> p c k", p=128), gp)
        P.dma("gpsimd", wuk_b[:].rearrange("p (c k) -> p c k", c=8),
              wuk_h.rearrange("(c p) k -> p c k", p=128), gp)
        P.dma("gpsimd", wuv_b[:].rearrange("p (c k) -> p c k", c=8),
              wuv_h.rearrange("(c p) k -> p c k", p=128), gp)
        gp_w = gp.n
        P.dma("gpsimd", wo_b[:].rearrange("p (c k) -> p c k", c=4),
              wo_h[0:512, :].rearrange("(c p) k -> p c k", p=128), gp)
        gp_wo = [gp.n, None]
        P.dma("sync", rw_f[:], rw_h, sy)
        sy_c = sy.n

        tto_free = [0]

        def transpose_to(dst_sb, src_sb, nr, nf, pst):
            blocks = [(a, f) for a in range(nr) for f in range(nf)]
            for i0 in range(0, len(blocks), 8):
                grp = blocks[i0:i0 + 8]
                P.wait("tensor", ve, tto_free[0])
                for j, (a, f) in enumerate(grp):
                    P.op("tensor", "transpose", pst[:, j * 128:(j + 1) * 128],
                         src_sb[:, (a * nf + f) * 128:(a * nf + f) * 128 + 128],
                         identb[:], inc=(pe if j == len(grp) - 1 else None))
                P.wait("vector", pe, pe.n)
                for j, (a, f) in enumerate(grp):
                    P.op("vector", "tensor_copy",
                         out=dst_sb[:, (f * nr + a) * 128:(f * nr + a) * 128 + 128],
                         in_=pst[:, j * 128:(j + 1) * 128],
                         inc=(ve if j == len(grp) - 1 else None))
                tto_free[0] = ve.n

        P.wait("tensor", sy, sy_identb)
        P.wait("tensor", gp, gp_dq)
        transpose_to(wdqT, wdq_b, 2, 8, pt0)
        transpose_to(wdkvT, wdkv_b, 2, 8, pt0)

        P.wait("tensor", gp, gp_w)
        for mt in range(2):
            tgt = pa if mt == 0 else pb
            for ch in range(8):
                P.op("tensor", "matmul", tgt[:, 0:KVL],
                     wuq_b[:, ch * KVL + mt * 128: ch * KVL + mt * 128 + 128],
                     wuk_b[:, ch * KVL:(ch + 1) * KVL],
                     start=(ch == 0), stop=(ch == 7), inc=(pe if ch == 7 else None))
        P.wait("vector", pe, pe.n)
        P.op("vector", "tensor_copy", out=t1[:, 0:KVL], in_=pa[:, 0:KVL])
        P.op("vector", "tensor_copy", out=t1[:, KVL:2 * KVL], in_=pb[:, 0:KVL], inc=ve)
        P.wait("tensor", ve, ve.n)
        m_ps_free = [0, 0]
        for mt in range(8):
            tgt = pa if mt % 2 == 0 else pb
            P.wait("tensor", ve, m_ps_free[mt % 2])
            for ch in range(2):
                P.op("tensor", "matmul", tgt[:, 0:KVL],
                     wdq_b[:, ch * C + mt * 128: ch * C + mt * 128 + 128],
                     t1[:, ch * KVL:(ch + 1) * KVL],
                     start=(ch == 0), stop=(ch == 1), inc=(pe if ch == 1 else None))
            P.wait("vector", pe, pe.n)
            P.op("vector", "tensor_copy", out=m_b[:, mt * KVL:(mt + 1) * KVL],
                 in_=tgt[:, 0:KVL], inc=ve)
            m_ps_free[mt % 2] = ve.n
        P.wait("tensor", ve, ve.n)
        bht_ps_free = 0
        for h in range(NH):
            c2, off = h // 2, (h % 2) * 64
            P.wait("tensor", ve, bht_ps_free)
            for mt in range(2):
                tgt = pa if mt == 0 else pb
                P.op("tensor", "matmul", tgt[:, 0:KVL],
                     wuq_b[off:off + 64, c2 * KVL + mt * 128: c2 * KVL + mt * 128 + 128],
                     m_b[off:off + 64, c2 * KVL:(c2 + 1) * KVL],
                     start=True, stop=True, inc=(pe if mt == 1 else None))
            P.wait("vector", pe, pe.n)
            P.op("vector", "tensor_copy",
                 out=bht[:, (2 * h) * KVL:(2 * h + 1) * KVL], in_=pa[:, 0:KVL])
            P.op("vector", "tensor_copy",
                 out=bht[:, (2 * h + 1) * KVL:(2 * h + 2) * KVL], in_=pb[:, 0:KVL],
                 inc=ve)
            bht_ps_free = ve.n

        v_ps_free = 0
        for half in range(2):
            P.wait("tensor", gp, gp_wo[half])
            transpose_to(woT, wo_b, 4, 8, pt0)
            woT_done_h = ve.n
            if half == 0:
                P.wait("gpsimd", pe, pe.n)
                P.dma("gpsimd", wo_b[:].rearrange("p (c k) -> p c k", c=4),
                      wo_h[512:1024, :].rearrange("(c p) k -> p c k", p=128), gp)
                gp_wo[1] = gp.n
            P.wait("tensor", ve, woT_done_h)
            for mt in range(2):
                tgt = pa if mt == 0 else pb
                P.wait("tensor", ve, v_ps_free)
                for ch in range(8):
                    P.op("tensor", "matmul", tgt[:],
                         wuv_b[:, ch * KVL + mt * 128: ch * KVL + mt * 128 + 128],
                         woT[:, ch * 512: ch * 512 + 512],
                         start=(ch == 0), stop=(ch == 7),
                         inc=(pe if ch == 7 else None))
            P.wait("vector", pe, pe.n)
            P.op("vector", "tensor_copy",
                 out=v_b[:, half * 512: half * 512 + 512], in_=pa[:])
            P.op("vector", "tensor_copy",
                 out=v_b[:, C + half * 512: C + half * 512 + 512],
                 in_=pb[:], inc=ve)
            v_ps_free = ve.n
        P.wait("tensor", sy, sy_c)
        rwt_ps_free = v_ps_free
        for grp in range(2):
            P.wait("tensor", ve, rwt_ps_free)
            for j4 in range(4):
                j = grp * 4 + j4
                P.op("tensor", "transpose", pa[:, j4 * 32:(j4 + 1) * 32],
                     rw_f[:, j * 128:(j + 1) * 128], identf[0:32, 0:32],
                     inc=(pe if j4 == 3 else None))
            P.wait("vector", pe, pe.n)
            for j4 in range(4):
                j = grp * 4 + j4
                P.op("vector", "tensor_copy", out=rwT[:, j * NE:(j + 1) * NE],
                     in_=pa[:, j4 * 32:(j4 + 1) * 32],
                     inc=(ve if j4 == 3 else None))
            rwt_ps_free = ve.n
        pe_0a_done = pe.n
        pab_free_0a = ve.n
        proj_ps_free_init = ve.n
        s0a.close()

        sAt = ctx.enter_context(ExitStack())
        st6 = sb(sAt, "st6", [128, 96], F32)
        st2 = sb(sAt, "st2", [128, 16], F32)
        lnv = sb(sAt, "lnv", [128, 8], F32)
        rstd = sb(sAt, "rstd", [128, 8], F32)
        xnfm = sb(sAt, "xnfm", [128, 4096], BF16)
        gwf_ring = sb(sAt, "gwf_ring", [128, 4 * C], F32)
        gwb_ring = sb(sAt, "gwb_ring", [128, 2 * C], BF16)
        gt_ring = sb(sAt, "gt_ring", [128, 2 * C], BF16)
        sq_tmp = sb(sAt, "sq_tmp", [128, 2 * 512], BF16)
        ones_col = sb(sAt, "ones_col", [128, 1], BF16)
        nrm_tmp = sb(sAt, "nrm_tmp", [1, 16], F32)
        nq2 = sb(sAt, "nq2", [1, 4], F32)
        frx = sb(sAt, "frx", [1, 16], F32)
        frx2 = sb(sAt, "frx2", [1, 32], F32)

        ob_gw_free = [0, 0, 0, 0]
        ob_gt_free = [0, 0]
        ob_ps_free = [0, 0, 0]

        ob_gw_at = {}
        ob_gwb_free = [0, 0]

        ob_jobs = [(k, e) for k in ("g", "u", "d") for e in range(NE)]
        ob_jobs += [("sg", i) for i in range(4)] + [("su", i) for i in range(4)] \
            + [("sd", i) for i in range(4)]
        OB_NPAIRED = 2 * NE

        def ob_load_pair(j0):
            kind, e = ob_jobs[j0]
            slot = j0 % 4
            P.wait("gpsimd", ae, ob_gw_free[slot])
            P.wait("gpsimd", ae, ob_gw_free[slot + 1])
            dst = gwf_ring[:, slot * C: (slot + 2) * C]
            if kind == "g":
                P.dma("gpsimd", dst.rearrange("p (a k) -> p a k", a=2),
                      gw_h[e:e + 2].rearrange("a p k -> p a k"), gp)
            else:
                P.dma("gpsimd", dst.rearrange("p (a k) -> p a k", a=2),
                      uw_h[e:e + 2].rearrange("a p k -> p a k"), gp)
            ob_gw_at[j0] = gp.n
            ob_gw_at[j0 + 1] = gp.n

        def ob_load_single(jidx, src_h3, nr, nf):
            slot = jidx % 4
            ncols = nf * 128
            P.wait("gpsimd", ae, ob_gw_free[slot])
            P.dma("gpsimd",
                  gwf_ring[:, slot * C: slot * C + nr * ncols]
                  .rearrange("p (a k) -> p a k", a=nr),
                  src_h3, gp)
            ob_gw_at[jidx] = gp.n

        def ob_store_pair(jidx):
            kind, e = ob_jobs[jidx]
            if kind == "g":
                P.dma("scalar", gate_sc[e - 1:e + 1].rearrange("a p k -> p a k"),
                      gt_ring[:, 0:2 * C].rearrange("p (a k) -> p a k", a=2), sc)
            else:
                P.dma("scalar", up_sc[e - 1:e + 1].rearrange("a p k -> p a k"),
                      gt_ring[:, 0:2 * C].rearrange("p (a k) -> p a k", a=2), sc)
            ob_gt_free[0] = sc.n
            ob_gt_free[1] = sc.n

        def ob_compute(jidx, nr, nf, write_out_ap, gt_view):
            paired = jidx < OB_NPAIRED
            slot = jidx % 4
            gts = jidx % 2
            gb2 = jidx % 2
            P.wait("scalar", gp, ob_gw_at[jidx])
            P.wait("scalar", pe, ob_gwb_free[gb2])
            P.op("scalar", "activation", out=gwb_ring[:, gb2 * C:(gb2 + 1) * C],
                 in_=gwf_ring[:, slot * C:(slot + 1) * C], func=AF.Copy, inc=ae)
            ob_gw_free[slot] = ae.n
            P.wait("tensor", ae, ae.n)
            P.wait("tensor", sc, ob_gt_free[gts])
            P.wait("tensor", ve, ob_ps_free[jidx % 3])
            pst = (pob0, pob1, pob2)[jidx % 3]
            blocks = [(a, f) for a in range(nr) for f in range(nf)]
            for j, (a, f) in enumerate(blocks):
                P.op("tensor", "transpose", pst[:, j * 128:(j + 1) * 128],
                     gwb_ring[:, gb2 * C + (a * nf + f) * 128:
                              gb2 * C + (a * nf + f) * 128 + 128],
                     identb[:], inc=(pe if j == 7 else None))
            ob_gwb_free[gb2] = pe.n
            P.wait("vector", pe, pe.n)
            if nr == 1 or nf == 1:
                P.op("vector", "tensor_copy",
                     out=gt_ring[:, gts * C:(gts + 1) * C],
                     in_=pst[:, 0:1024], inc=ve)
            else:
                for j, (a, f) in enumerate(blocks):
                    P.op("vector", "tensor_copy",
                         out=gt_ring[:, gts * C + (f * nr + a) * 128:
                                     gts * C + (f * nr + a) * 128 + 128],
                         in_=pst[:, j * 128:(j + 1) * 128],
                         inc=(ve if j == 7 else None))
            ob_ps_free[jidx % 3] = ve.n
            P.wait("scalar", ve, ve.n)
            if paired:
                if gts == 1:
                    ob_store_pair(jidx)
            else:
                P.dma("scalar", write_out_ap,
                      gt_view(gt_ring[:, gts * C: gts * C + C]), sc)
                ob_gt_free[gts] = sc.n

        def ob_spec(job):
            kind, e = job
            if kind == "g":
                return (gw_h[e].rearrange("(a p) k -> p a k", p=128), 1, 8,
                        gate_sc[e], lambda v: v)
            if kind == "u":
                return (uw_h[e].rearrange("(a p) k -> p a k", p=128), 1, 8,
                        up_sc[e], lambda v: v)
            if kind == "d":
                return (dw_h[e].rearrange("(a p) k -> p a k", p=128), 8, 1,
                        down_sc[:, :, e * 256:(e + 1) * 256]
                        .rearrange("dc p k -> p dc k"),
                        lambda v: v.rearrange("p (a k) -> p a k", a=4))
            if kind == "sg":
                return (sgw_h[e * 128:(e + 1) * 128, :]
                        .rearrange("(a p) k -> p a k", p=128), 1, 8,
                        sg_sc.rearrange("p (cc q k) -> p cc q k", cc=8, q=4)
                        [:, :, e, :],
                        lambda v: v.rearrange("p (a k) -> p a k", a=8))
            if kind == "su":
                return (suw_h[e * 128:(e + 1) * 128, :]
                        .rearrange("(a p) k -> p a k", p=128), 1, 8,
                        su_sc.rearrange("p (cc q k) -> p cc q k", cc=8, q=4)
                        [:, :, e, :],
                        lambda v: v.rearrange("p (a k) -> p a k", a=8))
            if kind == "sd":
                return (sdw_h[e * 256:(e + 1) * 256, :]
                        .rearrange("(a p) k -> p a k", p=128), 2, 4,
                        sd_sc.rearrange("p (jc dc k) -> p jc dc k", jc=4, dc=8)
                        [:, :, e * 2:e * 2 + 2, :],
                        lambda v: v.rearrange("p (a b k) -> p a b k", a=4, b=2))

        ob_next_load = [0]
        ob_next_comp = [0]

        def ob_pump(n):
            for _ in range(n):
                if ob_next_comp[0] >= len(ob_jobs):
                    return
                while (ob_next_load[0] < len(ob_jobs)
                       and ob_next_load[0] < ob_next_comp[0] + 3):
                    jl = ob_next_load[0]
                    if jl < OB_NPAIRED:
                        ob_load_pair(jl)
                        ob_next_load[0] += 2
                    else:
                        s = ob_spec(ob_jobs[jl])
                        ob_load_single(jl, s[0], s[1], s[2])
                        ob_next_load[0] += 1
                jc_ = ob_next_comp[0]
                s = ob_spec(ob_jobs[jc_])
                ob_compute(jc_, s[1], s[2], s[3], s[4])
                ob_next_comp[0] += 1

        x_slot_free = [0] * 6
        xn_pe_done = [0, 0, 0, 0]
        xnfm_pe_free = [0, 0]
        proj_ps_free = [0, pab_free_0a]
        ob_gw_free = [0, 0, 0, 0]
        ob_ps_free = [proj_ps_free_init] * 3
        P.wait("gpsimd", pe, pe_0a_done)
        P.wait("gpsimd", ve, proj_ps_free_init)
        rstd_at = {}
        stat_at = {}
        evac_at = [0, 0, 0, 0]

        loads_at = {}

        def a_loads(ck, ts):
            for t in ts:
                tt = ck * 4 + t
                xs = tt % 6
                P.wait("sync", ve, x_slot_free[xs])
                P.dma("sync", x_ring[:, xs * C:(xs + 1) * C],
                      x_h[tt * 128:(tt + 1) * 128, :], sy)
                loads_at[(ck, t)] = sy.n

        def a_s1a(ck):
            sc2 = ck % 2
            for t in range(2):
                xs = (ck * 4 + t) % 6
                P.wait("vector", sy, loads_at[(ck, t)])
                P.op("vector", "bn_stats", out=st6[:, sc2 * 48 + t * 12: sc2 * 48 + t * 12 + 6],
                     in_=x_ring[:, xs * C: xs * C + 512])
                P.op("vector", "bn_stats",
                     out=st6[:, sc2 * 48 + t * 12 + 6: sc2 * 48 + t * 12 + 12],
                     in_=x_ring[:, xs * C + 512: xs * C + 1024])

        def a_s1b(ck):
            sc2 = ck % 2
            for t in range(2, 4):
                xs = (ck * 4 + t) % 6
                P.wait("vector", sy, loads_at[(ck, t)])
                P.op("vector", "bn_stats", out=st6[:, sc2 * 48 + t * 12: sc2 * 48 + t * 12 + 6],
                     in_=x_ring[:, xs * C: xs * C + 512])
                P.op("vector", "bn_stats",
                     out=st6[:, sc2 * 48 + t * 12 + 6: sc2 * 48 + t * 12 + 12],
                     in_=x_ring[:, xs * C + 512: xs * C + 1024],
                     inc=(ve if t == 3 else None))
            P.wait("vector", ve, ve.n)
            for t in range(4):
                P.op("vector", "bn_aggr", out=st2[:, sc2 * 8 + t * 2: sc2 * 8 + t * 2 + 2],
                     in_=st6[:, sc2 * 48 + t * 12: sc2 * 48 + t * 12 + 12],
                     inc=(ve if t == 3 else None))
            P.wait("vector", ve, ve.n)
            P.op("vector", "tensor_scalar", out=lnv[:, sc2 * 4: sc2 * 4 + 4],
                 in0=st2[:, sc2 * 8 + 1: sc2 * 8 + 8: 2],
                 scalar1=1e-5, scalar2=None, op0=ALU.add, inc=ve)
            P.wait("scalar", ve, ve.n)
            P.op("scalar", "activation", out=rstd[:, sc2 * 4: sc2 * 4 + 4],
                 in_=lnv[:, sc2 * 4: sc2 * 4 + 4], func=AF.Ln, inc=ae)
            P.wait("scalar", ae, ae.n)
            P.op("scalar", "activation", out=rstd[:, sc2 * 4: sc2 * 4 + 4],
                 in_=rstd[:, sc2 * 4: sc2 * 4 + 4], func=AF.Exp, scale=-0.5, inc=ae)
            rstd_at[ck] = ae.n

        xn_at_g = {}

        def a_apply(ck, t):
            sc2 = ck % 2
            xs = (ck * 4 + t) % 6
            if t == 0:
                P.wait("vector", ae, rstd_at[ck])
            P.wait("vector", pe, xn_pe_done[t])
            P.op("vector", "tensor_scalar",
                 out=xn_ring[:, t * C:(t + 1) * C],
                 in0=x_ring[:, xs * C:(xs + 1) * C],
                 scalar1=st2[:, sc2 * 8 + t * 2: sc2 * 8 + t * 2 + 1],
                 scalar2=rstd[:, sc2 * 4 + t: sc2 * 4 + t + 1],
                 op0=ALU.subtract, op1=ALU.mult, inc=ve)
            x_slot_free[xs] = ve.n
            xn_at_g[(ck, t)] = ve.n

        def a_s2b(ck):
            cbuf = 0
            rate = 0 if ck < 2 else 3
            for t in range(4):
                ob_pump(rate)
                pst = pt0 if t % 2 == 0 else pt1
                P.wait("tensor", ve, xn_at_g.pop((ck, t)))
                if t >= 2:
                    P.wait("tensor", ve, evac_at[t - 2])
                for j in range(8):
                    P.op("tensor", "transpose", pst[:, j * 128:(j + 1) * 128],
                         xn_ring[:, t * C + j * 128: t * C + j * 128 + 128],
                         identb[:], inc=(pe if j == 7 else None))
                xn_pe_done[t] = pe.n
                P.wait("vector", pe, pe.n)
                P.wait("vector", pe, xnfm_pe_free[cbuf])
                P.op("vector", "tensor_copy",
                     out=xnfm[:, cbuf * 4096:(cbuf + 1) * 4096]
                     .rearrange("p (c q) -> p c q", c=8)[:, :, t * 128:(t + 1) * 128],
                     in_=pst[:].rearrange("p (c q) -> p c q", c=8),
                     inc=ve)
                evac_at[t] = ve.n
                if ck + 1 < 8:
                    a_apply(ck + 1, t)
            xnfm_ready = ve.n
            P.wait("tensor", ve, xnfm_ready)
            for gi, (wT, mt, dst) in enumerate(
                    ((wdkvT, 0, ckv), (wdkvT, 1, ckv),
                     (wdqT, 0, qlow), (wdqT, 1, qlow))):
                ob_pump(0 if ck < 2 else 2)
                tgt = pf if gi % 2 == 0 else pb
                P.wait("tensor", ve, proj_ps_free[gi % 2])
                for ch in range(8):
                    P.op("tensor", "matmul", tgt[:],
                         wT[:, ch * KVL + mt * 128: ch * KVL + mt * 128 + 128],
                         xnfm[:, cbuf * 4096 + ch * 512:
                              cbuf * 4096 + ch * 512 + 512],
                         start=(ch == 0), stop=(ch == 7),
                         inc=(pe if ch == 7 else None))
                P.wait("vector", pe, pe.n)
                P.op("vector", "tensor_copy",
                     out=dst[:, mt * T + ck * 512: mt * T + ck * 512 + 512],
                     in_=tgt[:], inc=ve)
                proj_ps_free[gi % 2] = ve.n
            xnfm_pe_free[cbuf] = pe.n

        a_loads(0, range(4))
        a_loads(1, (0, 1))
        a_s1a(0)
        a_s1b(0)
        for t in range(4):
            a_apply(0, t)
        for ck in range(8):
            if ck + 1 < 8:
                a_s1a(ck + 1)
            if ck + 1 < 8:
                a_loads(ck + 1, (2, 3))
            if ck + 2 < 8:
                a_loads(ck + 2, (0, 1))
            if ck + 1 < 8:
                a_s1b(ck + 1)
            a_s2b(ck)
        ob_pump(len(ob_jobs))
        ob_all_done = sc.n

        P.op("vector", "memset",
             cv[:].rearrange("p (kt h o) -> p kt h o", kt=32, h=NH)[:, :, :, 64:65],
             1.0, inc=ve)
        cv_ps_free = [pab_free_0a, proj_ps_free[1]]
        for kt in range(32):
            for half in range(2):
                tgt = pa if half == 0 else pb
                P.wait("tensor", ve, cv_ps_free[half])
                for ch in range(2):
                    P.op("tensor", "matmul", tgt[:],
                         ckv[:, ch * T + kt * 128: ch * T + kt * 128 + 128],
                         v_b[:, ch * C + half * 512: ch * C + half * 512 + 512],
                         start=(ch == 0), stop=(ch == 1),
                         inc=(pe if ch == 1 else None))
                P.wait("vector", pe, pe.n)
                base = kt * NH * CVW + half * 8 * CVW
                P.op("vector", "tensor_copy",
                     out=cv[:, base: base + 8 * CVW]
                     .rearrange("p (h z) -> p h z", h=8)[:, :, 0:64],
                     in_=tgt[:].rearrange("p (h d) -> p h d", h=8),
                     inc=ve)
                cv_ps_free[half] = ve.n
        P.op("vector", "memset", ones_col[:], 1.0, inc=ve)
        ones_done = ve.n
        nrm_ps_free = cv_ps_free[0]
        sq_free = [0, 0]
        nrm_idx = 0
        for which, srcbuf in ((0, qlow), (1, ckv)):
            for kk in range(8):
                par2 = kk % 2
                tgt = pa if par2 == 0 else pb
                for ch in range(2):
                    P.wait("scalar", pe, sq_free[ch])
                    P.op("scalar", "activation",
                         out=sq_tmp[:, ch * 512:(ch + 1) * 512],
                         in_=srcbuf[:, ch * T + kk * 512: ch * T + kk * 512 + 512],
                         func=AF.Square, inc=ae)
                sq_done = ae.n
                P.wait("tensor", ae, sq_done)
                P.wait("tensor", ve, nrm_ps_free)
                P.wait("tensor", ve, ones_done)
                for ch in range(2):
                    P.op("tensor", "matmul", tgt[0:1, 0:512],
                         ones_col[:], sq_tmp[:, ch * 512:(ch + 1) * 512],
                         start=(ch == 0), stop=(ch == 1),
                         inc=(pe if ch == 1 else None))
                sq_free[0] = pe.n
                sq_free[1] = pe.n
                P.wait("vector", pe, pe.n)
                P.op("vector", "reduce_max",
                     out=nrm_tmp[:, nrm_idx: nrm_idx + 1],
                     in_=tgt[0:1, 0:512], axis=AX.X, inc=ve)
                nrm_ps_free = ve.n
                nrm_idx += 1
            P.wait("vector", ve, ve.n)
            P.op("vector", "reduce_max", out=nq2[:, which:which + 1],
                 in_=nrm_tmp[:, 0:8] if which == 0 else nrm_tmp[:, 8:16],
                 axis=AX.X, inc=ve)
        nq2_done = ve.n
        sqb_free = [sq_free[0], sq_free[1]]
        frx_evac = [nrm_ps_free, nrm_ps_free]
        for j in range(NH):
            sl = j % 2
            tgt = pa if sl == 0 else pb
            P.wait("scalar", pe, sqb_free[sl])
            P.op("scalar", "activation", out=sq_tmp[:, sl * 512:(sl + 1) * 512],
                 in_=bht[:, j * 512:(j + 1) * 512], func=AF.Square, inc=ae)
            P.wait("tensor", ae, ae.n)
            P.wait("tensor", ve, frx_evac[sl])
            P.op("tensor", "matmul", tgt[0:1, 0:512], ones_col[:],
                 sq_tmp[:, sl * 512:(sl + 1) * 512], start=True, stop=True,
                 inc=pe)
            sqb_free[sl] = pe.n
            P.wait("vector", pe, pe.n)
            P.op("vector", "reduce_sum", out=frx[:, j:j + 1],
                 in_=tgt[0:1, 0:512], axis=AX.X, inc=ve)
            frx_evac[sl] = ve.n
        P.wait("vector", ve, nq2_done)
        P.wait("vector", ve, ve.n)
        P.op("vector", "tensor_tensor", out=nq2[:, 2:3], in0=nq2[:, 0:1],
             in1=nq2[:, 1:2], op=ALU.mult, inc=ve)
        P.wait("vector", ve, ve.n)
        P.op("vector", "tensor_tensor", out=frx2[:, 0:16], in0=frx[:, 0:16],
             in1=nq2[:, 2:3].broadcast_to([1, 16]), op=ALU.mult, inc=ve)
        P.wait("scalar", ve, ve.n)
        P.op("scalar", "activation", out=frx2[:, 16:32], in_=frx2[:, 0:16],
             func=AF.Ln, inc=ae)
        P.wait("scalar", ae, ae.n)
        P.op("scalar", "activation", out=frx2[:, 0:16], in_=frx2[:, 16:32],
             func=AF.Exp, scale=0.5, inc=ae)
        P.wait("vector", ae, ae.n)
        P.op("vector", "tensor_scalar", out=frx2[:, 16:32], in0=frx2[:, 0:16],
             scalar1=-0.12875, scalar2=-80.0, op0=ALU.mult, op1=ALU.max, inc=ve)
        P.wait("tensor", ve, ve.n)
        P.op("tensor", "matmul", pa[:, 0:16], onesf[0:1, :], frx2[:, 16:32],
             start=True, stop=True, inc=pe)
        P.wait("vector", pe, pe.n)
        P.op("vector", "tensor_copy", out=negc_all[:], in_=pa[:, 0:16], inc=ve)
        negc_all_done = ve.n

        cv_done = ve.n
        if stop_after == "A":
            finish([("d_ckv", ckv), ("d_qlow", qlow), ("d_cv", cv), ("d_bht", bht),
                    ("d_t1", t1), ("d_m", m_b), ("d_v", v_b), ("d_woT", woT),
                    ("d_wdkvT", wdkvT), ("d_xnfm", xnfm)])
            return nc
        sAt.close()
        sXr.close()
        sA.close()

        sBC = ctx.enter_context(ExitStack())
        yfm = sb(sBC, "yfm", [128, 8 * T], BF16, side="right")
        sBt = ctx.enter_context(ExitStack())
        P.wait("vector", sc, ob_all_done)
        P.wait("scalar", sc, ob_all_done)
        P.wait("vector", gp, gp.n)
        P.wait("scalar", gp, gp.n)
        qlat = sb(sBt, "qlat", [128, 2 * 1024], BF16)
        ptb = sb(sBt, "ptb", [128, 2 * 1024], BF16)
        masks = sb(sBt, "masks", [128, 128], F32)
        P.wait("sync", sc, ob_all_done)
        P.wait("sync", ve, negc_all_done)
        P.wait("sync", ve, cv_done)
        P.dma("sync", masks[:], masks_h, sy)
        masks_ld = sy.n
        P.wait("tensor", sy, masks_ld)
        srow = sb(sBt, "srow", [1, 512], F32)
        rsrow = sb(sBt, "rsrow", [1, 512], BF16)
        rsb_sb = sb(sBt, "rsb_sb", [64, 512], BF16)
        sPA.close()
        sPB = ctx.enter_context(ExitStack())
        ps_s0 = psm(sPB, "ps_s0", [128, 1024], F32)
        ps_s1 = psm(sPB, "ps_s1", [128, 1024], F32)
        ps_y0 = psm(sPB, "ps_y0", [128, 512], F32)
        ps_y1 = psm(sPB, "ps_y1", [128, 512], F32)

        sps_free = [0, 0]
        pt_free = [0, 0]
        pair_spans = {}
        mask_at = {}
        psy_free = [0, 0]
        ql_ps_free = [cv_done, cv_done]
        negc_at = {}
        ql_at = {}
        P.wait("tensor", ve, cv_done)
        P.wait("tensor", ve, negc_all_done)
        P.wait("scalar", ve, negc_all_done)
        P.wait("tensor", ae, ae.n)

        def b_prep(it2):
            qg2, h2 = it2 % NQG, it2 // NQG
            par = it2 % 2
            for mt in range(2):
                tgt = pa if mt == 0 else pb
                P.wait("tensor", ve, ql_ps_free[mt])
                for ch in range(2):
                    P.op("tensor", "matmul", tgt[:],
                         bht[:, (2 * h2 + ch) * KVL + mt * 128:
                             (2 * h2 + ch) * KVL + mt * 128 + 128],
                         qlow[:, ch * T + qg2 * 512: ch * T + qg2 * 512 + 512],
                         start=(ch == 0), stop=(ch == 1),
                         inc=(pe if ch == 1 else None))
            P.wait("vector", pe, pe.n)
            for mt in range(2):
                tgt = pa if mt == 0 else pb
                P.op("vector", "tensor_copy",
                     out=qlat[:, par * 1024 + mt * 512: par * 1024 + mt * 512 + 512],
                     in_=tgt[:], inc=ve)
            ql_evac = ve.n
            ql_at[it2] = ql_evac
            ql_ps_free[0] = ql_evac
            ql_ps_free[1] = ql_evac

        def b_smm(it2, p):
            qg2, h2 = it2 % NQG, it2 // NQG
            par = it2 % 2
            pp = p % 2
            sp = ps_s0 if pp == 0 else ps_s1
            P.wait("tensor", ae, sps_free[pp])
            off = 0
            spans = []
            for k2 in range(2):
                kb = 2 * p + k2
                d = kb - 4 * qg2
                qoff = 128 * d if d > 0 else 0
                w = 512 - qoff
                spans.append((off, qoff, w, d))
                for ch in range(2):
                    P.op("tensor", "matmul", sp[:, off:off + w],
                         ckv[:, ch * T + kb * 128: ch * T + kb * 128 + 128],
                         qlat[:, par * 1024 + ch * 512 + qoff:
                              par * 1024 + ch * 512 + qoff + w],
                         start=(ch == 0), stop=(ch == 1),
                         inc=(pe if (ch == 1 and k2 == 1) else None))
                off += w
            s_mm = pe.n
            pair_spans[(it2, p)] = spans
            m_at = None
            for (offk, qoff, w, d) in spans:
                if d >= 0:
                    P.wait("vector", pe, s_mm)
                    P.op("vector", "tensor_tensor",
                         out=sp[:, offk:offk + 128],
                         in0=sp[:, offk:offk + 128],
                         in1=masks[:], op=ALU.add, inc=ve)
                    m_at = ve.n
            P.wait("scalar", pe, s_mm)
            if m_at is not None:
                P.wait("scalar", ve, m_at)
            P.wait("scalar", pe, pt_free[pp])
            P.op("scalar", "activation",
                 out=ptb[:, pp * 1024: pp * 1024 + off],
                 in_=sp[:, 0:off], func=AF.Exp, scale=0.125,
                 bias=negc_all[:, h2:h2 + 1], inc=ae)
            sps_free[pp] = ae.n

        def b_ymm(it2, p, npair, yps):
            qg2, h2 = it2 % NQG, it2 // NQG
            pp = p % 2
            P.wait("tensor", ae, sps_free[pp])
            spans = pair_spans.pop((it2, p))
            for k2 in range(2):
                kb = 2 * p + k2
                off, qoff, w, d = spans[k2]
                P.op("tensor", "matmul", yps[0:CVW, qoff:qoff + w],
                     cv[:, kb * NH * CVW + h2 * CVW:
                        kb * NH * CVW + (h2 + 1) * CVW],
                     ptb[:, pp * 1024 + off: pp * 1024 + off + w],
                     start=(kb == 0), stop=(kb == npair * 2 - 1),
                     inc=(pe if k2 == 1 else None))
            pt_free[pp] = pe.n

        def b_tail_a(it2, yps, yb):
            P.wait("vector", pe, pe.n)
            P.op("vector", "tensor_copy", out=srow[:], in_=yps[64:65, :], inc=ve)
            P.wait("scalar", ve, ve.n)
            P.op("scalar", "activation", out=srow[:], in_=srow[:], func=AF.Ln,
                 inc=ae)
            P.wait("scalar", ae, ae.n)
            P.op("scalar", "activation", out=rsrow[:], in_=srow[:],
                 func=AF.Exp, scale=-1.0, inc=ae)
            return ae.n

        def b_tail_b(it2, yps, yb, rs_at):
            qg2, h2 = it2 % NQG, it2 // NQG
            P.wait("tensor", ae, rs_at)
            P.op("tensor", "matmul", yps[64:128, :], onesb[:], rsrow[:],
                 start=True, stop=True, inc=pe)
            P.wait("vector", pe, pe.n)
            P.op("vector", "tensor_copy", out=rsb_sb[:], in_=yps[64:128, :],
                 inc=ve)
            P.wait("vector", ve, ve.n)
            co = (h2 // 2) * T + qg2 * 512
            po = (h2 % 2) * 64
            P.op("vector", "tensor_tensor",
                 out=yfm[po:po + 64, co:co + 512],
                 in0=yps[0:64, :], in1=rsb_sb[:], op=ALU.mult, inc=ve)
            psy_free[yb] = ve.n

        NIT = NQG * NH
        b_prep(0)
        pend = None
        for it in range(NIT):
            qg = it % NQG
            npair = 2 * qg + 2
            yb = it % 2
            yps = ps_y0 if yb == 0 else ps_y1
            P.wait("tensor", ve, psy_free[yb])
            P.wait("tensor", ve, ql_at[it])
            rs_at = None
            for p in range(npair + 1):
                if p < npair:
                    b_smm(it, p)
                if p == 1:
                    if pend is not None:
                        rs_at = b_tail_a(*pend)
                    if it + 1 < NIT:
                        b_prep(it + 1)
                if p == max(1, npair - 1) and pend is not None:
                    b_tail_b(*pend, rs_at)
                    pend = None
                if p >= 1:
                    b_ymm(it, p - 1, npair, yps)
            pend = (it, yps, yb)
        rs_at = b_tail_a(*pend)
        b_tail_b(*pend, rs_at)
        attn_done = ve.n
        if stop_after == "B":
            finish([("d_y", yfm), ("d_ckv", ckv), ("d_cv", cv), ("d_negc", negc_all)])
            return nc

        sBt.close()
        sAB.close()
        sPB.close()

        sCD = ctx.enter_context(ExitStack())
        xn2 = sb(sCD, "xn2", [128, 8 * T], BF16)
        w_fm = sb(sCD, "w_fm", [NE, T], BF16)
        sCt = ctx.enter_context(ExitStack())
        xC = sb(sCt, "xC", [128, 4 * C], F32)
        x2_sb = sb(sCt, "x2_sb", [128, 8 * C], F32)
        xn2t = sb(sCt, "xn2t", [128, 2 * C], F32)
        xn2T = sb(sCt, "xn2T", [128, 2 * C], F32)
        c_st6 = sb(sCt, "c_st6", [128, 96], F32)
        c_st2 = sb(sCt, "c_st2", [128, 16], F32)
        c_lnv = sb(sCt, "c_lnv", [128, 8], F32)
        c_rstd = sb(sCt, "c_rstd", [128, 8], F32)
        lg_sb = sb(sCt, "lg_sb", [128, 4 * NE], F32)
        tk_t0 = sb(sCt, "tk_t0", [128, 4 * NE], F32)
        tk_mx = sb(sCt, "tk_mx", [128, 4 * 4], F32)
        tk_msk = sb(sCt, "tk_msk", [128, 4 * NE], F32)
        tk_es = sb(sCt, "tk_es", [128, 4 * NE], F32)
        tk_s = sb(sCt, "tk_s", [128, 8], F32)
        w_tok = sb(sCt, "w_tok", [128, 2 * 4 * NE], F32)

        sPC = ctx.enter_context(ExitStack())
        pc_t0 = psm(sPC, "pc_t0", [128, 1024], BF16)
        pc_t1 = psm(sPC, "pc_t1", [128, 1024], BF16)
        pc_f = psm(sPC, "pc_f", [128, 512], F32)
        pc_f2 = psm(sPC, "pc_f2", [128, 512], F32)

        xC_free = [0, 0, 0, 0]
        x2_free4 = [0] * 8
        x2_dma_done = [0] * 8
        yT_ps_free = [0, 0]
        xn2T_ps_free = [0, 0]
        c_rnd = [0]
        wfm_ps_free = 0
        wfm_pe_at = [0, 0]
        router_ps_free = 0
        xn2T_ps_free_pe = [0, 0]
        xn2T_router_free = [0, 0]
        c_rstd_at = {}
        P.wait("sync", ve, attn_done)
        g4 = lambda v: v.rearrange("p (g k) -> p g k", g=4)

        c_pend = [None]

        def c_trans(grp, t):
            tt = grp * 4 + t
            cb = t % 2
            pst = pc_t0 if cb == 0 else pc_t1
            P.wait("tensor", ae, yT_ps_free[cb])
            if tt == 0:
                P.wait("tensor", ve, attn_done)
            for j in range(8):
                P.op("tensor", "transpose", pst[:, j * 128:(j + 1) * 128],
                     yfm[:, j * T + tt * 128: j * T + tt * 128 + 128],
                     identb[:], inc=(pe if j == 7 else None))
            return pe.n

        def c_evac(grp, t, load_at, guard_mm):
            gb = grp % 2
            tt = grp * 4 + t
            t8 = gb * 4 + t
            cb = t % 2
            pst = pc_t0 if cb == 0 else pc_t1
            P.wait("scalar", pe, guard_mm)
            P.wait("scalar", gp, x2_dma_done[t8])
            P.wait("scalar", ve, x2_free4[t8])
            P.op("scalar", "activation", out=x2_sb[:, t8 * C:(t8 + 1) * C],
                 in_=pst[:], func=AF.Copy, inc=ae)
            yT_ps_free[cb] = ae.n
            P.wait("vector", ae, ae.n)
            P.wait("vector", sy, load_at)
            P.op("vector", "tensor_tensor", out=x2_sb[:, t8 * C:(t8 + 1) * C],
                 in0=x2_sb[:, t8 * C:(t8 + 1) * C],
                 in1=xC[:, t * C:(t + 1) * C], op=ALU.add, inc=ve)
            xC_free[t] = ve.n
            P.wait("gpsimd", ve, ve.n)
            P.dma("gpsimd", x2_sc[tt * 128:(tt + 1) * 128, :],
                  x2_sb[:, t8 * C:(t8 + 1) * C], gp)
            x2_dma_done[t8] = gp.n
            P.wait("vector", ve, ve.n)
            P.op("vector", "bn_stats",
                 out=c_st6[:, gb * 48 + t * 12: gb * 48 + t * 12 + 6],
                 in_=x2_sb[:, t8 * C: t8 * C + 512])
            P.op("vector", "bn_stats",
                 out=c_st6[:, gb * 48 + t * 12 + 6: gb * 48 + t * 12 + 12],
                 in_=x2_sb[:, t8 * C + 512: t8 * C + 1024],
                 inc=(ve if t == 3 else None))
            if t != 3:
                return
            P.wait("vector", ve, ve.n)
            for t2 in range(4):
                P.op("vector", "bn_aggr",
                     out=c_st2[:, gb * 8 + t2 * 2: gb * 8 + t2 * 2 + 2],
                     in_=c_st6[:, gb * 48 + t2 * 12: gb * 48 + t2 * 12 + 12],
                     inc=(ve if t2 == 3 else None))
            P.wait("vector", ve, ve.n)
            P.op("vector", "tensor_scalar", out=c_lnv[:, gb * 4: gb * 4 + 4],
                 in0=c_st2[:, gb * 8 + 1: gb * 8 + 8: 2],
                 scalar1=1e-5, scalar2=None, op0=ALU.add, inc=ve)
            P.wait("scalar", ve, ve.n)
            P.op("scalar", "activation", out=c_rstd[:, gb * 4: gb * 4 + 4],
                 in_=c_lnv[:, gb * 4: gb * 4 + 4], func=AF.Ln, inc=ae)
            P.wait("scalar", ae, ae.n)
            P.op("scalar", "activation", out=c_rstd[:, gb * 4: gb * 4 + 4],
                 in_=c_rstd[:, gb * 4: gb * 4 + 4], func=AF.Exp, scale=-0.5,
                 inc=ae)
            c_rstd_at[grp] = ae.n

        def c_s1(grp):
            g_mm = c_trans(grp, 0)
            if c_pend[0] is not None:
                c_evac(*c_pend[0], g_mm)
            loads = []
            for t in range(4):
                tt = grp * 4 + t
                P.wait("sync", ve, xC_free[t])
                P.dma("sync", xC[:, t * C:(t + 1) * C],
                      x_h[tt * 128:(tt + 1) * 128, :], sy)
                loads.append(sy.n)
            prev = (grp, 0, loads[0])
            for t in range(1, 4):
                g_mm = c_trans(grp, t)
                c_evac(*prev, g_mm)
                prev = (grp, t, loads[t])
            c_pend[0] = prev


        def c_heavy(grp, t, xn2_done):
            gb = grp % 2
            tt = grp * 4 + t
            cb = t % 2
            P.wait("tensor", ve, xn2_done)
            for gg in range(2):
                rb = c_rnd[0] % 2
                pstf = pc_f if rb == 0 else pc_f2
                P.wait("tensor", ve, xn2T_ps_free[rb])
                for j4 in range(4):
                    j = gg * 4 + j4
                    P.op("tensor", "transpose",
                         pstf[:, j4 * 128:(j4 + 1) * 128],
                         xn2t[:, cb * C + j * 128: cb * C + j * 128 + 128],
                         identf[:], inc=(pe if j4 == 3 else None))
                P.wait("vector", pe, pe.n)
                for j4 in range(4):
                    j = gg * 4 + j4
                    P.op("vector", "tensor_copy",
                         out=xn2T[:, cb * C + j * 128: cb * C + j * 128 + 128],
                         in_=pstf[:, j4 * 128:(j4 + 1) * 128],
                         inc=(ve if j4 == 3 else None))
                xn2T_ps_free[rb] = ve.n
                c_rnd[0] += 1
            xn2T_ps_free_pe[cb] = pe.n
            xn2T_done = ve.n
            P.wait("vector", ve, ve.n)
            P.op("vector", "tensor_copy",
                 out=xn2[:].rearrange("p (c q) -> p c q", c=8)
                 [:, :, tt * 128:(tt + 1) * 128],
                 in_=xn2T[:, cb * C:(cb + 1) * C]
                 .rearrange("p (c q) -> p c q", c=8),
                 inc=ve)
            P.wait("tensor", ve, xn2T_done)
            if t == 0:
                P.wait("tensor", ve, router_ps_free)
            for j in range(8):
                P.op("tensor", "matmul", pa[:, t * NE:(t + 1) * NE],
                     xn2T[:, cb * C + j * 128: cb * C + j * 128 + 128],
                     rwT[:, j * NE:(j + 1) * NE],
                     start=(j == 0), stop=False)
            P.op("tensor", "matmul", pa[:, t * NE:(t + 1) * NE], onesf[:],
                 rbias[:], start=False, stop=(t == 3),
                 inc=(pe if t == 3 else None))

        def c_s2(grp):
            nonlocal router_ps_free, wfm_ps_free
            gb = grp % 2
            P.wait("vector", ae, c_rstd_at[grp])
            xn_at = {}
            for t in range(4):
                cb = t % 2
                t8 = gb * 4 + t
                if t >= 2:
                    P.wait("vector", pe, xn2T_ps_free_pe[cb])
                    P.wait("vector", pe, xn2T_router_free[cb])
                P.op("vector", "tensor_scalar", out=xn2t[:, cb * C:(cb + 1) * C],
                     in0=x2_sb[:, t8 * C:(t8 + 1) * C],
                     scalar1=c_st2[:, gb * 8 + t * 2: gb * 8 + t * 2 + 1],
                     scalar2=c_rstd[:, gb * 4 + t: gb * 4 + t + 1],
                     op0=ALU.subtract, op1=ALU.mult, inc=ve)
                x2_free4[t8] = ve.n
                xn_at[t] = ve.n
                if t % 2 == 0:
                    continue
                for tt2 in (t - 1, t):
                    c_heavy(grp, tt2, xn_at[tt2])
            r_mm = pe.n
            xn2T_router_free[0] = r_mm
            xn2T_router_free[1] = r_mm
            lg = lg_sb
            t0 = tk_t0
            msk = tk_msk
            es = tk_es
            mx = tk_mx
            ts_ = tk_s
            wt = w_tok[:, gb * 4 * NE:(gb + 1) * 4 * NE]
            P.wait("vector", pe, r_mm)
            P.wait("vector", pe, wfm_pe_at[gb])
            P.op("vector", "tensor_copy", out=lg[:], in_=pa[:, 0:4 * NE], inc=ve)
            router_ps_free = ve.n
            P.wait("vector", ve, ve.n)
            P.op("vector", "reduce_max", out=mx[:, 0:4], in_=g4(lg[:]),
                 axis=AX.X)
            P.op("vector", "tensor_copy", out=t0[:], in_=lg[:], inc=ve)
            for k in range(3):
                P.wait("vector", ve, ve.n)
                P.op("vector", "tensor_tensor", out=g4(msk[:]), in0=g4(t0[:]),
                     in1=mx[:, k * 4:(k + 1) * 4]
                     .rearrange("p (g o) -> p g o", o=1).broadcast_to([128, 4, NE]),
                     op=ALU.is_ge, inc=ve)
                P.wait("vector", ve, ve.n)
                P.op("vector", "scalar_tensor_tensor", out=t0[:], in0=msk[:],
                     scalar=-1e30, in1=t0[:], op0=ALU.mult, op1=ALU.add, inc=ve)
                P.wait("vector", ve, ve.n)
                P.op("vector", "reduce_max", out=mx[:, (k + 1) * 4:(k + 2) * 4],
                     in_=g4(t0[:]), axis=AX.X, inc=ve)
            P.wait("vector", ve, ve.n)
            P.op("vector", "tensor_tensor", out=g4(msk[:]), in0=g4(lg[:]),
                 in1=mx[:, 12:16].rearrange("p (g o) -> p g o", o=1)
                 .broadcast_to([128, 4, NE]), op=ALU.is_ge, inc=ve)
            P.wait("vector", ve, ve.n)
            P.op("vector", "tensor_tensor", out=g4(es[:]), in0=g4(lg[:]),
                 in1=mx[:, 0:4].rearrange("p (g o) -> p g o", o=1)
                 .broadcast_to([128, 4, NE]), op=ALU.subtract, inc=ve)
            P.wait("scalar", ve, ve.n)
            P.op("scalar", "activation", out=es[:], in_=es[:], func=AF.Exp,
                 inc=ae)
            P.wait("vector", ae, ae.n)
            P.op("vector", "tensor_mul", out=es[:], in0=es[:], in1=msk[:],
                 inc=ve)
            P.wait("vector", ve, ve.n)
            P.op("vector", "reduce_sum", out=ts_[:, 0:4], in_=g4(es[:]),
                 axis=AX.X, inc=ve)
            P.wait("vector", ve, ve.n)
            P.op("vector", "reciprocal", out=ts_[:, 4:8], in_=ts_[:, 0:4], inc=ve)
            P.wait("vector", ve, ve.n)
            P.op("vector", "tensor_tensor", out=g4(wt[:]), in0=g4(es[:]),
                 in1=ts_[:, 4:8].rearrange("p (g o) -> p g o", o=1)
                 .broadcast_to([128, 4, NE]), op=ALU.mult, inc=ve)
            w_done = ve.n
            P.wait("tensor", ve, w_done)
            P.wait("tensor", ve, wfm_ps_free)
            for t in range(4):
                P.op("tensor", "transpose", pb[0:NE, t * 128:(t + 1) * 128],
                     wt[:, t * NE:(t + 1) * NE], identf[:],
                     inc=(pe if t == 3 else None))
            wfm_pe_at[gb] = pe.n
            P.wait("vector", pe, pe.n)
            for t in range(4):
                tt = grp * 4 + t
                P.op("vector", "tensor_copy",
                     out=w_fm[:, tt * 128:(tt + 1) * 128],
                     in_=pb[0:NE, t * 128:(t + 1) * 128],
                     inc=(ve if t == 3 else None))
            wfm_ps_free = ve.n

        c_s1(0)
        for grp in range(8):
            if grp + 1 < 8:
                c_s1(grp + 1)
            else:
                c_evac(*c_pend[0], pe.n)
                c_pend[0] = None
            c_s2(grp)
        phaseC_done = ve.n
        if stop_after == "C":
            finish([("d_xn2", xn2), ("d_wfm", w_fm), ("d_y", yfm)])
            return nc

        sPC.close()
        sCt.close()
        sBC.close()
        sDt = ctx.enter_context(ExitStack())
        hall = sb(sDt, "hall", [128, NE * 512], BF16)
        hs_fm = sb(sDt, "hs_fm", [128, 4 * 512], BF16)
        sgT = sb(sDt, "sgT", [128, 8 * SH], BF16)
        suT = sb(sDt, "suT", [128, 8 * SH], BF16)
        sdT = sb(sDt, "sdT", [128, 4 * C], BF16)
        gu_ring = sb(sDt, "gu_ring", [128, 4 * C], BF16)
        slab = sb(sDt, "slab", [128, 2 * NE * 256], BF16)
        x2r = sb(sDt, "x2r", [128, 4 * C], F32)
        sg_sb2 = sb(sDt, "sg_sb2", [128, 2 * 512], BF16)
        tmul = sb(sDt, "tmul", [128, 2 * 512], BF16)
        onehot = sb(sDt, "onehot", [NE, NE * 128], BF16)
        sPab.close()
        sPD = ctx.enter_context(ExitStack())
        ps_g = [psm(sPD, "ps_g0", [128, 512], F32),
                psm(sPD, "ps_g1", [128, 512], F32)]
        ps_u = [psm(sPD, "ps_u0", [128, 512], F32),
                psm(sPD, "ps_u1", [128, 512], F32)]
        ps_w = psm(sPD, "ps_w", [128, 512], F32)
        ps_m0 = psm(sPD, "ps_m0", [128, 256], F32)
        ps_m1 = psm(sPD, "ps_m1", [128, 256], F32)

        P.wait("sync", sc, ob_all_done)
        P.wait("sync", ve, phaseC_done)
        P.wait("sync", gp, gp.n)
        P.wait("scalar", ve, phaseC_done)
        P.dma("sync", onehot[:], onehot_h, sy)
        P.dma("sync", sgT[:], sg_sc, sy)
        P.dma("sync", suT[:], su_sc, sy)
        P.dma("sync", sdT[:], sd_sc, sy)
        shared_ld = sy.n

        gu_free = [0, 0, 0, 0]
        slab_free = [0, 0]
        x2_free = [0, 0, 0, 0]
        out_free = [0, 0, 0, 0]
        psm_free = [0, 0]
        psg_free = [0, 0]
        psu_free = [0, 0]
        psw_free = 0
        sg2_free = [0, 0]
        hall_free = 0
        def gu_load(e):
            gs, us = e % 2, 2 + e % 2
            P.wait("gpsimd", pe, gu_free[gs])
            P.dma("gpsimd", gu_ring[:, gs * C:(gs + 1) * C], gate_sc[e], gp)
            P.wait("gpsimd", pe, gu_free[us])
            P.dma("gpsimd", gu_ring[:, us * C:(us + 1) * C], up_sc[e], gp)
            guw_at[e] = gp.n

        guw_at = {}
        P.wait("gpsimd", sc, ob_all_done)
        P.wait("gpsimd", ve, phaseC_done)
        gu_load(0)
        gu_load(1)
        for ck in range(8):
            for e in range(NE):
                gs, us = e % 2, 2 + e % 2
                P.wait("tensor", gp, guw_at[e])
                if ck == 0 and e == 0:
                    P.wait("tensor", ve, phaseC_done)
                    P.wait("tensor", sy, shared_ld)
                pg, pu = ps_g[e % 2], ps_u[e % 2]
                P.wait("tensor", ae, psg_free[e % 2])
                for ch in range(8):
                    P.op("tensor", "matmul", pg[:],
                         gu_ring[:, gs * C + ch * 128: gs * C + ch * 128 + 128],
                         xn2[:, ch * T + ck * 512: ch * T + ck * 512 + 512],
                         start=(ch == 0), stop=(ch == 7),
                         inc=(pe if ch == 7 else None))
                g_mm = pe.n
                P.wait("tensor", ve, psu_free[e % 2])
                for ch in range(8):
                    P.op("tensor", "matmul", pu[:],
                         gu_ring[:, us * C + ch * 128: us * C + ch * 128 + 128],
                         xn2[:, ch * T + ck * 512: ch * T + ck * 512 + 512],
                         start=(ch == 0), stop=(ch == 7),
                         inc=(pe if ch == 7 else None))
                u_mm = pe.n
                gu_free[gs] = g_mm
                gu_free[us] = u_mm
                if e + 2 < NE:
                    gu_load(e + 2)
                elif ck + 1 < 8:
                    gu_load((e + 2) % NE)
                P.wait("tensor", ve, psw_free)
                P.op("tensor", "matmul", ps_w[:],
                     onehot[:, e * 128:(e + 1) * 128],
                     w_fm[:, ck * 512:(ck + 1) * 512],
                     start=True, stop=True, inc=pe)
                w_mm = pe.n
                es = e % 2
                P.wait("scalar", pe, g_mm)
                P.wait("scalar", ve, sg2_free[es])
                P.op("scalar", "activation", out=sg_sb2[:, es * 512:(es + 1) * 512],
                     in_=pg[:], func=AF.Silu, inc=ae)
                psg_free[e % 2] = ae.n
                P.wait("vector", ae, ae.n)
                P.wait("vector", pe, u_mm)
                P.op("vector", "tensor_tensor", out=tmul[:, es * 512:(es + 1) * 512],
                     in0=pu[:], in1=sg_sb2[:, es * 512:(es + 1) * 512],
                     op=ALU.mult, inc=ve)
                psu_free[e % 2] = ve.n
                sg2_free[es] = ve.n
                P.wait("vector", ve, ve.n)
                P.wait("vector", pe, w_mm)
                P.wait("vector", pe, hall_free)
                P.op("vector", "tensor_tensor", out=hall[:, e * 512:(e + 1) * 512],
                     in0=ps_w[:], in1=tmul[:, es * 512:(es + 1) * 512],
                     op=ALU.mult, inc=ve)
                psw_free = ve.n
            P.wait("tensor", sy, shared_ld)
            for jmt in range(4):
                pg, pu = ps_g[jmt % 2], ps_u[jmt % 2]
                P.wait("tensor", ae, psg_free[jmt % 2])
                for ch in range(8):
                    P.op("tensor", "matmul", pg[:],
                         sgT[:, ch * SH + jmt * 128: ch * SH + jmt * 128 + 128],
                         xn2[:, ch * T + ck * 512: ch * T + ck * 512 + 512],
                         start=(ch == 0), stop=(ch == 7),
                         inc=(pe if ch == 7 else None))
                g_mm = pe.n
                P.wait("tensor", ve, psu_free[jmt % 2])
                for ch in range(8):
                    P.op("tensor", "matmul", pu[:],
                         suT[:, ch * SH + jmt * 128: ch * SH + jmt * 128 + 128],
                         xn2[:, ch * T + ck * 512: ch * T + ck * 512 + 512],
                         start=(ch == 0), stop=(ch == 7),
                         inc=(pe if ch == 7 else None))
                u_mm = pe.n
                es = jmt % 2
                P.wait("scalar", pe, g_mm)
                P.wait("scalar", ve, sg2_free[es])
                P.op("scalar", "activation", out=sg_sb2[:, es * 512:(es + 1) * 512],
                     in_=pg[:], func=AF.Silu, inc=ae)
                psg_free[jmt % 2] = ae.n
                P.wait("vector", ae, ae.n)
                P.wait("vector", pe, u_mm)
                P.wait("vector", pe, hall_free)
                P.op("vector", "tensor_tensor",
                     out=hs_fm[:, jmt * 512:(jmt + 1) * 512],
                     in0=pu[:], in1=sg_sb2[:, es * 512:(es + 1) * 512],
                     op=ALU.mult, inc=ve)
                psu_free[jmt % 2] = ve.n
                sg2_free[es] = ve.n
            hall_done = ve.n
            for s in range(4):
                P.wait("sync", ve, x2_free[s])
                P.wait("sync", st, out_free[s])
                P.dma("sync", x2r[:, s * C:(s + 1) * C],
                      x2_sc[(ck * 4 + s) * 128:(ck * 4 + s) * 128 + 128, :], sy)
            x2_ld = sy.n
            for dc in range(4):
                sl = dc % 2
                P.wait("sync", pe, slab_free[sl])
                P.dma("sync", slab[:, sl * NE * 256:(sl + 1) * NE * 256],
                      down_sc[dc], sy)
                slab_ld = sy.n
                for s in range(4):
                    pmx = (dc * 4 + s) % 2
                    pm = ps_m0 if pmx == 0 else ps_m1
                    P.wait("tensor", sy, slab_ld)
                    P.wait("tensor", ve, hall_done)
                    P.wait("tensor", ve, psm_free[pmx])
                    for e in range(NE):
                        P.op("tensor", "matmul", pm[:],
                             hall[:, e * 512 + s * 128: e * 512 + s * 128 + 128],
                             slab[:, sl * NE * 256 + e * 256:
                                  sl * NE * 256 + e * 256 + 256],
                             start=(e == 0), stop=False)
                    for jc in range(4):
                        P.op("tensor", "matmul", pm[:],
                             hs_fm[:, jc * 512 + s * 128: jc * 512 + s * 128 + 128],
                             sdT[:, jc * C + dc * 256: jc * C + dc * 256 + 256],
                             start=False, stop=(jc == 3),
                             inc=(pe if jc == 3 else None))
                    m_mm = pe.n
                    if s == 3:
                        slab_free[sl] = m_mm
                    P.wait("vector", pe, m_mm)
                    P.wait("vector", sy, x2_ld)
                    P.op("vector", "tensor_tensor",
                         out=x2r[:, s * C + dc * 256: s * C + dc * 256 + 256],
                         in0=pm[:],
                         in1=x2r[:, s * C + dc * 256: s * C + dc * 256 + 256],
                         op=ALU.add, inc=ve)
                    psm_free[pmx] = ve.n
                    if dc == 3:
                        P.wait("sync", ve, ve.n)
                        P.dma("sync",
                              out_h[(ck * 4 + s) * 128:(ck * 4 + s) * 128 + 128, :],
                              x2r[:, s * C:(s + 1) * C], st)
                        out_free[s] = st.n
                        x2_free[s] = ve.n
            hall_free = pe.n

        if debug:
            finish([("d_xn2", xn2), ("d_hall", hall)])
            return nc
        finish([])

    return nc


def host_consts():
    ki = np.arange(128)[:, None]
    col = np.arange(128)[None, :]
    masks = np.where(col >= ki, 0.0, -1e30).astype(np.float32)
    onehot = np.zeros((NE, NE * 128), dtype=ml_dtypes.bfloat16)
    for e in range(NE):
        onehot[e, e * 128:(e + 1) * 128] = 1.0
    return dict(
        identb=np.eye(128, dtype=ml_dtypes.bfloat16),
        identf=np.eye(128, dtype=np.float32),
        masks=masks,
        onehot=onehot,
        onesb=np.ones((1, 64), dtype=ml_dtypes.bfloat16),
        onesf=np.ones((1, 128), dtype=np.float32),
        negonesf=-np.ones((1, 128), dtype=np.float32),
    )


def make_in_maps(inputs, n_cores=8):
    x = np.asarray(inputs["L0_attn_x"], dtype=np.float32)
    f = np.float32
    wmap = dict(
        wdq=np.ascontiguousarray(inputs["L0_attn_W_dq"], dtype=f),
        wuq=np.ascontiguousarray(inputs["L0_attn_W_uq"], dtype=f),
        wdkv=np.ascontiguousarray(inputs["L0_attn_W_dkv"], dtype=f),
        wuk=np.ascontiguousarray(inputs["L0_attn_W_uk"], dtype=f),
        wuv=np.ascontiguousarray(inputs["L0_attn_W_uv"], dtype=f),
        wo=np.ascontiguousarray(inputs["L0_attn_W_o"], dtype=f),
        rw=np.ascontiguousarray(inputs["L0_ffn_router_w"], dtype=f),
        rb=np.ascontiguousarray(
            np.asarray(inputs["L0_ffn_router_bias"], dtype=f).reshape(1, NE)),
        gw=np.ascontiguousarray(inputs["L0_ffn_gate_w"], dtype=f),
        uw=np.ascontiguousarray(inputs["L0_ffn_up_w"], dtype=f),
        dw=np.ascontiguousarray(inputs["L0_ffn_down_w"], dtype=f),
        sgw=np.ascontiguousarray(inputs["L0_ffn_sgate_w"], dtype=f),
        suw=np.ascontiguousarray(inputs["L0_ffn_sup_w"], dtype=f),
        sdw=np.ascontiguousarray(inputs["L0_ffn_sdown_w"], dtype=f),
        **host_consts(),
    )
    return [dict(x=np.ascontiguousarray(x[i]), **wmap) for i in range(n_cores)]


def kernel(**inputs):
    in_maps = make_in_maps(inputs, 8)
    nc = build_program(debug=False)
    res = run_bass_kernel_spmd(nc, in_maps, core_ids=list(range(8)))
    return np.stack([np.asarray(res.results[i]["out"]) for i in range(8)], axis=0)
